# version 16
# baseline (speedup 1.0000x reference)
import numpy as np
import ml_dtypes
from collections import deque
from contextlib import ExitStack

import concourse.bass as bass
import concourse.mybir as mybir
from concourse.bass_utils import run_bass_kernel_spmd

F32 = mybir.dt.float32
BF16 = mybir.dt.bfloat16
AF = mybir.ActivationFunctionType
ALU = mybir.AluOpType
AXX = mybir.AxisListType.X

DM = 2048
DI = 4096
DXB = 1024
DSTATE = 128
HEADDIM = 64
NH = 64
NG = 32
CONVD = 6144
DPROJ = 10304
INTER = 8192
EPS = 1e-5

TM = 2048
HALO = 128
TL = TM + HALO
PFX = 6144
LC = 128
NCH = TM // LC
LCP = 512
NCHP = PFX // LCP
LCPB = 1024
NCHB = PFX // LCPB
PFXB_IN_PH3 = 999

ENGS = ("sync", "gpsimd", "tensor", "vector", "scalar")
ECH = {"sync": "dS", "gpsimd": "dG", "tensor": "T", "vector": "V", "scalar": "A"}


def _H(n):
    h = np.array([[1.0]], dtype=np.float64)
    while h.shape[0] < n:
        h = np.block([[h, h], [h, -h]])
    return h


class Ch:
    def __init__(self, handle):
        self.h = handle
        self.n = 0

    def inc(self, k=1):
        self.n += k
        return self.n


class Builder:

    NDMA = 8

    def __init__(self, nc, es):
        self.nc = nc
        self.q = {e: [] for e in ENGS}
        self.S = {k: Ch(es.enter_context(nc.semaphore("s_" + k)))
                  for k in ["T", "V", "A", "bar"]}
        self.dring = {
            "sync": [Ch(es.enter_context(nc.semaphore(f"s_dS{i}"))) for i in range(self.NDMA)],
            "gpsimd": [Ch(es.enter_context(nc.semaphore(f"s_dG{i}"))) for i in range(self.NDMA)],
        }
        self.dcnt = {"sync": 0, "gpsimd": 0}
        self.nbar = 0

    def op(self, eng, fn, waits=(), inc=False):
        S = self.S
        ch = S[ECH[eng]] if ECH[eng] in S else None
        auto = eng in ("vector", "scalar")
        pre = ch.n if auto else None
        tgt = ch.inc(1) if (auto or inc) else None
        wl = list(waits)

        def f(e, wl=wl, pre=pre, tgt=tgt, fn=fn, ch=ch):
            if pre:
                e.wait_ge(ch.h, pre)
            for (c, t) in wl:
                if t:
                    e.wait_ge(c.h, t)
            ins = fn(e)
            if tgt is not None:
                ins.then_inc(ch.h, 1)
        self.q[eng].append(f)
        return (ch, tgt) if tgt is not None else None

    def raw(self, eng, fn):
        self.q[eng].append(fn)

    def dma(self, eng, out, in_, waits=()):
        assert eng in ("sync", "gpsimd")
        i = self.dcnt[eng]
        self.dcnt[eng] += 1
        ch = self.dring[eng][i % self.NDMA]
        tgt = ch.inc(16)
        wl = list(waits)

        def f(e, wl=wl, tgt=tgt, ch=ch):
            for (c, t) in wl:
                if t:
                    e.wait_ge(c.h, t)
            if tgt > 16:
                e.wait_ge(ch.h, tgt - 16)
            e.dma_start(out=out, in_=in_).then_inc(ch.h, 16)
        self.q[eng].append(f)
        return (ch, tgt)

    def dma_tokens(self, eng):
        return [(c, c.n) for c in self.dring[eng] if c.n > 0]

    def barrier(self):
        S = self.S
        self.nbar += 1
        n = self.nbar
        marks = {e: S[ECH[e]].n if ECH[e] in S else None for e in ENGS}
        dS_tok = self.dma_tokens("sync")
        dG_tok = self.dma_tokens("gpsimd")

        def mk(ename):
            def f(e):
                if ename == "sync":
                    for (c, t) in dS_tok:
                        e.wait_ge(c.h, t)
                elif ename == "gpsimd":
                    for (c, t) in dG_tok:
                        e.wait_ge(c.h, t)
                elif marks[ename]:
                    e.wait_ge(S[ECH[ename]].h, marks[ename])
                e.nop().then_inc(S["bar"].h, 1)
                e.wait_ge(S["bar"].h, 5 * n)
            return f
        for e in ENGS:
            self.q[e].append(mk(e))

    def emit(self):
        with self.nc.Block() as block:
            qs = self.q

            @block.sync
            def _(e):
                for f in qs["sync"]:
                    f(e)

            @block.gpsimd
            def _(e):
                for f in qs["gpsimd"]:
                    f(e)

            @block.tensor
            def _(e):
                for f in qs["tensor"]:
                    f(e)

            @block.vector
            def _(e):
                for f in qs["vector"]:
                    f(e)

            @block.scalar
            def _(e):
                for f in qs["scalar"]:
                    f(e)


def build_program(dbg=()):
    dbg = set(dbg)
    nc = bass.Bass("TRN2", target_bir_lowering=False, debug=False)

    d_u = nc.dram_tensor("u_loc", [TL, DM], F32, kind="ExternalInput").ap()
    d_upfx = nc.dram_tensor("u_pfx", [PFX, DM], F32, kind="ExternalInput").ap()
    d_dtmask = nc.dram_tensor("dt_mask", [NH, PFX], F32, kind="ExternalInput").ap()
    d_win = nc.dram_tensor("win_blk", [80, 128, 16, 128], F32, kind="ExternalInput").ap()
    d_windt = nc.dram_tensor("win_dt_blk", [128, 16, 64], F32, kind="ExternalInput").ap()
    d_wup = nc.dram_tensor("wup_blk", [64, 128, 16, 128], F32, kind="ExternalInput").ap()
    d_wgate = nc.dram_tensor("wgate_blk", [64, 128, 16, 128], F32, kind="ExternalInput").ap()
    d_wdown = nc.dram_tensor("wdown_blk", [16, 128, 64, 128], F32, kind="ExternalInput").ap()
    d_wout = nc.dram_tensor("wout_blk", [16, 128, 32, 128], F32, kind="ExternalInput").ap()
    d_convw = nc.dram_tensor("convw_pk", [128, 48, 4], F32, kind="ExternalInput").ap()
    d_convb = nc.dram_tensor("convb_pk", [128, 48], F32, kind="ExternalInput").ap()
    d_dtbias = nc.dram_tensor("dtbias", [64, 1], F32, kind="ExternalInput").ap()
    d_alog = nc.dram_tensor("alog", [64, 1], F32, kind="ExternalInput").ap()
    d_Drow = nc.dram_tensor("Drow", [1, 64], F32, kind="ExternalInput").ap()
    d_nw = nc.dram_tensor("nw_pk", [128, 32], F32, kind="ExternalInput").ap()
    d_suug = nc.dram_tensor("suug_pk", [128, 16], F32, kind="ExternalInput").ap()
    d_sudp = nc.dram_tensor("sudp_pk", [128, 64], F32, kind="ExternalInput").ap()
    d_wscup = nc.dram_tensor("wscup_pk", [128, 64], F32, kind="ExternalInput").ap()
    d_wscgate = nc.dram_tensor("wscgate_pk", [128, 64], F32, kind="ExternalInput").ap()
    d_wscdp = nc.dram_tensor("wscdp_pk", [128, 16], F32, kind="ExternalInput").ap()
    d_out = nc.dram_tensor("out_loc", [TM, DM], F32, kind="ExternalOutput").ap()

    ident_c = nc.inline_tensor(np.eye(128, dtype=ml_dtypes.bfloat16), "c_ident").ap()
    identf_c = nc.inline_tensor(np.eye(128, dtype=np.float32), "c_identf").ap()
    h128_c = nc.inline_tensor(np.ascontiguousarray(_H(128).astype(ml_dtypes.bfloat16)), "c_h128").ap()
    m1_c = nc.inline_tensor(np.ascontiguousarray(
        np.kron(np.eye(2), _H(64)).astype(ml_dtypes.bfloat16)), "c_m1").ap()
    ones1_c = nc.inline_tensor(np.ones((1, 128), dtype=ml_dtypes.bfloat16), "c_ones1").ap()
    slneg_np = np.where(np.arange(128)[None, :] < np.arange(128)[:, None], -1e30, 0.0)
    slneg_c = nc.inline_tensor(np.ascontiguousarray(
        slneg_np.astype(ml_dtypes.bfloat16)), "c_slneg").ap()
    og_np = np.zeros((32, 128, 32), dtype=ml_dtypes.bfloat16)
    for g in range(32):
        og_np[g, :, g] = 1
    og_c = nc.inline_tensor(np.ascontiguousarray(og_np.reshape(32 * 128, 32)), "c_og").ap()
    eb_np = np.zeros((32, 32, 128), dtype=ml_dtypes.bfloat16)
    for g in range(32):
        eb_np[g, g, :] = 1
    eb_c = nc.inline_tensor(np.ascontiguousarray(eb_np.reshape(32 * 32, 128)), "c_eb").ap()

    z_scr = nc.dram_tensor("z_scr", [DI, TL], BF16, kind="Internal").ap()
    xbc_scr = nc.dram_tensor("xbc_scr", [CONVD, TM], BF16, kind="Internal").ap()
    h1_scr = nc.dram_tensor("h1_scr", [INTER, TM], BF16, kind="Internal").ap()
    y_scr = nc.dram_tensor("y_scr", [DI, TM], BF16, kind="Internal").ap()
    g_scr = nc.dram_tensor("g_scr", [DM, TM], BF16, kind="Internal").ap()
    sg_scr = nc.dram_tensor("sg_scr", [DM, TM], BF16, kind="Internal").ap()
    csm_scr = nc.dram_tensor("csm_scr", [NCH * 4, 2048], F32, kind="Internal").ap()
    dcc_scr = nc.dram_tensor("dcc_scr", [NH, 1], BF16, kind="Internal").ap()
    ecs_scr = nc.dram_tensor("ecs_scr", [NCH * 4, 2048], BF16, kind="Internal").ap()
    xbcpre_scr = nc.dram_tensor("xbcpre_scr", [2048, PFX], BF16, kind="Internal").ap()
    wpfx_scr = nc.dram_tensor("wpfx_scr", [16, 128, 16, 128], BF16, kind="Internal").ap()
    wbf_scr = nc.dram_tensor("wbf_scr", [64, PFX], BF16, kind="Internal").ap()

    dbg_outs = {}

    def dbg_tensor(name, shape, dtype=F32):
        t = nc.dram_tensor("dbg_" + name, shape, dtype, kind="ExternalOutput").ap()
        dbg_outs[name] = t
        return t

    es = ExitStack()
    with es:
        B = Builder(nc, es)
        S = B.S

        def sb(name, shape, dtype):
            return es.enter_context(nc.sbuf_tensor(name, shape, dtype))

        IDENT = sb("IDENT", [128, 128], BF16)
        IDF = sb("IDF", [128, 128], F32)
        H128 = sb("H128", [128, 128], BF16)
        M1C = sb("M1C", [128, 128], BF16)
        SLNEG = sb("SLNEG", [128, 128], BF16)
        ONES1 = sb("ONES1", [1, 128], BF16)
        CONVW = sb("CONVW", [128, 48, 4], F32)
        CONVB = sb("CONVB", [128, 48], F32)
        DTBIAS = sb("DTBIAS", [64, 1], F32)
        AH = sb("AH", [64, 1], F32)
        DROW = sb("DROW", [1, 64], BF16)
        NW = sb("NW", [128, 32], F32)
        SUUG = sb("SUUG", [128, 16], F32)
        SUDP = sb("SUDP", [128, 64], F32)
        WSCUP = sb("WSCUP", [128, 64], F32)
        WSCGATE = sb("WSCGATE", [128, 64], F32)
        WSCDP = sb("WSCDP", [128, 16], F32)
        EPSC = sb("EPSC", [128, 1], F32)
        ONEC = sb("ONEC", [128, 1], F32)
        DCBB = sb("DCBB", [128, 64], BF16)

        es_mid = ExitStack()

        def sbm(name, shape, dtype):
            return es_mid.enter_context(nc.sbuf_tensor(name, shape, dtype))

        SST = sbm("SST", [128, NH * HEADDIM], F32)

        PS = [es.enter_context(nc.psum_tensor(f"ps{i}", [128, 512], F32)) for i in range(8)]

        def ptb(i):
            return PS[i][:].bitcast(BF16)

        op, dma, barrier = B.op, B.dma, B.barrier

        for dst, src in [
            (IDENT[:], ident_c), (IDF[:], identf_c), (H128[:], h128_c), (M1C[:], m1_c),
            (ONES1[:], ones1_c), (SLNEG[:], slneg_c),
            (CONVW[:], d_convw), (CONVB[:], d_convb), (DTBIAS[:], d_dtbias),
            (AH[:], d_alog), (NW[:], d_nw),
            (SUUG[:], d_suug), (SUDP[:], d_sudp), (WSCUP[:], d_wscup),
            (WSCGATE[:], d_wscgate), (WSCDP[:], d_wscdp),
        ]:
            dma("sync", dst, src)
        wS = B.dma_tokens("sync")
        wG0 = dma("gpsimd", DROW[:], d_Drow)

        op("vector", lambda v: v.memset(EPSC[:], EPS))
        op("vector", lambda v: v.memset(ONEC[:], 1.0))
        aExp = op("scalar", lambda a: a.activation(AH[:], AH[:], AF.Exp),
                  waits=wS)
        op("vector", lambda v: v.tensor_scalar_mul(AH[:], AH[:], -1.0),
           waits=[aExp])
        barrier()

        def make_ln(tag, sba=None):
            sba = sba or sb
            U = [sba(f"lnU{p}_{tag}", [128, DM], F32) for p in range(3)]
            XT = [sba(f"lnX{p}_{tag}", [128, DM], BF16) for p in range(2)]
            SQ = sba(f"lnSQ_{tag}", [128, DM], BF16)
            ST = [sba(f"lnST{p}_{tag}", [128, 8], F32) for p in range(3)]
            st = dict(f=0, s=0, t=0)
            descs = {}
            tok = {}
            tG = {}
            vEv = {}

            def feed(src_dram, src_tok, dst_cm, dst_tile, dst_guard=None,
                     done=None):
                i = st["f"]
                st["f"] += 1
                descs[i] = (dst_cm, dst_tile, dst_guard, done)
                p = i % 3
                wl = []
                if (i - 3, "vN") in tok:
                    wl = [tok[(i - 3, "vN")], tok[(i - 3, "aSq")]]
                tok[(i, "dma")] = dma("sync", U[p][:],
                                      src_dram[src_tok:src_tok + 128, :],
                                      waits=wl)

            def stats():
                i = st["s"]
                st["s"] += 1
                p = i % 3
                px = i % 2
                tok[(i, "vSum")] = op("scalar", lambda a: a.activation(
                    SQ[:], U[p][:], AF.Copy, accum_out=ST[p][:, 0:1]),
                    waits=[tok[(i, "dma")]] +
                    ([tok[(i - 3, "vN")]] if (i - 3, "vN") in tok else []))
                tok[(i, "aSq")] = op("scalar", lambda a: a.activation(
                    SQ[:], U[p][:], AF.Square, accum_out=ST[p][:, 1:2]))
                tok[(i, "aM")] = op("scalar", lambda a: a.mul(
                    ST[p][:, 2:3], ST[p][:, 0:1], 1.0 / DM))
                op("vector", lambda v: v.tensor_tensor(
                    ST[p][:, 3:4], ST[p][:, 2:3], ST[p][:, 2:3], ALU.mult),
                    waits=[tok[(i, "aM")]])
                op("vector", lambda v: v.tensor_scalar(
                    ST[p][:, 4:5], ST[p][:, 1:2], 1.0 / DM, None, ALU.mult))
                tok[(i, "vVar")] = op("vector", lambda v: v.tensor_tensor(
                    ST[p][:, 5:6], ST[p][:, 4:5], ST[p][:, 3:4], ALU.subtract))
                op("scalar", lambda a: a.activation(
                    ST[p][:, 6:7], ST[p][:, 5:6], AF.Ln, bias=EPSC[:]),
                    waits=[tok[(i, "vVar")]])
                tok[(i, "aR")] = op("scalar", lambda a: a.activation(
                    ST[p][:, 7:8], ST[p][:, 6:7], AF.Exp, scale=-0.5))
                tok[(i, "vN")] = op("vector", lambda v: v.tensor_scalar(
                    XT[px][:], U[p][:], ST[p][:, 2:3], ST[p][:, 7:8],
                    ALU.subtract, ALU.mult),
                    waits=[tok[(i, "aR")]] +
                    ([tG[i - 2]] if (i - 2) in tG else []))

            def tp():
                i = st["t"]
                st["t"] += 1
                dst_cm, dt0, dst_guard, done = descs.pop(i)
                px = i % 2
                for jj in range(4):
                    bank = 6 + jj % 2
                    key = (i, jj - 2) if jj >= 2 else (i - 1, jj + 2)
                    wl = [tok[(i, "vN")]]
                    if key in vEv:
                        wl.append(vEv[key])
                    if dst_guard is not None:
                        wl.append(dst_guard)

                    def tfn(t, jj=jj, px=px, bank=bank):
                        ins = None
                        for j in range(4):
                            ins = t.transpose(
                                ptb(bank)[:, j * 128:(j + 1) * 128],
                                XT[px][:, (jj * 4 + j) * 128:(jj * 4 + j + 1) * 128],
                                IDENT[:])
                        return ins
                    tG[i] = op("tensor", tfn, waits=wl, inc=True)
                    vEv[(i, jj)] = op("vector", lambda v, jj=jj, dt0=dt0, bank=bank: v.tensor_copy(
                        dst_cm[:, jj * 4:jj * 4 + 4, dt0 * 128:(dt0 + 1) * 128],
                        ptb(bank).rearrange("p (j q) -> p j q", j=8)[:, 0:4, :]),
                        waits=[tG[i]])
                last = vEv[(i, 3)]
                if done is not None:
                    done(last)
                return last
            return st, feed, stats, tp

        def emit_softplus(dst, srcv, tmp1, tmp2, mask_view=None, waits=()):
            t1 = op("vector", lambda v: v.tensor_scalar(
                tmp1, srcv, DTBIAS[:], None, ALU.add), waits=list(waits))
            a1 = op("scalar", lambda a: a.activation(tmp2, tmp1, AF.Abs), waits=[t1])
            op("scalar", lambda a: a.activation(tmp2, tmp2, AF.Exp, scale=-1.0))
            t2 = op("scalar", lambda a: a.activation(tmp2, tmp2, AF.Ln, bias=ONEC[0:64, :]))
            op("vector", lambda v: v.tensor_scalar(tmp1, tmp1, 0.0, None, ALU.max),
               waits=[a1])
            if mask_view is None:
                aw = op("vector", lambda v: v.tensor_tensor(dst, tmp1, tmp2, ALU.add),
                        waits=[t2])
            else:
                op("vector", lambda v: v.tensor_tensor(tmp1, tmp1, tmp2, ALU.add),
                   waits=[t2])
                aw = op("vector", lambda v: v.tensor_tensor(dst, tmp1, mask_view, ALU.mult))
            return aw

        DTM = sbm("DTM", [64, TM], F32)
        es_a = ExitStack()

        def sba(name, shape, dtype):
            return es_a.enter_context(nc.sbuf_tensor(name, shape, dtype))

        XN = sba("XN", [128, 16, TL], BF16)
        X1 = XN[:, :, HALO:HALO + TM].rearrange("p a b -> p a b")
        WPDT = sba("WPDT", [128, 16, 64], BF16)
        DCBA = sba("DCBA", [128, NCHB, 64], F32)

        with ExitStack() as esp:
            def sbp(name, shape, dtype):
                return esp.enter_context(nc.sbuf_tensor(name, shape, dtype))

            lnst, ln_feed, ln_stats, ln_tp = make_ln("c", sbp)
            XNP = [sbp(f"XNP{i}", [128, 16, LCP], BF16) for i in range(2)]
            WPF = [sbp(f"WPF{i}", [128, 16, 128], BF16) for i in range(3)]
            ZEVP = [sbp(f"ZEVP{i}", [128, 512], BF16) for i in range(2)]
            DTP = sbp("DTP", [64, LCPB], F32)
            DAPC = sbp("DAPC", [64, LCPB], F32)
            CSP = sbp("CSP", [64, LCPB], F32)
            DSW = sbp("DSW", [64, LCPB], F32)
            DTMK = sbp("DTMK", [64, LCPB], BF16)
            ZER64 = sbp("ZER64", [64, LCPB], BF16)
            WBFP = sbp("WBFP", [64, LCPB], BF16)
            DCC = sbp("DCC", [64, 1], BF16)

            wPD = dma("gpsimd", WPDT[:], d_windt)
            op("vector", lambda v: v.memset(SST[:], 0.0))
            op("vector", lambda v: v.memset(ZER64[:], 0.0))

            bank_tok = {i: None for i in range(8)}
            abk = [0]
            wpf_free = {0: [], 1: [], 2: []}
            wb_tok = {}
            zevp_free = {0: None, 1: None}
            ln_guard = {0: None, 1: None}
            dtp_guard = [None]
            dtp_evac = {}
            aDt_prev = [None]
            dcc_prev = [None]
            dcbb_prev = [None]
            wbf_free = [None]
            lnv = {}

            tiles_pri = deque()
            tiles_main = deque()
            main_pos = [0]

            def push_lnc(c):
                pc = c % 2
                nt = LCP // 128
                for tix in range(nt):
                    def f(c=c, pc=pc, tix=tix, last=(tix == nt - 1)):
                        def done(tokl, c=c):
                            lnv[c] = tokl
                        ln_feed(d_upfx, c * LCP + tix * 128, XNP[pc], tix,
                                dst_guard=ln_guard[pc],
                                done=done if last else None)
                    tiles_pri.append(f)

            def push_main():
                def f():
                    ln_feed(d_u, main_pos[0] * 128, XN, main_pos[0])
                    main_pos[0] += 1
                tiles_main.append(f)

            def feed_one():
                if tiles_pri:
                    tiles_pri.popleft()()
                    return True
                if tiles_main:
                    tiles_main.popleft()()
                    return True
                return False

            def drain_one():
                if lnst["t"] < lnst["s"] - 1:
                    ln_tp()
                    return True
                if lnst["s"] < lnst["f"]:
                    ln_stats()
                    return True
                if feed_one():
                    return True
                if lnst["t"] < lnst["s"]:
                    ln_tp()
                    return True
                return False

            def pump():
                if lnst["t"] < lnst["s"] - 1:
                    ln_tp()
                if lnst["s"] < lnst["f"]:
                    ln_stats()
                feed_one()

            def emit_mms(c):
                pc = c % 2
                vLN = lnv[c]
                last_mm = None
                for idx, mi in enumerate([16] + list(range(16))):
                    if mi < 16:
                        wp = mi % 3
                        if c == 0:
                            wW = [dma("gpsimd", WPF[wp][:], d_win[32 + mi],
                                      waits=wpf_free[wp])]
                            wb_tok[mi] = dma("sync", wpfx_scr[mi], WPF[wp][:],
                                             waits=wW)
                        else:
                            wl = list(wpf_free[wp])
                            if c == 1:
                                wl.append(wb_tok[mi])
                            wW = [dma("sync", WPF[wp][:, 0:8, :],
                                      wpfx_scr[mi][:, 0:8, :], waits=wl),
                                  dma("sync", WPF[wp][:, 8:16, :],
                                      wpfx_scr[mi][:, 8:16, :], waits=wl)]
                    else:
                        wW = [wPD]
                    bank = abk[0] % 4
                    abk[0] += 1
                    wl = wW + [vLN]
                    if bank_tok[bank] is not None:
                        wl.append(bank_tok[bank])

                    def mmf(t, mi=mi, bank=bank, pc=pc):
                        ins = None
                        for k in range(16):
                            lhs = WPF[mi % 3][:, k, :] if mi < 16 else WPDT[:, k, :]
                            mrows = 128 if mi < 16 else 64
                            ins = t.matmul(PS[bank][0:mrows, :], lhs,
                                           XNP[pc][:, k, :],
                                           start=(k == 0), stop=(k == 15))
                        return ins
                    tMM = op("tensor", mmf, waits=wl, inc=True)
                    if mi < 16:
                        wpf_free[mi % 3] = [tMM]
                        if c == 0:
                            wpf_free[mi % 3].append(wb_tok[mi])
                        zp = mi % 2
                        wl2 = [tMM]
                        if zevp_free[zp] is not None:
                            wl2.append(zevp_free[zp])
                        aE = op("scalar", lambda a, zp=zp, bank=bank: a.copy(
                            ZEVP[zp][:], PS[bank][:, :]), waits=wl2)
                        bank_tok[bank] = aE
                        zevp_free[zp] = dma(
                            "sync" if c == 0 else "gpsimd",
                            xbcpre_scr[mi * 128:(mi + 1) * 128,
                                       c * LCP:(c + 1) * LCP],
                            ZEVP[zp][:], waits=[aE])
                    else:
                        ho = (c % 2) * 512
                        wl2 = [tMM]
                        if dtp_guard[0] is not None:
                            wl2.append(dtp_guard[0])
                        aD = op("scalar", lambda a, ho=ho, bank=bank: a.copy(
                            DTP[:, ho:ho + 512], PS[bank][0:64, :]), waits=wl2)
                        bank_tok[bank] = aD
                        dtp_evac[c] = aD
                    last_mm = tMM
                    if idx in (2, 5, 8, 11, 14, 16):
                        pump()
                ln_guard[pc] = last_mm

            def emit_dt(cp):
                wMK = dma("gpsimd", DTMK[:],
                          d_dtmask[:, cp * LCPB:(cp + 1) * LCPB],
                          waits=[aDt_prev[0]] if aDt_prev[0] else [])
                aDt = emit_softplus(
                    DTP[:], DTP[:], DAPC[:], CSP[:], mask_view=DTMK[:],
                    waits=[dtp_evac[2 * cp], dtp_evac[2 * cp + 1], wMK])
                aDt_prev[0] = aDt
                op("vector", lambda v: v.tensor_scalar(
                    DAPC[:], DTP[:], AH[:], None, ALU.mult))
                vCS = op("vector", lambda v: v.tensor_tensor_scan(
                    CSP[:], DAPC[:], ZER64[:], 0.0, ALU.add, ALU.add))
                vD1 = op("vector", lambda v: v.tensor_tensor(
                    DSW[:], CSP[:, LCPB - 1:LCPB].broadcast_to([64, LCPB]),
                    CSP[:], ALU.subtract))
                aD2 = op("scalar", lambda a: a.activation(DSW[:], DSW[:], AF.Exp),
                         waits=[vD1])
                vD3 = op("vector", lambda v: v.tensor_tensor(
                    DSW[:], DSW[:], DTP[:], ALU.mult), waits=[aD2])
                dtp_guard[0] = vD3
                vWBF = op("vector", lambda v: v.tensor_copy(WBFP[:], DSW[:]),
                          waits=[wbf_free[0]] if wbf_free[0] else [])
                wbf_free[0] = dma("gpsimd", wbf_scr[:, cp * LCPB:(cp + 1) * LCPB],
                                  WBFP[:], waits=[vWBF])
                aDC = op("scalar", lambda a: a.activation(
                    DCC[:], CSP[:, LCPB - 1:LCPB], AF.Exp),
                    waits=[vCS] + ([dcc_prev[0]] if dcc_prev[0] else []))
                wDC0 = dma("gpsimd", dcc_scr, DCC[:], waits=[aDC])
                dcc_prev[0] = wDC0
                wl = [wDC0]
                if dcbb_prev[0] is not None:
                    wl.append(dcbb_prev[0])
                wDB = dma("gpsimd", DCBB[:],
                          dcc_scr.rearrange("a b -> b a").broadcast_to([128, 64]),
                          waits=wl)
                dcbb_prev[0] = op("vector", lambda v, cp=cp: v.tensor_copy(
                    DCBA[:, cp, :], DCBB[:]), waits=[wDB])

            push_lnc(0)
            push_lnc(1)
            for _ in range(TL // 128):
                push_main()
            for _ in range(3):
                feed_one()
            for c in range(NCHP):
                while lnv.get(c) is None:
                    if not drain_one():
                        break
                emit_mms(c)
                if c % 2 == 1:
                    emit_dt(c // 2)
                if c + 2 < NCHP:
                    push_lnc(c + 2)
            while (tiles_pri or tiles_main or lnst["s"] < lnst["f"]
                   or lnst["t"] < lnst["s"]):
                if not drain_one():
                    break
            barrier()

        if "xn" in dbg:
            t = dbg_tensor("xn", [128, 16 * TL], BF16)
            dma("sync", t.rearrange("p (a b) -> p a b", a=16), XN[:])
            barrier()
        if "pfxa" in dbg:
            t = dbg_tensor("xbp", [2048, PFX], BF16)
            dma("sync", t, xbcpre_scr)
            t2 = dbg_tensor("wbfd", [64, PFX], BF16)
            dma("sync", t2, wbf_scr)
            t3 = dbg_tensor("dcba", [128, NCHB * 64])
            dma("sync", t3.rearrange("p (a b) -> p a b", a=NCHB), DCBA[:])
            barrier()

        es_b2 = ExitStack()

        def sbb2(name, shape, dtype):
            return es_b2.enter_context(nc.sbuf_tensor(name, shape, dtype))

        XBPRE = sbb2("XBPRE", [128, 16, LCPB + 4], BF16)
        ACCP = [sbb2(f"ACCP{i}", [128, LCPB], F32) for i in range(2)]
        XTC8 = sbb2("XTC8", [128, LCPB // 128, 2, 128], BF16)
        XWG = sbb2("XWG", [128, LCPB // 128, 512], BF16)
        WTQ = sbb2("WTQ", [128, LCPB // 128, 64], BF16)
        WBF = sbb2("WBF", [64, LCPB], BF16)
        pbank = {i: None for i in range(8)}

        def pfxb_gen():
            vZB = op("vector", lambda v: v.memset(XBPRE[:, :, 0:4], 0.0))
            ld_guard = [vZB]
            wbf_guard = [None]
            wtq_tok = {i: None for i in range(LCPB // 128)}
            st_upd = {gb: None for gb in range(8)}
            accp_g = {0: None, 1: None}
            for cb in range(NCHB):
                ldt = []
                wl0 = [t for t in ld_guard if t is not None]
                for mi in range(16):
                    if cb == 0:
                        dst = XBPRE[:, mi, 4:4 + LCPB]
                        src = xbcpre_scr[mi * 128:(mi + 1) * 128, 0:LCPB]
                    else:
                        dst = XBPRE[:, mi, 1:4 + LCPB]
                        src = xbcpre_scr[mi * 128:(mi + 1) * 128,
                                         cb * LCPB - 3:(cb + 1) * LCPB]
                    ldt.append(dma("sync", dst, src,
                                   waits=wl0 if mi == 0 else []))
                wWBF = dma("sync", WBF[:],
                           wbf_scr[:, cb * LCPB:(cb + 1) * LCPB],
                           waits=[wbf_guard[0]] if wbf_guard[0] else [])
                yield
                vConvDone = []
                vCl = None
                for mi in range(16):
                    ap = mi % 2
                    wl = [ldt[mi]]
                    if accp_g[ap] is not None:
                        wl.append(accp_g[ap])
                    if cb == 0:
                        wl.append(vZB)
                    op("vector", lambda v, mi=mi, ap=ap: v.tensor_scalar(
                        ACCP[ap][:], XBPRE[:, mi, 1:1 + LCPB], CONVW[:, mi, 0:1],
                        None, ALU.mult), waits=wl)
                    for k in range(1, 4):
                        vCl = op("vector", lambda v, mi=mi, k=k, ap=ap: v.scalar_tensor_tensor(
                            ACCP[ap][:], XBPRE[:, mi, 1 + k:1 + k + LCPB],
                            CONVW[:, mi, k:k + 1], ACCP[ap][:], ALU.mult, ALU.add))
                    vConvDone.append(op("scalar", lambda a, mi=mi, ap=ap: a.activation(
                        XBPRE[:, mi, 4:4 + LCPB], ACCP[ap][:], AF.Silu,
                        bias=CONVB[:, mi:mi + 1]), waits=[vCl]))
                    accp_g[ap] = vConvDone[-1]
                    if mi % 4 == 3:
                        yield
                yield
                for blk in range(LCPB // 128):
                    tb = 6 + blk % 2
                    wl2 = [wWBF]
                    if wtq_tok[blk] is not None:
                        wl2.append(wtq_tok[blk])
                    if pbank[tb] is not None:
                        wl2.append(pbank[tb])
                    tW = op("tensor", lambda t, blk=blk, tb=tb: t.transpose(
                        ptb(tb)[:, 256:320],
                        WBF[:, blk * 128:(blk + 1) * 128],
                        IDENT[0:64, 0:64]), waits=wl2, inc=True)
                    wtq_tok[blk] = op("vector", lambda v, blk=blk, tb=tb: v.tensor_copy(
                        WTQ[:, blk, :], ptb(tb)[:, 256:320]), waits=[tW])
                    pbank[tb] = wtq_tok[blk]
                yield
                last_st = None
                for gb in range(8):
                    evs = []
                    for blk in range(LCPB // 128):
                        tb = 6 + blk % 2
                        wl = [vConvDone[gb], vConvDone[8 + gb]]
                        if pbank[tb] is not None:
                            wl.append(pbank[tb])

                        def tfn(t, gb=gb, blk=blk, tb=tb):
                            t.transpose(ptb(tb)[:, 0:128],
                                        XBPRE[:, gb, 4 + blk * 128:4 + (blk + 1) * 128],
                                        IDENT[:])
                            return t.transpose(
                                ptb(tb)[:, 128:256],
                                XBPRE[:, 8 + gb, 4 + blk * 128:4 + (blk + 1) * 128],
                                IDENT[:])
                        tT = op("tensor", tfn, waits=wl, inc=True)
                        pbank[tb] = op("vector", lambda v, blk=blk, tb=tb: v.tensor_copy(
                            XTC8[:, blk, :, :],
                            ptb(tb).rearrange("p (a q) -> p a q", a=8)[:, 0:2, :]),
                            waits=[tT])
                        evs.append(pbank[tb])
                    xw_toks = []
                    for blk in range(LCPB // 128):
                        xw_toks.append(op("vector", lambda v, gb=gb, blk=blk: v.tensor_tensor(
                            XWG[:, blk, :].rearrange("p (a b q) -> p a b q", a=4, b=2),
                            XTC8[:, blk, 0, :].rearrange("p (a q) -> p a q", a=2)
                            .unsqueeze(1).broadcast_to([128, 4, 2, 64]),
                            WTQ[:, blk, gb * 8:gb * 8 + 8]
                            .rearrange("p (a b) -> p a b", a=4)
                            .unsqueeze(3).broadcast_to([128, 4, 2, 64]),
                            ALU.mult), waits=[wtq_tok[blk], evs[blk]]))
                    bank = 4 + gb % 2
                    wl4 = [xw_toks[-1]]
                    if pbank[bank] is not None:
                        wl4.append(pbank[bank])

                    def mms(t, gb=gb, bank=bank):
                        ins = None
                        nbl = LCPB // 128
                        for blk in range(nbl):
                            ins = t.matmul(PS[bank][:, :], XTC8[:, blk, 1, :],
                                           XWG[:, blk, :],
                                           start=(blk == 0), stop=(blk == nbl - 1))
                        return ins
                    tSt = op("tensor", mms, waits=wl4, inc=True)
                    last_st = tSt
                    wl5 = [tSt]
                    if st_upd[gb] is not None:
                        wl5.append(st_upd[gb])
                    op("vector", lambda v, gb=gb, cb=cb: v.tensor_tensor(
                        SST[:, gb * 512:(gb + 1) * 512].rearrange("p (a q) -> p a q", a=8),
                        SST[:, gb * 512:(gb + 1) * 512].rearrange("p (a q) -> p a q", a=8),
                        DCBA[:, cb, gb * 8:gb * 8 + 8].unsqueeze(2).broadcast_to([128, 8, 64]),
                        ALU.mult), waits=wl5)
                    st_upd[gb] = op("vector", lambda v, gb=gb, bank=bank: v.tensor_tensor(
                        SST[:, gb * 512:(gb + 1) * 512],
                        SST[:, gb * 512:(gb + 1) * 512],
                        PS[bank][:, :], ALU.add))
                    pbank[bank] = st_upd[gb]
                    if gb % 2 == 1:
                        yield
                yield
                ld_guard[:] = [last_st, vCl]
                wbf_guard[0] = wtq_tok[LCPB // 128 - 1]

        gen = pfxb_gen()
        bpump = [0]

        with ExitStack() as es3:
            def sb3(name, shape, dtype):
                return es3.enter_context(nc.sbuf_tensor(name, shape, dtype))

            WB = [sb3(f"WB{i}", [128, 16, 128], BF16) for i in range(2)]
            ZEV = [sb3(f"ZEV{i}", [128, 512], BF16) for i in range(2)]
            XBR = [sb3(f"XBR{i}", [128, TL], BF16) for i in range(2)]
            ACC3 = sb3("ACC3", [128, TM], F32)
            CVO = [sb3(f"CVO{i}", [128, TM], BF16) for i in range(2)]
            DTRW = sb3("DTRW", [64, TL], F32)

            bank_tok = {i: None for i in range(8)}

            strips5 = [(0, 512), (512, 512), (1024, 512), (1536, 512), (2048, 128)]
            w_free = {0: None, 1: None}
            zev_free = {0: None, 1: None}
            cvo_free = {0: None, 1: None}
            xbr_free = {0: None, 1: None}
            acc3_prev = [None]
            dt_evs = []
            for m in range(81):
                if m < 80:
                    wl = [w_free[m % 2]] if w_free[m % 2] else []
                    wW = dma("gpsimd", WB[m % 2][:], d_win[m], waits=wl)
                else:
                    wW = wPD
                mm_toks = []
                ev_toks = []
                for si, (o, w) in enumerate(strips5):
                    bank = (m * 5 + si) % 4
                    wl = [wW]
                    if bank_tok[bank] is not None:
                        wl.append(bank_tok[bank])
                    if m >= 32 and m < 80 and si == 0 and xbr_free[m % 2] is not None:
                        wl.append(xbr_free[m % 2])

                    def mmf(t, m=m, o=o, w=w, bank=bank):
                        ins = None
                        for k in range(16):
                            lhs = WB[m % 2][:, k, :] if m < 80 else WPDT[:, k, :]
                            mrows = 128 if m < 80 else 64
                            ins = t.matmul(PS[bank][0:mrows, 0:w], lhs,
                                           XN[:, k, o:o + w],
                                           start=(k == 0), stop=(k == 15))
                        return ins
                    tMM = op("tensor", mmf, waits=wl, inc=True)
                    mm_toks.append(tMM)
                    if m < 32:
                        zp = si % 2
                        wl2 = [tMM]
                        if zev_free[zp] is not None:
                            wl2.append(zev_free[zp])
                        aZ = op("scalar", lambda a, o=o, w=w, bank=bank, zp=zp: a.copy(
                            ZEV[zp][:, 0:w], PS[bank][:, 0:w]), waits=wl2)
                        bank_tok[bank] = aZ
                        zev_free[zp] = dma(
                            "sync", z_scr[m * 128:(m + 1) * 128, o:o + w],
                            ZEV[zp][:, 0:w], waits=[aZ])
                    elif m < 80:
                        aX = op("scalar", lambda a, m=m, o=o, w=w, bank=bank: a.copy(
                            XBR[m % 2][:, o:o + w], PS[bank][:, 0:w]), waits=[tMM])
                        bank_tok[bank] = aX
                        ev_toks.append(aX)
                    else:
                        aD = op("scalar", lambda a, o=o, w=w, bank=bank: a.copy(
                            DTRW[:, o:o + w], PS[bank][0:64, 0:w]), waits=[tMM])
                        bank_tok[bank] = aD
                        dt_evs.append(aD)
                w_free[m % 2] = mm_toks[-1]
                if 32 <= m < 80:
                    mi = m - 32
                    wl = list(ev_toks)
                    if cvo_free[m % 2] is not None:
                        wl.append(cvo_free[m % 2])
                    if acc3_prev[0] is not None:
                        wl.append(acc3_prev[0])
                    op("vector", lambda v, m=m, mi=mi: v.tensor_scalar(
                        ACC3[:], XBR[m % 2][:, HALO - 3:HALO - 3 + TM],
                        CONVW[:, mi, 0:1], None, ALU.mult), waits=wl)
                    vC4 = None
                    for k in range(1, 4):
                        vC4 = op("vector", lambda v, m=m, mi=mi, k=k: v.scalar_tensor_tensor(
                            ACC3[:], XBR[m % 2][:, HALO - 3 + k:HALO - 3 + k + TM],
                            CONVW[:, mi, k:k + 1], ACC3[:], ALU.mult, ALU.add))
                    xbr_free[m % 2] = vC4
                    aCV = op("scalar", lambda a, m=m, mi=mi: a.activation(
                        CVO[m % 2][:, :], ACC3[:], AF.Silu, bias=CONVB[:, mi:mi + 1]),
                        waits=[vC4])
                    acc3_prev[0] = aCV
                    cvo_free[m % 2] = dma(
                        "sync", xbc_scr[mi * 128:(mi + 1) * 128, :], CVO[m % 2][:],
                        waits=[aCV])
                if bpump[0] < PFXB_IN_PH3:
                    next(gen, None)
                    bpump[0] += 1
            if PFXB_IN_PH3 > 100:
                for _ in gen:
                    pass
            op("vector", lambda v: v.tensor_copy(ACC3[0:64, :], DTRW[:, HALO:]),
               waits=dt_evs + ([acc3_prev[0]] if acc3_prev[0] else []))
            emit_softplus(DTM[:], ACC3[0:64, :], ACC3[0:64, :], DTRW[:, 0:TM])
            barrier()

        if "sin" in dbg:
            t = dbg_tensor("sin", [128, NH * HEADDIM])
            dma("sync", t, SST[:])
            barrier()

        if "xbc" in dbg:
            t = dbg_tensor("xbc", [CONVD, TM], BF16)
            dma("sync", t, xbc_scr)
            t2 = dbg_tensor("zd", [DI, TL], BF16)
            dma("sync", t2, z_scr)
            t3 = dbg_tensor("dtm", [64, TM])
            dma("sync", t3, DTM[:])
            barrier()

        with ExitStack() as es4:
            def sb4(name, shape, dtype):
                return es4.enter_context(nc.sbuf_tensor(name, shape, dtype))

            HSG = [sb4(f"HSG{i}", [128, 128], BF16) for i in range(2)]
            BTMP = sb4("BTMP", [128, TM], BF16)
            bank_tok = {i: None for i in range(8)}
            hs_free = {0: None, 1: None}
            SC2048 = 1.0 / float(np.sqrt(2048.0))
            for g in range(16):
                wl = [hs_free[g % 2]] if hs_free[g % 2] else []
                vH = op("vector", lambda v, g=g: v.tensor_scalar(
                    HSG[g % 2][:], H128[:], SUUG[:, g:g + 1], None, ALU.mult),
                    waits=wl)
                for s4 in range(4):
                    bank = (g * 4 + s4) % 4
                    wl = [vH]
                    if bank_tok[bank] is not None:
                        wl.append(bank_tok[bank])
                    tMM = op("tensor", lambda t, g=g, s4=s4, bank=bank: t.matmul(
                        PS[bank][:, :], HSG[g % 2][:],
                        XN[:, g, HALO + s4 * 512:HALO + (s4 + 1) * 512],
                        start=True, stop=True), waits=wl, inc=True)
                    if s4 == 3:
                        hs_free[g % 2] = tMM
                    bank_tok[bank] = op("scalar", lambda a, g=g, s4=s4, bank=bank: a.activation(
                        X1[:, g, s4 * 512:(s4 + 1) * 512], PS[bank][:, :],
                        AF.Copy, scale=SC2048), waits=[tMM])
            aH128 = (S["A"], S["A"].n)
            prev = aH128
            for st4 in range(4):
                d = 1 << st4
                for g0 in range(16):
                    if g0 & d:
                        continue
                    g1 = g0 + d
                    op("vector", lambda v, g0=g0: v.tensor_copy(
                        BTMP[:], X1[:, g0, :]), waits=[prev])
                    op("vector", lambda v, g0=g0, g1=g1: v.tensor_tensor(
                        X1[:, g0, :], BTMP[:], X1[:, g1, :], ALU.add))
                    op("vector", lambda v, g0=g0, g1=g1: v.tensor_tensor(
                        X1[:, g1, :], BTMP[:], X1[:, g1, :], ALU.subtract))
                prev = (S["V"], S["V"].n)
            barrier()

        if "x1" in dbg:
            t = dbg_tensor("x1", [128, 16 * TM], BF16)
            dma("sync", t.rearrange("p (a b) -> p a b", a=16), X1[:])
            barrier()

        with ExitStack() as es5:
            def sb5(name, shape, dtype):
                return es5.enter_context(nc.sbuf_tensor(name, shape, dtype))

            WUP = [sb5(f"WUP{i}", [128, 16, 128], BF16) for i in range(2)]
            WGT = [sb5(f"WGT{i}", [128, 16, 128], BF16) for i in range(2)]
            UPT = [sb5(f"UPT{i}", [128, TM], BF16) for i in range(2)]
            GTT = [sb5(f"GTT{i}", [128, TM], BF16) for i in range(2)]
            HPB0 = sb5("HPB0", [128, TM], BF16)
            HPB = [HPB0, HPB0]
            H1T = [sb5(f"H1T{i}", [128, TM], BF16) for i in range(2)]
            bank_tok = {i: None for i in range(8)}
            w_free = {0: None, 1: None}
            buf_free = {0: None, 1: None}
            SC81 = 0.125
            for i in range(64):
                p = i % 2
                wl = [w_free[p]] if w_free[p] else []
                wU = dma("gpsimd", WUP[p][:], d_wup[i], waits=wl)
                wG = dma("gpsimd", WGT[p][:], d_wgate[i], waits=wl)
                st_toks = []

                def evac(s4, tU, tG, i=i, p=p):
                    bU, bG = s4 % 2, 2 + s4 % 2
                    aU = op("vector", lambda v, i=i, p=p, s4=s4, bU=bU: v.tensor_scalar(
                        UPT[p][:, s4 * 512:(s4 + 1) * 512], PS[bU][:, :],
                        WSCUP[:, i:i + 1], None, ALU.mult), waits=[tU])
                    aG = op("scalar", lambda a, i=i, p=p, s4=s4, bG=bG: a.activation(
                        GTT[p][:, s4 * 512:(s4 + 1) * 512], PS[bG][:, :],
                        AF.Silu, scale=WSCGATE[:, i:i + 1]), waits=[tG])
                    bank_tok[bG] = aG
                    vHP = op("vector", lambda v, i=i, p=p, s4=s4: v.scalar_tensor_tensor(
                        HPB[p][:, s4 * 512:(s4 + 1) * 512],
                        GTT[p][:, s4 * 512:(s4 + 1) * 512], SUDP[:, i:i + 1],
                        UPT[p][:, s4 * 512:(s4 + 1) * 512], ALU.mult, ALU.mult),
                        waits=[aU, aG])
                    tS1 = op("tensor", lambda t, p=p, s4=s4, bU=bU: t.matmul(
                        PS[bU][:, :], M1C[:],
                        HPB[p][:, s4 * 512:(s4 + 1) * 512],
                        start=True, stop=True), waits=[vHP, aU], inc=True)
                    aH1 = op("scalar", lambda a, p=p, s4=s4, bU=bU: a.activation(
                        H1T[p][:, s4 * 512:(s4 + 1) * 512], PS[bU][:, :],
                        AF.Copy, scale=SC81), waits=[tS1])
                    bank_tok[bU] = aH1
                    st_toks.append(aH1)

                pend = None
                last_g = None
                for s4 in range(4):
                    bU, bG = s4 % 2, 2 + s4 % 2
                    wl = [wU]
                    if bank_tok[bU] is not None:
                        wl.append(bank_tok[bU])
                    if s4 == 0 and buf_free[p] is not None:
                        wl.append(buf_free[p])

                    def mmu(t, i=i, p=p, s4=s4, bU=bU):
                        ins = None
                        for k in range(16):
                            ins = t.matmul(PS[bU][:, :], WUP[p][:, k, :],
                                           X1[:, k, s4 * 512:(s4 + 1) * 512],
                                           start=(k == 0), stop=(k == 15))
                        return ins
                    tU = op("tensor", mmu, waits=wl, inc=True)
                    wl = [wG]
                    if bank_tok[bG] is not None:
                        wl.append(bank_tok[bG])

                    def mmg(t, i=i, p=p, s4=s4, bG=bG):
                        ins = None
                        for k in range(16):
                            ins = t.matmul(PS[bG][:, :], WGT[p][:, k, :],
                                           X1[:, k, s4 * 512:(s4 + 1) * 512],
                                           start=(k == 0), stop=(k == 15))
                        return ins
                    tG = op("tensor", mmg, waits=wl, inc=True)
                    last_g = tG
                    if pend is not None:
                        pend()
                    pend = (lambda s4=s4, tU=tU, tG=tG: evac(s4, tU, tG))
                pend()
                w_free[p] = last_g
                buf_free[p] = dma("sync", h1_scr[i * 128:(i + 1) * 128, :],
                                  H1T[p][:], waits=[st_toks[-1]])
                next(gen, None)
            for _ in gen:
                pass
            barrier()
        es_b2.close()
        es_a.close()

        if "h1" in dbg:
            t = dbg_tensor("h1", [INTER, TM], BF16)
            dma("sync", t, h1_scr)
            barrier()

        with ExitStack() as es6:
            def sb6(name, shape, dtype):
                return es6.enter_context(nc.sbuf_tensor(name, shape, dtype))

            CSD = sb6("CSD", [64, TM], F32)
            ECSR = sb6("ECSR", [64, TM], BF16)
            WMB = sb6("WMB", [64, TM], BF16)
            CTMP = sb6("CTMP", [64, LC], F32)
            CTM2 = sb6("CTM2", [64, LC], F32)
            DCALL = sb6("DCALL", [64, NCH], BF16)
            Z64 = sb6("Z64", [64, LC], F32)
            XSL = [sb6(f"XSL{i}", [128, 8, 256], BF16) for i in range(2)]
            BSL = [sb6(f"BSL{i}", [128, 8, 256], BF16) for i in range(2)]
            CCB = sb6("CCB", [128, 32, 256], BF16)
            XTT = sb6("XTT", [128, 8, 128], BF16)
            BTT = sb6("BTT", [128, 8, 128], BF16)
            WTT = sb6("WTT", [128, 64], BF16)
            CST = sb6("CST", [128, 64], F32)
            CSREP0 = sb6("CSREP0", [128, 2048], F32)
            CSREP = [CSREP0, CSREP0]
            ECSREP = [sb6(f"ECSREP{i}", [128, 2048], BF16) for i in range(2)]
            T1B0 = sb6("T1B0", [128, 2048], F32)
            T1B = [T1B0, T1B0]
            LTB = [sb6(f"LTB{i}", [128, 2048], BF16) for i in range(2)]
            GTS = [sb6(f"GTS{i}", [128, 1024], BF16) for i in range(2)]
            GLB = [sb6(f"GLB{i}", [128, 2048], BF16) for i in range(2)]
            CSBQ = [sb6(f"CSBQ{i}", [128, 2048], BF16) for i in range(2)]
            XWS = [sb6(f"XWS{i}", [128, 4, 2, 64], BF16) for i in range(2)]
            SSB = sb6("SSB", [128, NH * HEADDIM], BF16)
            Y4 = sb6("Y4", [128, 32, 256], BF16)
            DCC6 = sb6("DCC6", [64, 1], BF16)
            DCB6 = sb6("DCB6", [128, 64], F32)
            DIALL = sb6("DIALL", [128, NH * 128], BF16)
            DBRD = sb6("DBRD", [128, 64], BF16)

            tD = op("tensor", lambda t: t.matmul(PS[0][:, 0:64], ONES1[:], DROW[:],
                                                 start=True, stop=True),
                    waits=[wG0], inc=True)
            vDB = op("vector", lambda v: v.tensor_copy(DBRD[:], PS[0][:, 0:64]),
                     waits=[tD])
            op("vector", lambda v: v.tensor_tensor(
                DIALL[:].rearrange("p (h l) -> p h l", h=64),
                IDENT[:].unsqueeze(1).broadcast_to([128, 64, 128]),
                DBRD[:].unsqueeze(2).broadcast_to([128, 64, 128]),
                ALU.mult))

            vZ = op("vector", lambda v: v.memset(Z64[:], 0.0))
            aLDT = op("scalar", lambda a: a.activation(CSD[:], DTM[:], AF.Ln))
            wCS = None
            cs_spill = {}
            ecs_spill = {}
            for c in range(NCH):
                slc = slice(c * LC, (c + 1) * LC)
                op("vector", lambda v, slc=slc: v.tensor_scalar(
                    CTM2[:], DTM[:, slc], AH[:], None, ALU.mult))
                vC = op("vector", lambda v: v.tensor_tensor_scan(
                    CTMP[:], CTM2[:], Z64[:], 0.0, ALU.add, ALU.add),
                    waits=[vZ] + ([wCS] if wCS else []))
                wCS = dma("sync",
                          csm_scr[c * 4:c * 4 + 4, :].rearrange("q (h l) -> (q h) l", h=16),
                          CTMP[:], waits=[vC])
                cs_spill[c] = wCS
                op("scalar", lambda a, slc=slc: a.activation(
                    ECSR[:, slc], CTMP[:], AF.Exp), waits=[vC])
                wEC = dma("sync",
                          ecs_scr[c * 4:c * 4 + 4, :].rearrange("q (h l) -> (q h) l", h=16),
                          ECSR[:, slc], waits=[(S["A"], S["A"].n)])
                ecs_spill[c] = wEC
                op("scalar", lambda a, c=c: a.activation(
                    DCALL[:, c:c + 1], CTMP[:, LC - 1:LC], AF.Exp))
                op("vector", lambda v: v.tensor_tensor(
                    CTM2[:], CTMP[:, LC - 1:LC].broadcast_to([64, LC]),
                    CTMP[:], ALU.subtract))
                aD = op("scalar", lambda a: a.activation(CTM2[:], CTM2[:], AF.Exp),
                        waits=[(S["V"], S["V"].n)])
                op("vector", lambda v, slc=slc: v.tensor_tensor(
                    WMB[:, slc], CTM2[:], DTM[:, slc], ALU.mult), waits=[aD])
                op("vector", lambda v, slc=slc: v.tensor_tensor(
                    CSD[:, slc], CTMP[:], CSD[:, slc], ALU.subtract), waits=[aLDT])
            vWMB = (S["V"], S["V"].n)
            vCSD = vWMB
            wECS = wEC

            bank_tok = {i: None for i in range(8)}
            bank_tok[0] = vDB
            slab_tok = {0: None, 1: None}
            slab_used = {0: None, 1: None}
            cc_tok = None
            cc_used = None
            rep_used = {0: None, 1: None}
            y4_used = None
            xw_used = {0: None, 1: None}
            dcc_prev = None
            dcbb_prev6 = None
            y_last_prev = None
            glb_tok = {}
            cs_q_tok = {}
            lt_tok = {}
            gts_tok = {}
            rep_tok = {}
            t1_tok = {}

            def stage_rep(c, q4):
                qp = q4 % 2
                prev_t1 = t1_tok.get((c, q4 - 1) if q4 >= 1 else (c - 1, 3))
                wl = [cs_spill[c]] + ([prev_t1] if prev_t1 else [])
                r1 = dma("sync", CSREP[qp][:],
                         csm_scr[c * 4 + q4:c * 4 + q4 + 1, :]
                         .broadcast_to([128, 2048]), waits=wl)
                prev_cs = cs_q_tok.get((c, q4 - 2) if q4 >= 2 else (c - 1, q4 + 2))
                wl2 = [ecs_spill[c]] + ([prev_cs] if prev_cs else [])
                r2 = dma("sync", ECSREP[qp][:],
                         ecs_scr[c * 4 + q4:c * 4 + q4 + 1, :]
                         .broadcast_to([128, 2048]), waits=wl2)
                rep_tok[(c, q4)] = [r1, r2]

            def stage_t1(c, q4, vT3):
                qp = q4 % 2
                wl = rep_tok[(c, q4)] + [vT3]
                if (c, q4, "lt") in lt_tok:
                    pass
                prev = lt_tok.get((c, q4 - 1) if q4 >= 1 else (c - 1, 3))
                if prev is not None:
                    wl.append(prev)
                op("vector", lambda v, c=c, q4=q4, qp=qp: v.tensor_tensor(
                    T1B[qp][:].rearrange("p (h l) -> p h l", h=16),
                    CSREP[qp][:].rearrange("p (h l) -> p h l", h=16),
                    CST[:, 16 * q4:16 * q4 + 16].unsqueeze(2)
                    .broadcast_to([128, 16, 128]), ALU.subtract), waits=wl)
                t1_tok[(c, q4)] = op("vector", lambda v, qp=qp: v.tensor_tensor(
                    T1B[qp][:].rearrange("p (h l) -> p h l", l=128),
                    T1B[qp][:].rearrange("p (h l) -> p h l", l=128),
                    SLNEG[:].unsqueeze(1).broadcast_to([128, 16, 128]), ALU.add))
                lt_tok[(c, q4)] = op("scalar", lambda a, qp=qp: a.activation(
                    LTB[qp][:], T1B[qp][:], AF.Exp), waits=[t1_tok[(c, q4)]])

            def stage_gt(c, q4, p2, csl):
                qp = q4 % 2
                gt_mm = []
                for gg in range(8):
                    gc = q4 * 8 + gg
                    bank = gg % 2
                    slot = gg // 2
                    wl = list(cc_tok)
                    if slot == 0 and bank_tok[bank] is not None:
                        wl.append(bank_tok[bank])
                    gt_mm.append(op("tensor", lambda t, gc=gc, bank=bank, slot=slot, p2=p2, csl=csl: t.matmul(
                        PS[bank][:, slot * 128:(slot + 1) * 128],
                        BSL[p2][:, (gc // 4) % 8, csl:csl + 128],
                        CCB[:, gc, csl:csl + 128],
                        start=True, stop=True), waits=wl, inc=True))
                prevg = gts_tok.get((c, q4 - 2) if q4 >= 2 else (c - 1, q4 + 2))
                wl = [gt_mm[6]] + ([prevg] if prevg else [])
                aG1 = op("scalar", lambda a, qp=qp: a.activation(
                    GTS[qp][:].rearrange("p (k two l) -> p k two l", two=2, l=128)[:, :, 0, :],
                    PS[0][:].rearrange("p (k l) -> p k l", k=4), AF.Copy), waits=wl)
                aG2 = op("scalar", lambda a, qp=qp: a.activation(
                    GTS[qp][:].rearrange("p (k two l) -> p k two l", two=2, l=128)[:, :, 1, :],
                    PS[1][:].rearrange("p (k l) -> p k l", k=4), AF.Copy),
                    waits=[gt_mm[7]])
                bank_tok[0] = aG1
                bank_tok[1] = aG2
                gts_tok[(c, q4)] = aG2

            def stage_mid(c, q4, p2, csl, vSSB, vT2, trx):
                qp = q4 % 2
                wl = [lt_tok[(c, q4)], gts_tok[(c, q4)], bank_tok[0]]
                prev = glb_tok.get((c, q4 - 2) if q4 >= 2 else (c - 1, q4 + 2))
                if prev is not None:
                    wl.append(prev)
                op("vector", lambda v, qp=qp: v.tensor_tensor(
                    GLB[qp][:].rearrange("p (h two l) -> p h two l", two=2, l=128),
                    LTB[qp][:].rearrange("p (h two l) -> p h two l", two=2, l=128),
                    GTS[qp][:].rearrange("p (h l) -> p h l", h=8).unsqueeze(2)
                    .broadcast_to([128, 8, 2, 128]), ALU.mult), waits=wl)
                glb_tok[(c, q4)] = op("vector", lambda v, q4=q4, qp=qp: v.tensor_tensor(
                    GLB[qp][:], GLB[qp][:],
                    DIALL[:, q4 * 2048:(q4 + 1) * 2048], ALU.add))
                wl = rep_tok[(c, q4)][1:]
                prev = cs_q_tok.get((c, q4 - 2) if q4 >= 2 else (c - 1, q4 + 2))
                if prev is not None:
                    wl.append(prev)
                cs_q_tok[(c, q4)] = op("vector", lambda v, q4=q4, qp=qp, csl=csl: v.tensor_tensor(
                    CSBQ[qp][:].rearrange("p (h two l) -> p h two l", two=2, l=128),
                    CCB[:, q4 * 8:(q4 + 1) * 8, csl:csl + 128].unsqueeze(2)
                    .broadcast_to([128, 8, 2, 128]),
                    ECSREP[qp][:].rearrange("p (h two l) -> p h two l", two=2, l=128),
                    ALU.mult), waits=wl)
                st_toks = {}
                for gg in range(2):
                    gb = 2 * q4 + gg
                    bank = 2 + gg
                    xwp = gg
                    wl = [vT2, trx]
                    if xw_used[xwp] is not None:
                        wl.append(xw_used[xwp])
                    vXW = op("vector", lambda v, gb=gb, xwp=xwp, c=c: v.tensor_tensor(
                        XWS[xwp][:],
                        XTT[:, gb, :].rearrange("p (a q) -> p a q", a=2)
                        .unsqueeze(1).broadcast_to([128, 4, 2, 64]),
                        WTT[:, gb * 8:gb * 8 + 8]
                        .rearrange("p (a b) -> p a b", a=4)
                        .unsqueeze(3).broadcast_to([128, 4, 2, 64]),
                        ALU.mult), waits=wl)
                    wl = [vXW]
                    if bank_tok[bank] is not None:
                        wl.append(bank_tok[bank])
                    tST = op("tensor", lambda t, gb=gb, bank=bank, xwp=xwp: t.matmul(
                        PS[bank][:, :], BTT[:, gb, :],
                        XWS[xwp][:].rearrange("p a b q -> p (a b q)"),
                        start=True, stop=True), waits=wl, inc=True)
                    st_toks[gg] = tST
                    xw_used[xwp] = tST
                return st_toks

            def stage_y(c, q4, csl, vSSB, vDCB, st_toks):
                qp = q4 % 2
                y_toks = []
                for gg in range(8):
                    g = q4 * 8 + gg
                    bank = 6 + gg % 2
                    slot = gg // 2
                    h0, h1 = 2 * g, 2 * g + 1
                    wl = [glb_tok[(c, q4)], cs_q_tok[(c, q4)], vSSB]
                    if slot == 0:
                        wl.append(bank_tok[bank])

                    def ymm(t, gg=gg, bank=bank, slot=slot, h0=h0, h1=h1, qp=qp):
                        xs0 = XTT[:, h0 // 8, (h0 % 2) * 64:(h0 % 2) * 64 + 64]
                        xs1 = XTT[:, h1 // 8, (h1 % 2) * 64:(h1 % 2) * 64 + 64]
                        sl = slice(slot * 128, (slot + 1) * 128)
                        t.matmul(PS[bank][0:64, sl], xs0,
                                 GLB[qp][:, (gg * 2) * 128:(gg * 2 + 1) * 128],
                                 start=True, stop=False, tile_position=(0, 0))
                        t.matmul(PS[bank][0:64, sl], SSB[:, h0 * 64:h0 * 64 + 64],
                                 CSBQ[qp][:, (gg * 2) * 128:(gg * 2 + 1) * 128],
                                 start=False, stop=True, tile_position=(0, 0))
                        t.matmul(PS[bank][64:128, sl], xs1,
                                 GLB[qp][:, (gg * 2 + 1) * 128:(gg * 2 + 2) * 128],
                                 start=True, stop=False, tile_position=(0, 64))
                        return t.matmul(PS[bank][64:128, sl],
                                        SSB[:, h1 * 64:h1 * 64 + 64],
                                        CSBQ[qp][:, (gg * 2 + 1) * 128:(gg * 2 + 2) * 128],
                                        start=False, stop=True,
                                        tile_position=(0, 64))
                    y_toks.append(op("tensor", ymm, waits=wl, inc=True))
                for bb in range(2):
                    bank = 6 + bb
                    wl = [y_toks[6 + bb]]
                    if y4_used is not None:
                        wl.extend(y4_used)
                    aY = op("scalar", lambda a, q4=q4, bb=bb, bank=bank, csl=csl: a.activation(
                        Y4[:].rearrange("p (a b2) q -> p a b2 q", b2=2)
                        [:, q4 * 4:(q4 + 1) * 4, bb, csl:csl + 128],
                        PS[bank][:].rearrange("p (j q) -> p j q", j=4),
                        AF.Copy), waits=wl)
                    bank_tok[bank] = aY
                for gg in range(2):
                    gb = 2 * q4 + gg
                    bank = 2 + gg
                    wl = [vDCB, st_toks[gg], vSSB]
                    op("vector", lambda v, gb=gb: v.tensor_tensor(
                        SST[:, gb * 512:(gb + 1) * 512].rearrange("p (a q) -> p a q", a=8),
                        SST[:, gb * 512:(gb + 1) * 512].rearrange("p (a q) -> p a q", a=8),
                        DCB6[:, gb * 8:gb * 8 + 8].unsqueeze(2).broadcast_to([128, 8, 64]),
                        ALU.mult), waits=wl)
                    bank_tok[bank] = op("vector", lambda v, gb=gb, bank=bank: v.tensor_tensor(
                        SST[:, gb * 512:(gb + 1) * 512],
                        SST[:, gb * 512:(gb + 1) * 512],
                        PS[bank][:, :], ALU.add))
                return y_toks[-1]

            stage_rep(0, 0)
            for c in range(NCH):
                c2 = c // 2
                p2 = c2 % 2
                csl = 128 * (c % 2)
                if c % 2 == 0:
                    wl = [slab_used[p2]] if slab_used[p2] else []
                    for xg in range(8):
                        dma("gpsimd", XSL[p2][:, xg, :],
                            xbc_scr[xg * 128:(xg + 1) * 128, c * LC:(c + 2) * LC],
                            waits=wl if xg == 0 else [])
                    for xg in range(8):
                        dma("gpsimd", BSL[p2][:, xg, :],
                            xbc_scr[1024 + xg * 128:1024 + (xg + 1) * 128,
                                    c * LC:(c + 2) * LC])
                    wl = [cc_used] if cc_used else []
                    for gc in range(32):
                        dma("gpsimd", CCB[:, gc, :],
                            xbc_scr[2048 + gc * 128:2048 + (gc + 1) * 128,
                                    c * LC:(c + 2) * LC],
                            waits=wl if gc == 0 else [])
                    slab_all = B.dma_tokens("gpsimd")
                    slab_tok[p2] = slab_all
                    cc_tok = slab_all
                trev = []
                for half in range(4):
                    bank = 6 + half % 2
                    wl = list(slab_tok[p2])
                    if bank_tok[bank] is not None:
                        wl.append(bank_tok[bank])
                    if y_last_prev is not None:
                        wl.append(y_last_prev)

                    def tfn(t, half=half, p2=p2, csl=csl, bank=bank):
                        ins = None
                        src_t = XSL[p2] if half < 2 else BSL[p2]
                        for j in range(4):
                            xg = (half % 2) * 4 + j
                            ins = t.transpose(
                                ptb(bank)[:, j * 128:(j + 1) * 128],
                                src_t[:, xg, csl:csl + 128], IDENT[:])
                        return ins
                    tT = op("tensor", tfn, waits=wl, inc=True)

                    def evf(v, half=half, bank=bank):
                        dst = XTT if half < 2 else BTT
                        gi = (half % 2) * 4
                        return v.tensor_copy(
                            dst[:, gi:gi + 4, :],
                            ptb(bank).rearrange("p (j q) -> p j q", j=8)[:, 0:4, :])
                    bank_tok[bank] = op("vector", evf, waits=[tT])
                    trev.append(bank_tok[bank])
                trx = trev[-1]
                wl = [vWMB] + ([bank_tok[6]] if bank_tok[6] else [])
                t2 = op("tensor", lambda t, c=c: t.transpose(
                    ptb(6)[:, 128:192], WMB[:, c * LC:(c + 1) * LC], IDENT[0:64, 0:64]),
                    waits=wl, inc=True)
                vT2 = op("vector", lambda v: v.tensor_copy(WTT[:], ptb(6)[:, 128:192]),
                         waits=[t2])
                bank_tok[6] = vT2
                wl = [vCSD] + ([bank_tok[7]] if bank_tok[7] else [])
                tCS = op("tensor", lambda t, c=c: t.transpose(
                    PS[7][:, 0:64], CSD[:, c * LC:(c + 1) * LC], IDF[0:64, 0:64]),
                    waits=wl, inc=True)
                vT3 = op("vector", lambda v: v.tensor_copy(CST[:], PS[7][:, 0:64]),
                         waits=[tCS])
                bank_tok[7] = vT3
                wl = [vWMB] + ([dcc_prev] if dcc_prev else [])
                aDC = op("scalar", lambda a, c=c: a.copy(
                    DCC6[:], DCALL[:, c:c + 1]), waits=wl)
                wD0 = dma("sync", dcc_scr, DCC6[:], waits=[aDC])
                dcc_prev = wD0
                wl = [wD0] + ([dcbb_prev6] if dcbb_prev6 else [])
                wDB6 = dma("sync", DCBB[:],
                           dcc_scr.rearrange("a b -> b a").broadcast_to([128, 64]),
                           waits=wl)
                vDCB = op("vector", lambda v: v.tensor_copy(DCB6[:], DCBB[:]),
                          waits=[wDB6])
                dcbb_prev6 = vDCB
                wl = [(S["V"], S["V"].n)] + ([y_last_prev] if y_last_prev else [])
                vSSB = op("vector", lambda v: v.tensor_copy(SSB[:], SST[:]), waits=wl)
                stage_t1(c, 0, vT3)
                stage_gt(c, 0, p2, csl)
                for q4 in range(4):
                    if q4 + 1 < 4:
                        stage_rep(c, q4 + 1)
                        stage_t1(c, q4 + 1, vT3)
                        stage_gt(c, q4 + 1, p2, csl)
                    elif c + 1 < NCH:
                        stage_rep(c + 1, 0)
                    st_toks = stage_mid(c, q4, p2, csl, vSSB, vT2, trx)
                    y_last_prev = stage_y(c, q4, csl, vSSB, vDCB, st_toks)
                    qp = q4 % 2
                    rep_used[qp] = y_last_prev
                if c % 2 == 1:
                    wl = [(S["A"], S["A"].n)]
                    for j in range(32):
                        dma("sync", y_scr[j * 128:(j + 1) * 128,
                                          (c - 1) * LC:(c + 1) * LC],
                            Y4[:, j, :], waits=wl if j == 0 else [])
                    y4_used = B.dma_tokens("sync")
                    slab_used[p2] = y_last_prev
                    cc_used = y_last_prev
            barrier()

        es_mid.close()

        if "ys" in dbg:
            t = dbg_tensor("ys", [DI, TM], BF16)
            dma("sync", t, y_scr)
            barrier()

        SC82 = float(2.0 ** -3.5)
        with ExitStack() as es7:
            def sb7(name, shape, dtype):
                return es7.enter_context(nc.sbuf_tensor(name, shape, dtype))

            X2H = sb7("X2H", [128, 64, 1024], BF16)
            WDB = [sb7(f"WDB{i}", [128, 64, 128], BF16) for i in range(2)]
            G0T = [sb7(f"G0T{i}", [128, 512], BF16) for i in range(2)]
            h1v = h1_scr.rearrange("(q v) t -> v q t", v=64)
            bank_tok7 = {i: None for i in range(8)}
            wdb_free = {0: None, 1: None}
            g0_free = {0: None, 1: None}
            x2_used = None
            for half in range(2):
                ho = half * 1024
                for v64 in range(64):
                    wl = [x2_used] if x2_used else []
                    wH = dma("sync", X2H[:, v64, :],
                             h1v[v64, :, ho:ho + 1024], waits=wl)
                    for s2 in range(2):
                        bank = (v64 * 2 + s2) % 6
                        wl = [wH]
                        if bank_tok7[bank] is not None:
                            wl.append(bank_tok7[bank])
                        tMM = op("tensor", lambda t, v64=v64, s2=s2, bank=bank: t.matmul(
                            PS[bank][:, :], H128[:],
                            X2H[:, v64, s2 * 512:(s2 + 1) * 512],
                            start=True, stop=True), waits=wl, inc=True)
                        bank_tok7[bank] = op("scalar", lambda a, v64=v64, s2=s2, bank=bank: a.activation(
                            X2H[:, v64, s2 * 512:(s2 + 1) * 512], PS[bank][:, :],
                            AF.Copy, scale=SC82), waits=[tMM])
                aX2 = (S["A"], S["A"].n)
                for m in range(16):
                    p = m % 2
                    wl = [wdb_free[p]] if wdb_free[p] else []
                    wW = dma("gpsimd", WDB[p][:], d_wdown[m], waits=wl)
                    for s2 in range(2):
                        bank = 6 + s2
                        gp = s2
                        wl = [wW, aX2]
                        if bank_tok7[bank] is not None:
                            wl.append(bank_tok7[bank])

                        def mmd(t, p=p, s2=s2, bank=bank):
                            ins = None
                            for v64 in range(64):
                                ins = t.matmul(PS[bank][:, :], WDB[p][:, v64, :],
                                               X2H[:, v64, s2 * 512:(s2 + 1) * 512],
                                               start=(v64 == 0), stop=(v64 == 63))
                            return ins
                        tMM = op("tensor", mmd, waits=wl, inc=True)
                        if s2 == 1:
                            wdb_free[p] = tMM
                        wl = [tMM]
                        if g0_free[gp] is not None:
                            wl.append(g0_free[gp])
                        aG0 = op("scalar", lambda a, m=m, gp=gp, bank=bank: a.activation(
                            G0T[gp][:], PS[bank][:, :], AF.Silu,
                            scale=WSCDP[:, m:m + 1]), waits=wl)
                        bank_tok7[bank] = aG0
                        g0_free[gp] = dma(
                            "gpsimd", g_scr[m * 128:(m + 1) * 128,
                                            ho + s2 * 512:ho + (s2 + 1) * 512],
                            G0T[gp][:], waits=[aG0])
                x2_used = bank_tok7[7]
            barrier()

        if "gs" in dbg:
            t = dbg_tensor("gs", [DM, TM], BF16)
            dma("sync", t, g_scr)
            barrier()

        es_b = ExitStack()

        def sbb(name, shape, dtype):
            return es_b.enter_context(nc.sbuf_tensor(name, shape, dtype))

        YN = sbb("YN", [128, 32, TM], BF16)
        with ExitStack() as es8:
            def sb8(name, shape, dtype):
                return es8.enter_context(nc.sbuf_tensor(name, shape, dtype))

            OG8 = sb8("OG8", [128, 32, 32], BF16)
            EB8 = sb8("EB8", [32, 32, 128], BF16)
            YT = [sb8(f"YT{i}", [128, TM], BF16) for i in range(2)]
            ZT = [sb8(f"ZT{i}", [128, TM], BF16) for i in range(2)]
            ZS = sb8("ZS", [128, TM], BF16)
            SQG = sb8("SQG", [128, TM], BF16)
            SMS = sb8("SMS", [32, TM], F32)
            RSTB = sb8("RSTB", [32, TM], BF16)
            dma("sync", OG8[:], og_c.rearrange("(g p) m -> p g m", g=32))
            dma("sync", EB8[:], eb_c.rearrange("(g k) m -> k g m", g=32))
            wC8 = B.dma_tokens("sync")
            yz_free = {0: None, 1: None}
            sq_toks = []
            for grp in range(32):
                p = grp % 2
                wl = ([yz_free[p]] if yz_free[p] else [])
                wY = dma("sync", YT[p][:], y_scr[grp * 128:(grp + 1) * 128, :], waits=wl)
                wZ = dma("sync", ZT[p][:], z_scr[grp * 128:(grp + 1) * 128, HALO:], waits=wl)
                aZ = op("scalar", lambda a, p=p: a.activation(ZS[:], ZT[p][:], AF.Silu),
                        waits=[wZ, (S["V"], S["V"].n)])
                vYG = op("vector", lambda v, grp=grp, p=p: v.tensor_tensor(
                    YN[:, grp, :], YT[p][:], ZS[:], ALU.mult), waits=[wY, aZ])
                yz_free[p] = vYG
                aSQ = op("scalar", lambda a, grp=grp: a.activation(
                    SQG[:], YN[:, grp, :], AF.Square),
                    waits=[vYG] + (sq_toks[-4:] if grp >= 1 else []))
                sq4 = []
                for s4 in range(4):
                    wl = [aSQ] + (wC8 if grp == 0 else [])
                    sq4.append(op("tensor", lambda t, grp=grp, s4=s4: t.matmul(
                        PS[s4][0:32, :], OG8[:, grp, :],
                        SQG[:, s4 * 512:(s4 + 1) * 512],
                        start=(grp == 0), stop=(grp == 31)), waits=wl, inc=True))
                sq_toks = sq4
            ev4 = []
            for s4 in range(4):
                ev4.append(op("vector", lambda v, s4=s4: v.tensor_copy(
                    SMS[:, s4 * 512:(s4 + 1) * 512], PS[s4][0:32, :]),
                    waits=[sq_toks[s4]]))
            aL = op("scalar", lambda a: a.activation(
                SMS[:], SMS[:], AF.Ln, scale=1.0 / 128.0, bias=EPSC[0:32, :]),
                waits=ev4)
            aR8 = op("scalar", lambda a: a.activation(
                SMS[:], SMS[:], AF.Exp, scale=-0.5))
            vRB = op("vector", lambda v: v.tensor_copy(RSTB[:], SMS[:]), waits=[aR8])
            bank_tok8 = {i: None for i in range(8)}
            for grp in range(32):
                for s4 in range(4):
                    bank = (grp * 4 + s4) % 8
                    wl = [vRB]
                    if bank_tok8[bank] is not None:
                        wl.append(bank_tok8[bank])
                    tB = op("tensor", lambda t, grp=grp, s4=s4, bank=bank: t.matmul(
                        PS[bank][:, :], EB8[:, grp, :],
                        RSTB[0:32, s4 * 512:(s4 + 1) * 512],
                        start=True, stop=True), waits=wl, inc=True)
                    bank_tok8[bank] = op("vector", lambda v, grp=grp, s4=s4, bank=bank: v.scalar_tensor_tensor(
                        YN[:, grp, s4 * 512:(s4 + 1) * 512],
                        YN[:, grp, s4 * 512:(s4 + 1) * 512],
                        NW[:, grp:grp + 1], PS[bank][:, :],
                        ALU.mult, ALU.mult), waits=[tB])
            barrier()

        if "yn" in dbg:
            t = dbg_tensor("yn", [128, 32 * TM], BF16)
            dma("sync", t.rearrange("p (a b) -> p a b", a=32), YN[:])
            barrier()

        with ExitStack() as es9:
            def sb9(name, shape, dtype):
                return es9.enter_context(nc.sbuf_tensor(name, shape, dtype))

            WOB = [sb9(f"WOB{i}", [128, 32, 128], BF16) for i in range(2)]
            GT9 = [sb9(f"GT9{i}", [128, TM], BF16) for i in range(2)]
            SOT = sb9("SOT", [128, 512], BF16)
            SGT = [sb9(f"SGT{i}", [128, TM], BF16) for i in range(2)]
            bank_tok9 = {i: None for i in range(8)}
            w_free9 = {0: None, 1: None}
            sg_free = {0: None, 1: None}
            for m in range(16):
                p = m % 2
                wl = [w_free9[p]] if w_free9[p] else []
                wW = dma("gpsimd", WOB[p][:], d_wout[m], waits=wl)
                wG = dma("sync", GT9[p][:], g_scr[m * 128:(m + 1) * 128, :],
                         waits=[sg_free[p]] if sg_free[p] else [])
                for s4 in range(4):
                    bank = (m * 4 + s4) % 8

                    def mmo(t, p=p, s4=s4, bank=bank):
                        ins = None
                        for k in range(32):
                            ins = t.matmul(PS[bank][:, :], WOB[p][:, k, :],
                                           YN[:, k, s4 * 512:(s4 + 1) * 512],
                                           start=(k == 0), stop=(k == 31))
                        return ins
                    wl = [wW]
                    if bank_tok9[bank] is not None:
                        wl.append(bank_tok9[bank])
                    tMM = op("tensor", mmo, waits=wl, inc=True)
                    if s4 == 3:
                        w_free9[p] = tMM
                    aS = op("scalar", lambda a, s4=s4, bank=bank: a.activation(
                        SOT[:], PS[bank][:, :], AF.Copy),
                        waits=[tMM, (S["V"], S["V"].n)])
                    bank_tok9[bank] = aS
                    op("vector", lambda v, p=p, s4=s4: v.tensor_tensor(
                        SGT[p][:, s4 * 512:(s4 + 1) * 512], SOT[:],
                        GT9[p][:, s4 * 512:(s4 + 1) * 512], ALU.mult),
                        waits=[aS, wG])
                sg_free[p] = dma("sync", sg_scr[m * 128:(m + 1) * 128, :], SGT[p][:],
                                 waits=[(S["V"], S["V"].n)])
            barrier()
        es_b.close()

        with ExitStack() as es10:
            def sb10(name, shape, dtype):
                return es10.enter_context(nc.sbuf_tensor(name, shape, dtype))

            OUTS = sb10("OUTS", [128, 16, DM], BF16)
            SGB = [sb10(f"SGB{i}", [128, TM], BF16) for i in range(2)]
            UB = [sb10(f"UB{i}", [128, DM], F32) for i in range(2)]
            OUTF = [sb10(f"OUTF{i}", [128, DM], F32) for i in range(2)]
            bank_tokX = {6: None, 7: None}
            sgb_free = {0: None, 1: None}
            for m in range(16):
                p = m % 2
                wl = [sgb_free[p]] if sgb_free[p] else []
                wS9 = dma("sync", SGB[p][:], sg_scr[m * 128:(m + 1) * 128, :], waits=wl)
                for half in range(4):
                    bank = 6 + half % 2
                    wl = [wS9]
                    if bank_tokX[bank] is not None:
                        wl.append(bank_tokX[bank])

                    def tfn(t, p=p, half=half, bank=bank):
                        ins = None
                        for j in range(4):
                            tt = half * 4 + j
                            ins = t.transpose(
                                ptb(bank)[:, j * 128:(j + 1) * 128],
                                SGB[p][:, tt * 128:(tt + 1) * 128], IDENT[:])
                        return ins
                    tT = op("tensor", tfn, waits=wl, inc=True)
                    bank_tokX[bank] = op("vector", lambda v, m=m, half=half, bank=bank: v.tensor_copy(
                        OUTS[:, half * 4:half * 4 + 4, m * 128:(m + 1) * 128],
                        ptb(bank).rearrange("p (j q) -> p j q", j=8)[:, 0:4, :]),
                        waits=[tT])
                sgb_free[p] = (S["V"], S["V"].n)
            vTdone = (S["V"], S["V"].n)
            ub_free = {0: None, 1: None}
            for tt in range(16):
                p = tt % 2
                wl = [ub_free[p]] if ub_free[p] else []
                wU = dma("sync", UB[p][:],
                         d_u[HALO + tt * 128:HALO + (tt + 1) * 128, :], waits=wl)
                vO = op("vector", lambda v, tt=tt, p=p: v.tensor_tensor(
                    OUTF[p][:], UB[p][:], OUTS[:, tt, :], ALU.add),
                    waits=[wU, vTdone])
                ub_free[p] = dma("sync", d_out[tt * 128:(tt + 1) * 128, :],
                                 OUTF[p][:], waits=[vO])
            barrier()

        for (c, t) in B.dma_tokens("sync"):
            B.raw("sync", lambda e, c=c, t=t: e.wait_ge(c.h, t))
        for (c, t) in B.dma_tokens("gpsimd"):
            B.raw("gpsimd", lambda e, c=c, t=t: e.wait_ge(c.h, t))
        B.emit()
    return nc, dbg_outs


def _blk_weightT(wT, mt_w=128, kt=16):
    cin, cout = wT.shape
    assert cin == kt * 128
    blocks = []
    for m0 in range(0, cout, mt_w):
        w = min(mt_w, cout - m0)
        b = wT[:, m0:m0 + w].reshape(kt, 128, w).transpose(1, 0, 2)
        blocks.append(np.ascontiguousarray(b, dtype=np.float32))
    return blocks


def _host_prep(inputs):
    u = np.asarray(inputs["L0_attn_u"])
    WinT = np.ascontiguousarray(np.asarray(inputs["L0_attn_W_in"]).T)
    WupT = np.ascontiguousarray(np.asarray(inputs["L0_ffn_W_up"]).T)
    WgateT = np.ascontiguousarray(np.asarray(inputs["L0_ffn_W_gate"]).T)
    WoutT = np.ascontiguousarray(np.asarray(inputs["L0_attn_W_out"]).T)
    Wdown = np.asarray(inputs["L0_ffn_W_down"])

    win_blk = np.stack(_blk_weightT(WinT[:, :10240], 128, 16))
    win_dt_blk = _blk_weightT(WinT[:, 10240:10304], 64, 16)[0]
    wup_blk = np.stack(_blk_weightT(WupT, 128, 16))
    wgate_blk = np.stack(_blk_weightT(WgateT, 128, 16))
    wout_blk = np.stack(_blk_weightT(WoutT, 128, 32))

    qp = np.arange(128)
    vv = np.arange(64)
    cmap = (qp[None, :] >> 1) * 128 + (qp[None, :] & 1) * 64 + vv[:, None]
    WdT_perm = Wdown.T[cmap.reshape(-1), :].reshape(64, 128, 2048)
    wdown_blk = np.ascontiguousarray(
        WdT_perm.reshape(64, 128, 16, 128).transpose(2, 1, 0, 3), dtype=np.float32)

    def pack_pp(vec, ntile):
        v = np.asarray(vec, np.float32)
        return np.ascontiguousarray(v.reshape(ntile, 128).T)

    convw_pk = np.ascontiguousarray(
        np.asarray(inputs["L0_attn_conv_w"], np.float32).reshape(48, 128, 4).transpose(1, 0, 2))
    convb_pk = pack_pp(inputs["L0_attn_conv_b"], 48)
    wsc = np.asarray(inputs["L0_ffn_Wscale_ug"], np.float32)

    shared = dict(
        win_blk=win_blk, win_dt_blk=win_dt_blk, wup_blk=wup_blk,
        wgate_blk=wgate_blk, wdown_blk=wdown_blk, wout_blk=wout_blk,
        convw_pk=convw_pk, convb_pk=convb_pk,
        dtbias=np.asarray(inputs["L0_attn_dt_bias"], np.float32).reshape(64, 1),
        alog=np.asarray(inputs["L0_attn_A_log"], np.float32).reshape(64, 1),
        Drow=np.asarray(inputs["L0_attn_D"], np.float32).reshape(1, 64),
        nw_pk=pack_pp(inputs["L0_attn_norm_weight"], 32),
        suug_pk=pack_pp(inputs["L0_ffn_SU_ug"], 16),
        sudp_pk=pack_pp(inputs["L0_ffn_SU_dp"], 64),
        wscup_pk=pack_pp(wsc[:INTER], 64),
        wscgate_pk=pack_pp(wsc[INTER:], 64),
        wscdp_pk=pack_pp(inputs["L0_ffn_Wscale_dp"], 16),
    )

    in_maps = []
    for core in range(8):
        b, q = core // 4, core % 4
        start = q * TM
        u_loc = np.zeros((TL, DM), np.float32)
        lo = max(0, start - HALO)
        u_loc[HALO - (start - lo):] = u[b, lo:start + TM]
        u_pfx = np.zeros((PFX, DM), np.float32)
        plo = max(0, start - PFX)
        nreal = start - plo
        if nreal > 0:
            u_pfx[PFX - nreal:] = u[b, plo:start]
        dt_mask = np.zeros((NH, PFX), np.float32)
        if nreal > 0:
            dt_mask[:, PFX - nreal:] = 1.0
        m = dict(shared)
        m.update(u_loc=u_loc, u_pfx=u_pfx, dt_mask=dt_mask)
        in_maps.append(m)
    return in_maps


def kernel(**inputs):
    in_maps = _host_prep(inputs)
    nc, _ = build_program()
    res = run_bass_kernel_spmd(nc, in_maps, core_ids=list(range(8)))
    out = np.zeros((2, 8192, DM), np.float32)
    for core in range(8):
        b, q = core // 4, core % 4
        out[b, q * TM:(q + 1) * TM] = res.results[core]["out_loc"]
    return out


# version 17 (final)
# speedup vs baseline: 1.0796x; 1.0796x over previous
import numpy as np
import ml_dtypes
from collections import deque
from contextlib import ExitStack

import concourse.bass as bass
import concourse.mybir as mybir
from concourse.bass_utils import run_bass_kernel_spmd

F32 = mybir.dt.float32
BF16 = mybir.dt.bfloat16
AF = mybir.ActivationFunctionType
ALU = mybir.AluOpType
AXX = mybir.AxisListType.X

DM = 2048
DI = 4096
DXB = 1024
DSTATE = 128
HEADDIM = 64
NH = 64
NG = 32
CONVD = 6144
DPROJ = 10304
INTER = 8192
EPS = 1e-5

TM = 2048
HALO = 128
TL = TM + HALO
PFX = 6144
LC = 128
NCH = TM // LC
LCP = 512
NCHP = PFX // LCP
LCPB = 1024
NCHB = PFX // LCPB
PFXB_IN_PH3 = 999

ENGS = ("sync", "gpsimd", "tensor", "vector", "scalar")
ECH = {"sync": "dS", "gpsimd": "dG", "tensor": "T", "vector": "V", "scalar": "A"}


def _H(n):
    h = np.array([[1.0]], dtype=np.float64)
    while h.shape[0] < n:
        h = np.block([[h, h], [h, -h]])
    return h


class Ch:
    def __init__(self, handle):
        self.h = handle
        self.n = 0

    def inc(self, k=1):
        self.n += k
        return self.n


class Builder:

    NDMA = 8

    def __init__(self, nc, es):
        self.nc = nc
        self.q = {e: [] for e in ENGS}
        self.S = {k: Ch(es.enter_context(nc.semaphore("s_" + k)))
                  for k in ["T", "V", "A", "bar"]}
        self.dring = {
            "sync": [Ch(es.enter_context(nc.semaphore(f"s_dS{i}"))) for i in range(self.NDMA)],
            "gpsimd": [Ch(es.enter_context(nc.semaphore(f"s_dG{i}"))) for i in range(self.NDMA)],
        }
        self.dcnt = {"sync": 0, "gpsimd": 0}
        self.nbar = 0

    def op(self, eng, fn, waits=(), inc=False):
        S = self.S
        ch = S[ECH[eng]] if ECH[eng] in S else None
        auto = eng in ("vector", "scalar")
        pre = ch.n if auto else None
        tgt = ch.inc(1) if (auto or inc) else None
        wl = list(waits)

        def f(e, wl=wl, pre=pre, tgt=tgt, fn=fn, ch=ch):
            if pre:
                e.wait_ge(ch.h, pre)
            for (c, t) in wl:
                if t:
                    e.wait_ge(c.h, t)
            ins = fn(e)
            if tgt is not None:
                ins.then_inc(ch.h, 1)
        self.q[eng].append(f)
        return (ch, tgt) if tgt is not None else None

    def raw(self, eng, fn):
        self.q[eng].append(fn)

    def dma(self, eng, out, in_, waits=()):
        assert eng in ("sync", "gpsimd")
        i = self.dcnt[eng]
        self.dcnt[eng] += 1
        ch = self.dring[eng][i % self.NDMA]
        tgt = ch.inc(16)
        wl = list(waits)

        def f(e, wl=wl, tgt=tgt, ch=ch):
            for (c, t) in wl:
                if t:
                    e.wait_ge(c.h, t)
            if tgt > 16:
                e.wait_ge(ch.h, tgt - 16)
            e.dma_start(out=out, in_=in_).then_inc(ch.h, 16)
        self.q[eng].append(f)
        return (ch, tgt)

    def dma_tokens(self, eng):
        return [(c, c.n) for c in self.dring[eng] if c.n > 0]

    def barrier(self):
        S = self.S
        self.nbar += 1
        n = self.nbar
        marks = {e: S[ECH[e]].n if ECH[e] in S else None for e in ENGS}
        dS_tok = self.dma_tokens("sync")
        dG_tok = self.dma_tokens("gpsimd")

        def mk(ename):
            def f(e):
                if ename == "sync":
                    for (c, t) in dS_tok:
                        e.wait_ge(c.h, t)
                elif ename == "gpsimd":
                    for (c, t) in dG_tok:
                        e.wait_ge(c.h, t)
                elif marks[ename]:
                    e.wait_ge(S[ECH[ename]].h, marks[ename])
                e.nop().then_inc(S["bar"].h, 1)
                e.wait_ge(S["bar"].h, 5 * n)
            return f
        for e in ENGS:
            self.q[e].append(mk(e))

    def emit(self):
        with self.nc.Block() as block:
            qs = self.q

            @block.sync
            def _(e):
                for f in qs["sync"]:
                    f(e)

            @block.gpsimd
            def _(e):
                for f in qs["gpsimd"]:
                    f(e)

            @block.tensor
            def _(e):
                for f in qs["tensor"]:
                    f(e)

            @block.vector
            def _(e):
                for f in qs["vector"]:
                    f(e)

            @block.scalar
            def _(e):
                for f in qs["scalar"]:
                    f(e)


def build_program(dbg=()):
    dbg = set(dbg)
    nc = bass.Bass("TRN2", target_bir_lowering=False, debug=False)

    d_u = nc.dram_tensor("u_loc", [TL, DM], F32, kind="ExternalInput").ap()
    d_upfx = nc.dram_tensor("u_pfx", [PFX, DM], F32, kind="ExternalInput").ap()
    d_dtmask = nc.dram_tensor("dt_mask", [NH, PFX], F32, kind="ExternalInput").ap()
    d_win = nc.dram_tensor("win_blk", [80, 128, 16, 128], F32, kind="ExternalInput").ap()
    d_windt = nc.dram_tensor("win_dt_blk", [128, 16, 64], F32, kind="ExternalInput").ap()
    d_wup = nc.dram_tensor("wup_blk", [64, 128, 16, 128], F32, kind="ExternalInput").ap()
    d_wgate = nc.dram_tensor("wgate_blk", [64, 128, 16, 128], F32, kind="ExternalInput").ap()
    d_wdown = nc.dram_tensor("wdown_blk", [16, 128, 64, 128], F32, kind="ExternalInput").ap()
    d_wout = nc.dram_tensor("wout_blk", [16, 128, 32, 128], F32, kind="ExternalInput").ap()
    d_convw = nc.dram_tensor("convw_pk", [128, 48, 4], F32, kind="ExternalInput").ap()
    d_convb = nc.dram_tensor("convb_pk", [128, 48], F32, kind="ExternalInput").ap()
    d_dtbias = nc.dram_tensor("dtbias", [64, 1], F32, kind="ExternalInput").ap()
    d_alog = nc.dram_tensor("alog", [64, 1], F32, kind="ExternalInput").ap()
    d_Drow = nc.dram_tensor("Drow", [1, 64], F32, kind="ExternalInput").ap()
    d_nw = nc.dram_tensor("nw_pk", [128, 32], F32, kind="ExternalInput").ap()
    d_suug = nc.dram_tensor("suug_pk", [128, 16], F32, kind="ExternalInput").ap()
    d_sudp = nc.dram_tensor("sudp_pk", [128, 64], F32, kind="ExternalInput").ap()
    d_wscup = nc.dram_tensor("wscup_pk", [128, 64], F32, kind="ExternalInput").ap()
    d_wscgate = nc.dram_tensor("wscgate_pk", [128, 64], F32, kind="ExternalInput").ap()
    d_wscdp = nc.dram_tensor("wscdp_pk", [128, 16], F32, kind="ExternalInput").ap()
    d_out = nc.dram_tensor("out_loc", [TM, DM], F32, kind="ExternalOutput").ap()

    ident_c = nc.inline_tensor(np.eye(128, dtype=ml_dtypes.bfloat16), "c_ident").ap()
    identf_c = nc.inline_tensor(np.eye(128, dtype=np.float32), "c_identf").ap()
    h128_c = nc.inline_tensor(np.ascontiguousarray(_H(128).astype(ml_dtypes.bfloat16)), "c_h128").ap()
    m1_c = nc.inline_tensor(np.ascontiguousarray(
        np.kron(np.eye(2), _H(64)).astype(ml_dtypes.bfloat16)), "c_m1").ap()
    ones1_c = nc.inline_tensor(np.ones((1, 128), dtype=ml_dtypes.bfloat16), "c_ones1").ap()
    slneg_np = np.where(np.arange(128)[None, :] < np.arange(128)[:, None], -1e30, 0.0)
    slneg_c = nc.inline_tensor(np.ascontiguousarray(
        slneg_np.astype(ml_dtypes.bfloat16)), "c_slneg").ap()
    og_np = np.zeros((32, 128, 32), dtype=ml_dtypes.bfloat16)
    for g in range(32):
        og_np[g, :, g] = 1
    og_c = nc.inline_tensor(np.ascontiguousarray(og_np.reshape(32 * 128, 32)), "c_og").ap()
    eb_np = np.zeros((32, 32, 128), dtype=ml_dtypes.bfloat16)
    for g in range(32):
        eb_np[g, g, :] = 1
    eb_c = nc.inline_tensor(np.ascontiguousarray(eb_np.reshape(32 * 32, 128)), "c_eb").ap()

    z_scr = nc.dram_tensor("z_scr", [DI, TL], BF16, kind="Internal").ap()
    xbc_scr = nc.dram_tensor("xbc_scr", [CONVD, TM], BF16, kind="Internal").ap()
    h1_scr = nc.dram_tensor("h1_scr", [INTER, TM], BF16, kind="Internal").ap()
    y_scr = nc.dram_tensor("y_scr", [DI, TM], BF16, kind="Internal").ap()
    g_scr = nc.dram_tensor("g_scr", [DM, TM], BF16, kind="Internal").ap()
    sg_scr = nc.dram_tensor("sg_scr", [DM, TM], BF16, kind="Internal").ap()
    csm_scr = nc.dram_tensor("csm_scr", [NCH * 4, 2048], F32, kind="Internal").ap()
    dcc_scr = nc.dram_tensor("dcc_scr", [NH, 1], BF16, kind="Internal").ap()
    ecs_scr = nc.dram_tensor("ecs_scr", [NCH * 4, 2048], BF16, kind="Internal").ap()
    xbcpre_scr = nc.dram_tensor("xbcpre_scr", [2048, PFX], BF16, kind="Internal").ap()
    wpfx_scr = nc.dram_tensor("wpfx_scr", [16, 128, 16, 128], BF16, kind="Internal").ap()
    wbf_scr = nc.dram_tensor("wbf_scr", [64, PFX], BF16, kind="Internal").ap()

    dbg_outs = {}

    def dbg_tensor(name, shape, dtype=F32):
        t = nc.dram_tensor("dbg_" + name, shape, dtype, kind="ExternalOutput").ap()
        dbg_outs[name] = t
        return t

    es = ExitStack()
    with es:
        B = Builder(nc, es)
        S = B.S

        def sb(name, shape, dtype):
            return es.enter_context(nc.sbuf_tensor(name, shape, dtype))

        IDENT = sb("IDENT", [128, 128], BF16)
        IDF = sb("IDF", [128, 128], F32)
        H128 = sb("H128", [128, 128], BF16)
        M1C = sb("M1C", [128, 128], BF16)
        SLNEG = sb("SLNEG", [128, 128], BF16)
        ONES1 = sb("ONES1", [1, 128], BF16)
        CONVW = sb("CONVW", [128, 48, 4], F32)
        CONVB = sb("CONVB", [128, 48], F32)
        DTBIAS = sb("DTBIAS", [64, 1], F32)
        AH = sb("AH", [64, 1], F32)
        DROW = sb("DROW", [1, 64], BF16)
        NW = sb("NW", [128, 32], F32)
        SUUG = sb("SUUG", [128, 16], F32)
        SUDP = sb("SUDP", [128, 64], F32)
        WSCUP = sb("WSCUP", [128, 64], F32)
        WSCGATE = sb("WSCGATE", [128, 64], F32)
        WSCDP = sb("WSCDP", [128, 16], F32)
        EPSC = sb("EPSC", [128, 1], F32)
        ONEC = sb("ONEC", [128, 1], F32)
        DCBB = sb("DCBB", [128, 64], BF16)

        es_mid = ExitStack()

        def sbm(name, shape, dtype):
            return es_mid.enter_context(nc.sbuf_tensor(name, shape, dtype))

        SST = sbm("SST", [128, NH * HEADDIM], F32)

        PS = [es.enter_context(nc.psum_tensor(f"ps{i}", [128, 512], F32)) for i in range(8)]

        def ptb(i):
            return PS[i][:].bitcast(BF16)

        op, dma, barrier = B.op, B.dma, B.barrier

        for dst, src in [
            (IDENT[:], ident_c), (IDF[:], identf_c), (H128[:], h128_c), (M1C[:], m1_c),
            (ONES1[:], ones1_c), (SLNEG[:], slneg_c),
            (CONVW[:], d_convw), (CONVB[:], d_convb), (DTBIAS[:], d_dtbias),
            (AH[:], d_alog), (NW[:], d_nw),
            (SUUG[:], d_suug), (SUDP[:], d_sudp), (WSCUP[:], d_wscup),
            (WSCGATE[:], d_wscgate), (WSCDP[:], d_wscdp),
        ]:
            dma("sync", dst, src)
        wS = B.dma_tokens("sync")
        wG0 = dma("gpsimd", DROW[:], d_Drow)

        op("vector", lambda v: v.memset(EPSC[:], EPS))
        op("vector", lambda v: v.memset(ONEC[:], 1.0))
        aExp = op("scalar", lambda a: a.activation(AH[:], AH[:], AF.Exp),
                  waits=wS)
        op("vector", lambda v: v.tensor_scalar_mul(AH[:], AH[:], -1.0),
           waits=[aExp])
        barrier()

        def make_ln(tag, sba=None):
            sba = sba or sb
            U = [sba(f"lnU{p}_{tag}", [128, DM], F32) for p in range(3)]
            XT = [sba(f"lnX{p}_{tag}", [128, DM], BF16) for p in range(2)]
            SQ = sba(f"lnSQ_{tag}", [128, DM], BF16)
            ST = [sba(f"lnST{p}_{tag}", [128, 8], F32) for p in range(3)]
            st = dict(f=0, s=0, t=0)
            descs = {}
            tok = {}
            tG = {}
            vEv = {}

            def feed(src_dram, src_tok, dst_cm, dst_tile, dst_guard=None,
                     done=None):
                i = st["f"]
                st["f"] += 1
                descs[i] = (dst_cm, dst_tile, dst_guard, done)
                p = i % 3
                wl = []
                if (i - 3, "vN") in tok:
                    wl = [tok[(i - 3, "vN")], tok[(i - 3, "aSq")]]
                tok[(i, "dma")] = dma("sync", U[p][:],
                                      src_dram[src_tok:src_tok + 128, :],
                                      waits=wl)

            def stats():
                i = st["s"]
                st["s"] += 1
                p = i % 3
                px = i % 2
                tok[(i, "vSum")] = op("scalar", lambda a: a.activation(
                    SQ[:], U[p][:], AF.Copy, accum_out=ST[p][:, 0:1]),
                    waits=[tok[(i, "dma")]] +
                    ([tok[(i - 3, "vN")]] if (i - 3, "vN") in tok else []))
                tok[(i, "aSq")] = op("scalar", lambda a: a.activation(
                    SQ[:], U[p][:], AF.Square, accum_out=ST[p][:, 1:2]))
                tok[(i, "aM")] = op("scalar", lambda a: a.mul(
                    ST[p][:, 2:3], ST[p][:, 0:1], 1.0 / DM))
                op("vector", lambda v: v.tensor_tensor(
                    ST[p][:, 3:4], ST[p][:, 2:3], ST[p][:, 2:3], ALU.mult),
                    waits=[tok[(i, "aM")]])
                op("vector", lambda v: v.tensor_scalar(
                    ST[p][:, 4:5], ST[p][:, 1:2], 1.0 / DM, None, ALU.mult))
                tok[(i, "vVar")] = op("vector", lambda v: v.tensor_tensor(
                    ST[p][:, 5:6], ST[p][:, 4:5], ST[p][:, 3:4], ALU.subtract))
                op("scalar", lambda a: a.activation(
                    ST[p][:, 6:7], ST[p][:, 5:6], AF.Ln, bias=EPSC[:]),
                    waits=[tok[(i, "vVar")]])
                tok[(i, "aR")] = op("scalar", lambda a: a.activation(
                    ST[p][:, 7:8], ST[p][:, 6:7], AF.Exp, scale=-0.5))
                tok[(i, "vN")] = op("vector", lambda v: v.tensor_scalar(
                    XT[px][:], U[p][:], ST[p][:, 2:3], ST[p][:, 7:8],
                    ALU.subtract, ALU.mult),
                    waits=[tok[(i, "aR")]] +
                    ([tG[i - 2]] if (i - 2) in tG else []))

            def tp():
                i = st["t"]
                st["t"] += 1
                dst_cm, dt0, dst_guard, done = descs.pop(i)
                px = i % 2
                for jj in range(4):
                    bank = 6 + jj % 2
                    key = (i, jj - 2) if jj >= 2 else (i - 1, jj + 2)
                    wl = [tok[(i, "vN")]]
                    if key in vEv:
                        wl.append(vEv[key])
                    if dst_guard is not None:
                        wl.append(dst_guard)

                    def tfn(t, jj=jj, px=px, bank=bank):
                        ins = None
                        for j in range(4):
                            ins = t.transpose(
                                ptb(bank)[:, j * 128:(j + 1) * 128],
                                XT[px][:, (jj * 4 + j) * 128:(jj * 4 + j + 1) * 128],
                                IDENT[:])
                        return ins
                    tG[i] = op("tensor", tfn, waits=wl, inc=True)
                    vEv[(i, jj)] = op("vector", lambda v, jj=jj, dt0=dt0, bank=bank: v.tensor_copy(
                        dst_cm[:, jj * 4:jj * 4 + 4, dt0 * 128:(dt0 + 1) * 128],
                        ptb(bank).rearrange("p (j q) -> p j q", j=8)[:, 0:4, :]),
                        waits=[tG[i]])
                last = vEv[(i, 3)]
                if done is not None:
                    done(last)
                return last
            return st, feed, stats, tp

        def emit_softplus(dst, srcv, tmp1, tmp2, mask_view=None, waits=()):
            t1 = op("vector", lambda v: v.tensor_scalar(
                tmp1, srcv, DTBIAS[:], None, ALU.add), waits=list(waits))
            a1 = op("scalar", lambda a: a.activation(tmp2, tmp1, AF.Abs), waits=[t1])
            op("scalar", lambda a: a.activation(tmp2, tmp2, AF.Exp, scale=-1.0))
            t2 = op("scalar", lambda a: a.activation(tmp2, tmp2, AF.Ln, bias=ONEC[0:64, :]))
            op("vector", lambda v: v.tensor_scalar(tmp1, tmp1, 0.0, None, ALU.max),
               waits=[a1])
            if mask_view is None:
                aw = op("vector", lambda v: v.tensor_tensor(dst, tmp1, tmp2, ALU.add),
                        waits=[t2])
            else:
                op("vector", lambda v: v.tensor_tensor(tmp1, tmp1, tmp2, ALU.add),
                   waits=[t2])
                aw = op("vector", lambda v: v.tensor_tensor(dst, tmp1, mask_view, ALU.mult))
            return aw

        DTM = sbm("DTM", [64, TM], F32)
        es_a = ExitStack()

        def sba(name, shape, dtype):
            return es_a.enter_context(nc.sbuf_tensor(name, shape, dtype))

        XN = sba("XN", [128, 16, TL], BF16)
        X1 = XN[:, :, HALO:HALO + TM].rearrange("p a b -> p a b")
        WPDT = sba("WPDT", [128, 16, 64], BF16)
        DCBA = sba("DCBA", [128, NCHB, 64], F32)

        with ExitStack() as esp:
            def sbp(name, shape, dtype):
                return esp.enter_context(nc.sbuf_tensor(name, shape, dtype))

            lnst, ln_feed, ln_stats, ln_tp = make_ln("c", sbp)
            XNP = [sbp(f"XNP{i}", [128, 16, LCP], BF16) for i in range(2)]
            WPF = [sbp(f"WPF{i}", [128, 16, 128], BF16) for i in range(3)]
            ZEVP = [sbp(f"ZEVP{i}", [128, 512], BF16) for i in range(2)]
            DTP = sbp("DTP", [64, LCPB], F32)
            DAPC = sbp("DAPC", [64, LCPB], F32)
            CSP = sbp("CSP", [64, LCPB], F32)
            DSW = sbp("DSW", [64, LCPB], F32)
            DTMK = sbp("DTMK", [64, LCPB], BF16)
            ZER64 = sbp("ZER64", [64, LCPB], BF16)
            WBFP = sbp("WBFP", [64, LCPB], BF16)
            DCC = sbp("DCC", [64, 1], BF16)

            wPD = dma("gpsimd", WPDT[:], d_windt)
            op("vector", lambda v: v.memset(SST[:], 0.0))
            op("vector", lambda v: v.memset(ZER64[:], 0.0))

            bank_tok = {i: None for i in range(8)}
            abk = [0]
            wpf_free = {0: [], 1: [], 2: []}
            wb_tok = {}
            zevp_free = {0: None, 1: None}
            ln_guard = {0: None, 1: None}
            dtp_guard = [None]
            dtp_evac = {}
            aDt_prev = [None]
            dcc_prev = [None]
            dcbb_prev = [None]
            wbf_free = [None]
            lnv = {}

            tiles_pri = deque()
            tiles_main = deque()
            main_pos = [0]

            def push_lnc(c):
                pc = c % 2
                nt = LCP // 128
                for tix in range(nt):
                    def f(c=c, pc=pc, tix=tix, last=(tix == nt - 1)):
                        def done(tokl, c=c):
                            lnv[c] = tokl
                        ln_feed(d_upfx, c * LCP + tix * 128, XNP[pc], tix,
                                dst_guard=ln_guard[pc],
                                done=done if last else None)
                    tiles_pri.append(f)

            def push_main():
                def f():
                    ln_feed(d_u, main_pos[0] * 128, XN, main_pos[0])
                    main_pos[0] += 1
                tiles_main.append(f)

            def feed_one():
                if tiles_pri:
                    tiles_pri.popleft()()
                    return True
                if tiles_main:
                    tiles_main.popleft()()
                    return True
                return False

            def drain_one():
                if lnst["t"] < lnst["s"] - 1:
                    ln_tp()
                    return True
                if lnst["s"] < lnst["f"]:
                    ln_stats()
                    return True
                if feed_one():
                    return True
                if lnst["t"] < lnst["s"]:
                    ln_tp()
                    return True
                return False

            def pump():
                if lnst["t"] < lnst["s"] - 1:
                    ln_tp()
                if lnst["s"] < lnst["f"]:
                    ln_stats()
                feed_one()

            def emit_mms(c):
                pc = c % 2
                vLN = lnv[c]
                last_mm = None
                for idx, mi in enumerate([16] + list(range(16))):
                    if mi < 16:
                        wp = mi % 3
                        if c == 0:
                            wW = [dma("gpsimd", WPF[wp][:], d_win[32 + mi],
                                      waits=wpf_free[wp])]
                            wb_tok[mi] = dma("sync", wpfx_scr[mi], WPF[wp][:],
                                             waits=wW)
                        else:
                            wl = list(wpf_free[wp])
                            if c == 1:
                                wl.append(wb_tok[mi])
                            wW = [dma("sync", WPF[wp][:, 0:8, :],
                                      wpfx_scr[mi][:, 0:8, :], waits=wl),
                                  dma("sync", WPF[wp][:, 8:16, :],
                                      wpfx_scr[mi][:, 8:16, :], waits=wl)]
                    else:
                        wW = [wPD]
                    bank = abk[0] % 4
                    abk[0] += 1
                    wl = wW + [vLN]
                    if bank_tok[bank] is not None:
                        wl.append(bank_tok[bank])

                    def mmf(t, mi=mi, bank=bank, pc=pc):
                        ins = None
                        for k in range(16):
                            lhs = WPF[mi % 3][:, k, :] if mi < 16 else WPDT[:, k, :]
                            mrows = 128 if mi < 16 else 64
                            ins = t.matmul(PS[bank][0:mrows, :], lhs,
                                           XNP[pc][:, k, :],
                                           start=(k == 0), stop=(k == 15))
                        return ins
                    tMM = op("tensor", mmf, waits=wl, inc=True)
                    if mi < 16:
                        wpf_free[mi % 3] = [tMM]
                        if c == 0:
                            wpf_free[mi % 3].append(wb_tok[mi])
                        zp = mi % 2
                        wl2 = [tMM]
                        if zevp_free[zp] is not None:
                            wl2.append(zevp_free[zp])
                        aE = op("scalar", lambda a, zp=zp, bank=bank: a.copy(
                            ZEVP[zp][:], PS[bank][:, :]), waits=wl2)
                        bank_tok[bank] = aE
                        zevp_free[zp] = dma(
                            "sync" if c == 0 else "gpsimd",
                            xbcpre_scr[mi * 128:(mi + 1) * 128,
                                       c * LCP:(c + 1) * LCP],
                            ZEVP[zp][:], waits=[aE])
                    else:
                        ho = (c % 2) * 512
                        wl2 = [tMM]
                        if dtp_guard[0] is not None:
                            wl2.append(dtp_guard[0])
                        aD = op("scalar", lambda a, ho=ho, bank=bank: a.copy(
                            DTP[:, ho:ho + 512], PS[bank][0:64, :]), waits=wl2)
                        bank_tok[bank] = aD
                        dtp_evac[c] = aD
                    last_mm = tMM
                    if idx in (2, 5, 8, 11, 14, 16):
                        pump()
                ln_guard[pc] = last_mm

            def emit_dt(cp):
                wMK = dma("gpsimd", DTMK[:],
                          d_dtmask[:, cp * LCPB:(cp + 1) * LCPB],
                          waits=[aDt_prev[0]] if aDt_prev[0] else [])
                aDt = emit_softplus(
                    DTP[:], DTP[:], DAPC[:], CSP[:], mask_view=DTMK[:],
                    waits=[dtp_evac[2 * cp], dtp_evac[2 * cp + 1], wMK])
                aDt_prev[0] = aDt
                op("vector", lambda v: v.tensor_scalar(
                    DAPC[:], DTP[:], AH[:], None, ALU.mult))
                vCS = op("vector", lambda v: v.tensor_tensor_scan(
                    CSP[:], DAPC[:], ZER64[:], 0.0, ALU.add, ALU.add))
                vD1 = op("vector", lambda v: v.tensor_tensor(
                    DSW[:], CSP[:, LCPB - 1:LCPB].broadcast_to([64, LCPB]),
                    CSP[:], ALU.subtract))
                aD2 = op("scalar", lambda a: a.activation(DSW[:], DSW[:], AF.Exp),
                         waits=[vD1])
                vD3 = op("vector", lambda v: v.tensor_tensor(
                    DSW[:], DSW[:], DTP[:], ALU.mult), waits=[aD2])
                dtp_guard[0] = vD3
                vWBF = op("vector", lambda v: v.tensor_copy(WBFP[:], DSW[:]),
                          waits=[wbf_free[0]] if wbf_free[0] else [])
                wbf_free[0] = dma("gpsimd", wbf_scr[:, cp * LCPB:(cp + 1) * LCPB],
                                  WBFP[:], waits=[vWBF])
                aDC = op("scalar", lambda a: a.activation(
                    DCC[:], CSP[:, LCPB - 1:LCPB], AF.Exp),
                    waits=[vCS] + ([dcc_prev[0]] if dcc_prev[0] else []))
                wDC0 = dma("gpsimd", dcc_scr, DCC[:], waits=[aDC])
                dcc_prev[0] = wDC0
                wl = [wDC0]
                if dcbb_prev[0] is not None:
                    wl.append(dcbb_prev[0])
                wDB = dma("gpsimd", DCBB[:],
                          dcc_scr.rearrange("a b -> b a").broadcast_to([128, 64]),
                          waits=wl)
                dcbb_prev[0] = op("vector", lambda v, cp=cp: v.tensor_copy(
                    DCBA[:, cp, :], DCBB[:]), waits=[wDB])

            push_lnc(0)
            push_lnc(1)
            for _ in range(TL // 128):
                push_main()
            for _ in range(3):
                feed_one()
            for c in range(NCHP):
                while lnv.get(c) is None:
                    if not drain_one():
                        break
                emit_mms(c)
                if c % 2 == 1:
                    emit_dt(c // 2)
                if c + 2 < NCHP:
                    push_lnc(c + 2)
            while (tiles_pri or tiles_main or lnst["s"] < lnst["f"]
                   or lnst["t"] < lnst["s"]):
                if not drain_one():
                    break
            barrier()

        if "xn" in dbg:
            t = dbg_tensor("xn", [128, 16 * TL], BF16)
            dma("sync", t.rearrange("p (a b) -> p a b", a=16), XN[:])
            barrier()
        if "pfxa" in dbg:
            t = dbg_tensor("xbp", [2048, PFX], BF16)
            dma("sync", t, xbcpre_scr)
            t2 = dbg_tensor("wbfd", [64, PFX], BF16)
            dma("sync", t2, wbf_scr)
            t3 = dbg_tensor("dcba", [128, NCHB * 64])
            dma("sync", t3.rearrange("p (a b) -> p a b", a=NCHB), DCBA[:])
            barrier()

        es_b2 = ExitStack()

        def sbb2(name, shape, dtype):
            return es_b2.enter_context(nc.sbuf_tensor(name, shape, dtype))

        XBPRE = sbb2("XBPRE", [128, 16, LCPB + 4], BF16)
        ACCP = [sbb2(f"ACCP{i}", [128, LCPB], F32) for i in range(2)]
        XTC8 = sbb2("XTC8", [128, LCPB // 128, 2, 128], BF16)
        XWG = sbb2("XWG", [128, LCPB // 128, 512], BF16)
        WTQ = sbb2("WTQ", [128, LCPB // 128, 64], BF16)
        WBF = sbb2("WBF", [64, LCPB], BF16)
        pbank = {i: None for i in range(8)}

        def pfxb_gen():
            vZB = op("vector", lambda v: v.memset(XBPRE[:, :, 0:4], 0.0))
            ld_guard = [vZB]
            wbf_guard = [None]
            wtq_tok = {i: None for i in range(LCPB // 128)}
            st_upd = {gb: None for gb in range(8)}
            accp_g = {0: None, 1: None}
            for cb in range(NCHB):
                ldt = []
                wl0 = [t for t in ld_guard if t is not None]
                for mi in range(16):
                    if cb == 0:
                        dst = XBPRE[:, mi, 4:4 + LCPB]
                        src = xbcpre_scr[mi * 128:(mi + 1) * 128, 0:LCPB]
                    else:
                        dst = XBPRE[:, mi, 1:4 + LCPB]
                        src = xbcpre_scr[mi * 128:(mi + 1) * 128,
                                         cb * LCPB - 3:(cb + 1) * LCPB]
                    ldt.append(dma("sync", dst, src,
                                   waits=wl0 if mi == 0 else []))
                wWBF = dma("sync", WBF[:],
                           wbf_scr[:, cb * LCPB:(cb + 1) * LCPB],
                           waits=[wbf_guard[0]] if wbf_guard[0] else [])
                yield
                vConvDone = []
                vCl = None
                for mi in range(16):
                    ap = mi % 2
                    wl = [ldt[mi]]
                    if accp_g[ap] is not None:
                        wl.append(accp_g[ap])
                    if cb == 0:
                        wl.append(vZB)
                    op("vector", lambda v, mi=mi, ap=ap: v.tensor_scalar(
                        ACCP[ap][:], XBPRE[:, mi, 1:1 + LCPB], CONVW[:, mi, 0:1],
                        None, ALU.mult), waits=wl)
                    for k in range(1, 4):
                        vCl = op("vector", lambda v, mi=mi, k=k, ap=ap: v.scalar_tensor_tensor(
                            ACCP[ap][:], XBPRE[:, mi, 1 + k:1 + k + LCPB],
                            CONVW[:, mi, k:k + 1], ACCP[ap][:], ALU.mult, ALU.add))
                    vConvDone.append(op("scalar", lambda a, mi=mi, ap=ap: a.activation(
                        XBPRE[:, mi, 4:4 + LCPB], ACCP[ap][:], AF.Silu,
                        bias=CONVB[:, mi:mi + 1]), waits=[vCl]))
                    accp_g[ap] = vConvDone[-1]
                    if mi % 4 == 3:
                        yield
                yield
                for blk in range(LCPB // 128):
                    tb = 6 + blk % 2
                    wl2 = [wWBF]
                    if wtq_tok[blk] is not None:
                        wl2.append(wtq_tok[blk])
                    if pbank[tb] is not None:
                        wl2.append(pbank[tb])
                    tW = op("tensor", lambda t, blk=blk, tb=tb: t.transpose(
                        ptb(tb)[:, 256:320],
                        WBF[:, blk * 128:(blk + 1) * 128],
                        IDENT[0:64, 0:64]), waits=wl2, inc=True)
                    wtq_tok[blk] = op("vector", lambda v, blk=blk, tb=tb: v.tensor_copy(
                        WTQ[:, blk, :], ptb(tb)[:, 256:320]), waits=[tW])
                    pbank[tb] = wtq_tok[blk]
                yield
                last_st = None
                for gb in range(8):
                    evs = []
                    for blk in range(LCPB // 128):
                        tb = 6 + blk % 2
                        wl = [vConvDone[gb], vConvDone[8 + gb]]
                        if pbank[tb] is not None:
                            wl.append(pbank[tb])

                        def tfn(t, gb=gb, blk=blk, tb=tb):
                            t.transpose(ptb(tb)[:, 0:128],
                                        XBPRE[:, gb, 4 + blk * 128:4 + (blk + 1) * 128],
                                        IDENT[:])
                            return t.transpose(
                                ptb(tb)[:, 128:256],
                                XBPRE[:, 8 + gb, 4 + blk * 128:4 + (blk + 1) * 128],
                                IDENT[:])
                        tT = op("tensor", tfn, waits=wl, inc=True)
                        pbank[tb] = op("vector", lambda v, blk=blk, tb=tb: v.tensor_copy(
                            XTC8[:, blk, :, :],
                            ptb(tb).rearrange("p (a q) -> p a q", a=8)[:, 0:2, :]),
                            waits=[tT])
                        evs.append(pbank[tb])
                    xw_toks = []
                    for blk in range(LCPB // 128):
                        xw_toks.append(op("vector", lambda v, gb=gb, blk=blk: v.tensor_tensor(
                            XWG[:, blk, :].rearrange("p (a b q) -> p a b q", a=4, b=2),
                            XTC8[:, blk, 0, :].rearrange("p (a q) -> p a q", a=2)
                            .unsqueeze(1).broadcast_to([128, 4, 2, 64]),
                            WTQ[:, blk, gb * 8:gb * 8 + 8]
                            .rearrange("p (a b) -> p a b", a=4)
                            .unsqueeze(3).broadcast_to([128, 4, 2, 64]),
                            ALU.mult), waits=[wtq_tok[blk], evs[blk]]))
                    bank = 4 + gb % 2
                    wl4 = [xw_toks[-1]]
                    if pbank[bank] is not None:
                        wl4.append(pbank[bank])

                    def mms(t, gb=gb, bank=bank):
                        ins = None
                        nbl = LCPB // 128
                        for blk in range(nbl):
                            ins = t.matmul(PS[bank][:, :], XTC8[:, blk, 1, :],
                                           XWG[:, blk, :],
                                           start=(blk == 0), stop=(blk == nbl - 1))
                        return ins
                    tSt = op("tensor", mms, waits=wl4, inc=True)
                    last_st = tSt
                    wl5 = [tSt]
                    if st_upd[gb] is not None:
                        wl5.append(st_upd[gb])
                    op("vector", lambda v, gb=gb, cb=cb: v.tensor_tensor(
                        SST[:, gb * 512:(gb + 1) * 512].rearrange("p (a q) -> p a q", a=8),
                        SST[:, gb * 512:(gb + 1) * 512].rearrange("p (a q) -> p a q", a=8),
                        DCBA[:, cb, gb * 8:gb * 8 + 8].unsqueeze(2).broadcast_to([128, 8, 64]),
                        ALU.mult), waits=wl5)
                    st_upd[gb] = op("vector", lambda v, gb=gb, bank=bank: v.tensor_tensor(
                        SST[:, gb * 512:(gb + 1) * 512],
                        SST[:, gb * 512:(gb + 1) * 512],
                        PS[bank][:, :], ALU.add))
                    pbank[bank] = st_upd[gb]
                    if gb % 2 == 1:
                        yield
                yield
                ld_guard[:] = [last_st, vCl]
                wbf_guard[0] = wtq_tok[LCPB // 128 - 1]

        gen = pfxb_gen()
        bpump = [0]

        with ExitStack() as es3:
            def sb3(name, shape, dtype):
                return es3.enter_context(nc.sbuf_tensor(name, shape, dtype))

            WB = [sb3(f"WB{i}", [128, 16, 128], BF16) for i in range(2)]
            ZEV = [sb3(f"ZEV{i}", [128, 512], BF16) for i in range(2)]
            XBR = [sb3(f"XBR{i}", [128, TL], BF16) for i in range(2)]
            ACC3 = sb3("ACC3", [128, TM], F32)
            CVO = [sb3(f"CVO{i}", [128, TM], BF16) for i in range(2)]
            DTRW = sb3("DTRW", [64, TL], F32)

            bank_tok = {i: None for i in range(8)}

            strips5 = [(0, 512), (512, 512), (1024, 512), (1536, 512), (2048, 128)]
            w_free = {0: None, 1: None}
            zev_free = {0: None, 1: None}
            cvo_free = {0: None, 1: None}
            xbr_free = {0: None, 1: None}
            acc3_prev = [None]
            dt_evs = []
            for m in range(81):
                if m < 80:
                    wl = [w_free[m % 2]] if w_free[m % 2] else []
                    wW = dma("gpsimd", WB[m % 2][:], d_win[m], waits=wl)
                else:
                    wW = wPD
                mm_toks = []
                ev_toks = []
                for si, (o, w) in enumerate(strips5):
                    bank = (m * 5 + si) % 4
                    wl = [wW]
                    if bank_tok[bank] is not None:
                        wl.append(bank_tok[bank])
                    if m >= 32 and m < 80 and si == 0 and xbr_free[m % 2] is not None:
                        wl.append(xbr_free[m % 2])

                    def mmf(t, m=m, o=o, w=w, bank=bank):
                        ins = None
                        for k in range(16):
                            lhs = WB[m % 2][:, k, :] if m < 80 else WPDT[:, k, :]
                            mrows = 128 if m < 80 else 64
                            ins = t.matmul(PS[bank][0:mrows, 0:w], lhs,
                                           XN[:, k, o:o + w],
                                           start=(k == 0), stop=(k == 15))
                        return ins
                    tMM = op("tensor", mmf, waits=wl, inc=True)
                    mm_toks.append(tMM)
                    if m < 32:
                        zp = si % 2
                        wl2 = [tMM]
                        if zev_free[zp] is not None:
                            wl2.append(zev_free[zp])
                        aZ = op("scalar", lambda a, o=o, w=w, bank=bank, zp=zp: a.copy(
                            ZEV[zp][:, 0:w], PS[bank][:, 0:w]), waits=wl2)
                        bank_tok[bank] = aZ
                        zev_free[zp] = dma(
                            "sync", z_scr[m * 128:(m + 1) * 128, o:o + w],
                            ZEV[zp][:, 0:w], waits=[aZ])
                    elif m < 80:
                        aX = op("scalar", lambda a, m=m, o=o, w=w, bank=bank: a.copy(
                            XBR[m % 2][:, o:o + w], PS[bank][:, 0:w]), waits=[tMM])
                        bank_tok[bank] = aX
                        ev_toks.append(aX)
                    else:
                        aD = op("scalar", lambda a, o=o, w=w, bank=bank: a.copy(
                            DTRW[:, o:o + w], PS[bank][0:64, 0:w]), waits=[tMM])
                        bank_tok[bank] = aD
                        dt_evs.append(aD)
                w_free[m % 2] = mm_toks[-1]
                if 32 <= m < 80:
                    mi = m - 32
                    wl = list(ev_toks)
                    if cvo_free[m % 2] is not None:
                        wl.append(cvo_free[m % 2])
                    if acc3_prev[0] is not None:
                        wl.append(acc3_prev[0])
                    op("vector", lambda v, m=m, mi=mi: v.tensor_scalar(
                        ACC3[:], XBR[m % 2][:, HALO - 3:HALO - 3 + TM],
                        CONVW[:, mi, 0:1], None, ALU.mult), waits=wl)
                    vC4 = None
                    for k in range(1, 4):
                        vC4 = op("vector", lambda v, m=m, mi=mi, k=k: v.scalar_tensor_tensor(
                            ACC3[:], XBR[m % 2][:, HALO - 3 + k:HALO - 3 + k + TM],
                            CONVW[:, mi, k:k + 1], ACC3[:], ALU.mult, ALU.add))
                    xbr_free[m % 2] = vC4
                    aCV = op("scalar", lambda a, m=m, mi=mi: a.activation(
                        CVO[m % 2][:, :], ACC3[:], AF.Silu, bias=CONVB[:, mi:mi + 1]),
                        waits=[vC4])
                    acc3_prev[0] = aCV
                    cvo_free[m % 2] = dma(
                        "sync", xbc_scr[mi * 128:(mi + 1) * 128, :], CVO[m % 2][:],
                        waits=[aCV])
                if bpump[0] < PFXB_IN_PH3:
                    next(gen, None)
                    bpump[0] += 1
            if PFXB_IN_PH3 > 100:
                for _ in gen:
                    pass
            op("vector", lambda v: v.tensor_copy(ACC3[0:64, :], DTRW[:, HALO:]),
               waits=dt_evs + ([acc3_prev[0]] if acc3_prev[0] else []))
            emit_softplus(DTM[:], ACC3[0:64, :], ACC3[0:64, :], DTRW[:, 0:TM])
            barrier()

        if "sin" in dbg:
            t = dbg_tensor("sin", [128, NH * HEADDIM])
            dma("sync", t, SST[:])
            barrier()

        if "xbc" in dbg:
            t = dbg_tensor("xbc", [CONVD, TM], BF16)
            dma("sync", t, xbc_scr)
            t2 = dbg_tensor("zd", [DI, TL], BF16)
            dma("sync", t2, z_scr)
            t3 = dbg_tensor("dtm", [64, TM])
            dma("sync", t3, DTM[:])
            barrier()

        with ExitStack() as es4:
            def sb4(name, shape, dtype):
                return es4.enter_context(nc.sbuf_tensor(name, shape, dtype))

            HSG = [sb4(f"HSG{i}", [128, 128], BF16) for i in range(2)]
            BTMP = sb4("BTMP", [128, TM], BF16)
            bank_tok = {i: None for i in range(8)}
            hs_free = {0: None, 1: None}
            SC2048 = 1.0 / float(np.sqrt(2048.0))
            for g in range(16):
                wl = [hs_free[g % 2]] if hs_free[g % 2] else []
                vH = op("vector", lambda v, g=g: v.tensor_scalar(
                    HSG[g % 2][:], H128[:], SUUG[:, g:g + 1], None, ALU.mult),
                    waits=wl)
                for s4 in range(4):
                    bank = (g * 4 + s4) % 4
                    wl = [vH]
                    if bank_tok[bank] is not None:
                        wl.append(bank_tok[bank])
                    tMM = op("tensor", lambda t, g=g, s4=s4, bank=bank: t.matmul(
                        PS[bank][:, :], HSG[g % 2][:],
                        XN[:, g, HALO + s4 * 512:HALO + (s4 + 1) * 512],
                        start=True, stop=True), waits=wl, inc=True)
                    if s4 == 3:
                        hs_free[g % 2] = tMM
                    bank_tok[bank] = op("scalar", lambda a, g=g, s4=s4, bank=bank: a.activation(
                        X1[:, g, s4 * 512:(s4 + 1) * 512], PS[bank][:, :],
                        AF.Copy, scale=SC2048), waits=[tMM])
            aH128 = (S["A"], S["A"].n)
            prev = aH128
            for st4 in range(4):
                d = 1 << st4
                for g0 in range(16):
                    if g0 & d:
                        continue
                    g1 = g0 + d
                    op("vector", lambda v, g0=g0: v.tensor_copy(
                        BTMP[:], X1[:, g0, :]), waits=[prev])
                    op("vector", lambda v, g0=g0, g1=g1: v.tensor_tensor(
                        X1[:, g0, :], BTMP[:], X1[:, g1, :], ALU.add))
                    op("vector", lambda v, g0=g0, g1=g1: v.tensor_tensor(
                        X1[:, g1, :], BTMP[:], X1[:, g1, :], ALU.subtract))
                prev = (S["V"], S["V"].n)
            barrier()

        if "x1" in dbg:
            t = dbg_tensor("x1", [128, 16 * TM], BF16)
            dma("sync", t.rearrange("p (a b) -> p a b", a=16), X1[:])
            barrier()

        with ExitStack() as es5:
            def sb5(name, shape, dtype):
                return es5.enter_context(nc.sbuf_tensor(name, shape, dtype))

            WUP = [sb5(f"WUP{i}", [128, 16, 128], BF16) for i in range(2)]
            WGT = [sb5(f"WGT{i}", [128, 16, 128], BF16) for i in range(2)]
            UPT = [sb5(f"UPT{i}", [128, TM], BF16) for i in range(2)]
            GTT = [sb5(f"GTT{i}", [128, TM], BF16) for i in range(2)]
            HPB0 = sb5("HPB0", [128, TM], BF16)
            HPB = [HPB0, HPB0]
            H1T = [sb5(f"H1T{i}", [128, TM], BF16) for i in range(2)]
            bank_tok = {i: None for i in range(8)}
            w_free = {0: None, 1: None}
            buf_free = {0: None, 1: None}
            SC81 = 0.125
            for i in range(64):
                p = i % 2
                wl = [w_free[p]] if w_free[p] else []
                wU = dma("gpsimd", WUP[p][:], d_wup[i], waits=wl)
                wG = dma("gpsimd", WGT[p][:], d_wgate[i], waits=wl)
                st_toks = []

                def evac(s4, tU, tG, i=i, p=p):
                    bU, bG = s4 % 2, 2 + s4 % 2
                    aU = op("vector", lambda v, i=i, p=p, s4=s4, bU=bU: v.tensor_scalar(
                        UPT[p][:, s4 * 512:(s4 + 1) * 512], PS[bU][:, :],
                        WSCUP[:, i:i + 1], None, ALU.mult), waits=[tU])
                    aG = op("scalar", lambda a, i=i, p=p, s4=s4, bG=bG: a.activation(
                        GTT[p][:, s4 * 512:(s4 + 1) * 512], PS[bG][:, :],
                        AF.Silu, scale=WSCGATE[:, i:i + 1]), waits=[tG])
                    bank_tok[bG] = aG
                    vHP = op("vector", lambda v, i=i, p=p, s4=s4: v.scalar_tensor_tensor(
                        HPB[p][:, s4 * 512:(s4 + 1) * 512],
                        GTT[p][:, s4 * 512:(s4 + 1) * 512], SUDP[:, i:i + 1],
                        UPT[p][:, s4 * 512:(s4 + 1) * 512], ALU.mult, ALU.mult),
                        waits=[aU, aG])
                    tS1 = op("tensor", lambda t, p=p, s4=s4, bU=bU: t.matmul(
                        PS[bU][:, :], M1C[:],
                        HPB[p][:, s4 * 512:(s4 + 1) * 512],
                        start=True, stop=True), waits=[vHP, aU], inc=True)
                    aH1 = op("scalar", lambda a, p=p, s4=s4, bU=bU: a.activation(
                        H1T[p][:, s4 * 512:(s4 + 1) * 512], PS[bU][:, :],
                        AF.Copy, scale=SC81), waits=[tS1])
                    bank_tok[bU] = aH1
                    st_toks.append(aH1)

                pend = None
                last_g = None
                for s4 in range(4):
                    bU, bG = s4 % 2, 2 + s4 % 2
                    wl = [wU]
                    if bank_tok[bU] is not None:
                        wl.append(bank_tok[bU])
                    if s4 == 0 and buf_free[p] is not None:
                        wl.append(buf_free[p])

                    def mmu(t, i=i, p=p, s4=s4, bU=bU):
                        ins = None
                        for k in range(16):
                            ins = t.matmul(PS[bU][:, :], WUP[p][:, k, :],
                                           X1[:, k, s4 * 512:(s4 + 1) * 512],
                                           start=(k == 0), stop=(k == 15))
                        return ins
                    tU = op("tensor", mmu, waits=wl, inc=True)
                    wl = [wG]
                    if bank_tok[bG] is not None:
                        wl.append(bank_tok[bG])

                    def mmg(t, i=i, p=p, s4=s4, bG=bG):
                        ins = None
                        for k in range(16):
                            ins = t.matmul(PS[bG][:, :], WGT[p][:, k, :],
                                           X1[:, k, s4 * 512:(s4 + 1) * 512],
                                           start=(k == 0), stop=(k == 15))
                        return ins
                    tG = op("tensor", mmg, waits=wl, inc=True)
                    last_g = tG
                    if pend is not None:
                        pend()
                    pend = (lambda s4=s4, tU=tU, tG=tG: evac(s4, tU, tG))
                pend()
                w_free[p] = last_g
                buf_free[p] = dma("sync", h1_scr[i * 128:(i + 1) * 128, :],
                                  H1T[p][:], waits=[st_toks[-1]])
                next(gen, None)
            for _ in gen:
                pass
            barrier()
        es_b2.close()
        es_a.close()

        if "h1" in dbg:
            t = dbg_tensor("h1", [INTER, TM], BF16)
            dma("sync", t, h1_scr)
            barrier()

        with ExitStack() as es6:
            def sb6(name, shape, dtype):
                return es6.enter_context(nc.sbuf_tensor(name, shape, dtype))

            CSD = sb6("CSD", [64, TM], F32)
            ECSR = sb6("ECSR", [64, TM], BF16)
            WMB = sb6("WMB", [64, TM], BF16)
            CTMP = sb6("CTMP", [64, LC], F32)
            CTM2 = sb6("CTM2", [64, LC], F32)
            DCALL = sb6("DCALL", [64, NCH], BF16)
            Z64 = sb6("Z64", [64, LC], F32)
            XSL = [sb6(f"XSL{i}", [128, 8, 256], BF16) for i in range(2)]
            BSL = [sb6(f"BSL{i}", [128, 8, 256], BF16) for i in range(2)]
            CCB = sb6("CCB", [128, 32, 256], BF16)
            XTT = sb6("XTT", [128, 8, 128], BF16)
            BTT = sb6("BTT", [128, 8, 128], BF16)
            WTT = sb6("WTT", [128, 64], BF16)
            CST = sb6("CST", [128, 64], F32)
            CSREP0 = sb6("CSREP0", [128, 2048], F32)
            CSREP = [CSREP0, CSREP0]
            ECSREP = [sb6(f"ECSREP{i}", [128, 2048], BF16) for i in range(2)]
            T1B0 = sb6("T1B0", [128, 2048], F32)
            T1B = [T1B0, T1B0]
            LTB = [sb6(f"LTB{i}", [128, 2048], BF16) for i in range(2)]
            GTS = [sb6(f"GTS{i}", [128, 1024], BF16) for i in range(2)]
            GLB = [sb6(f"GLB{i}", [128, 2048], BF16) for i in range(2)]
            CSBQ = [sb6(f"CSBQ{i}", [128, 2048], BF16) for i in range(2)]
            XWS = [sb6(f"XWS{i}", [128, 4, 2, 64], BF16) for i in range(2)]
            SSB = sb6("SSB", [128, NH * HEADDIM], BF16)
            Y4 = sb6("Y4", [128, 32, 256], BF16)
            DCC6 = sb6("DCC6", [64, 1], BF16)
            DCB6 = sb6("DCB6", [128, 64], F32)
            DIALL = sb6("DIALL", [128, NH * 128], BF16)
            DBRD = sb6("DBRD", [128, 64], BF16)

            tD = op("tensor", lambda t: t.matmul(PS[0][:, 0:64], ONES1[:], DROW[:],
                                                 start=True, stop=True),
                    waits=[wG0], inc=True)
            vDB = op("vector", lambda v: v.tensor_copy(DBRD[:], PS[0][:, 0:64]),
                     waits=[tD])
            op("vector", lambda v: v.tensor_tensor(
                DIALL[:].rearrange("p (h l) -> p h l", h=64),
                IDENT[:].unsqueeze(1).broadcast_to([128, 64, 128]),
                DBRD[:].unsqueeze(2).broadcast_to([128, 64, 128]),
                ALU.mult))

            vZ = op("vector", lambda v: v.memset(Z64[:], 0.0))
            aLDT = op("scalar", lambda a: a.activation(CSD[:], DTM[:], AF.Ln))
            wCS = None
            cs_spill = {}
            ecs_spill = {}
            for c in range(NCH):
                slc = slice(c * LC, (c + 1) * LC)
                op("vector", lambda v, slc=slc: v.tensor_scalar(
                    CTM2[:], DTM[:, slc], AH[:], None, ALU.mult))
                vC = op("vector", lambda v: v.tensor_tensor_scan(
                    CTMP[:], CTM2[:], Z64[:], 0.0, ALU.add, ALU.add),
                    waits=[vZ] + ([wCS] if wCS else []))
                wCS = dma("sync",
                          csm_scr[c * 4:c * 4 + 4, :].rearrange("q (h l) -> (q h) l", h=16),
                          CTMP[:], waits=[vC])
                cs_spill[c] = wCS
                op("scalar", lambda a, slc=slc: a.activation(
                    ECSR[:, slc], CTMP[:], AF.Exp), waits=[vC])
                wEC = dma("sync",
                          ecs_scr[c * 4:c * 4 + 4, :].rearrange("q (h l) -> (q h) l", h=16),
                          ECSR[:, slc], waits=[(S["A"], S["A"].n)])
                ecs_spill[c] = wEC
                op("scalar", lambda a, c=c: a.activation(
                    DCALL[:, c:c + 1], CTMP[:, LC - 1:LC], AF.Exp))
                op("vector", lambda v: v.tensor_tensor(
                    CTM2[:], CTMP[:, LC - 1:LC].broadcast_to([64, LC]),
                    CTMP[:], ALU.subtract))
                aD = op("scalar", lambda a: a.activation(CTM2[:], CTM2[:], AF.Exp),
                        waits=[(S["V"], S["V"].n)])
                op("vector", lambda v, slc=slc: v.tensor_tensor(
                    WMB[:, slc], CTM2[:], DTM[:, slc], ALU.mult), waits=[aD])
                op("vector", lambda v, slc=slc: v.tensor_tensor(
                    CSD[:, slc], CTMP[:], CSD[:, slc], ALU.subtract), waits=[aLDT])
            vWMB = (S["V"], S["V"].n)
            vCSD = vWMB
            wECS = wEC

            bank_tok = {i: None for i in range(8)}
            bank_tok[0] = vDB
            slab_tok = {0: None, 1: None}
            slab_used = {0: None, 1: None}
            cc_tok = None
            cc_used = None
            rep_used = {0: None, 1: None}
            y4_used = None
            xw_used = {0: None, 1: None}
            dcc_prev = None
            dcbb_prev6 = None
            y_last_prev = None
            glb_tok = {}
            cs_q_tok = {}
            lt_tok = {}
            gts_tok = {}
            rep_tok = {}
            t1_tok = {}

            def stage_rep(c, q4):
                qp = q4 % 2
                prev_t1 = t1_tok.get((c, q4 - 1) if q4 >= 1 else (c - 1, 3))
                wl = [cs_spill[c]] + ([prev_t1] if prev_t1 else [])
                r1 = dma("sync", CSREP[qp][:],
                         csm_scr[c * 4 + q4:c * 4 + q4 + 1, :]
                         .broadcast_to([128, 2048]), waits=wl)
                prev_cs = cs_q_tok.get((c, q4 - 2) if q4 >= 2 else (c - 1, q4 + 2))
                wl2 = [ecs_spill[c]] + ([prev_cs] if prev_cs else [])
                r2 = dma("sync", ECSREP[qp][:],
                         ecs_scr[c * 4 + q4:c * 4 + q4 + 1, :]
                         .broadcast_to([128, 2048]), waits=wl2)
                rep_tok[(c, q4)] = [r1, r2]

            def stage_t1(c, q4, vT3):
                qp = q4 % 2
                wl = rep_tok[(c, q4)] + [vT3]
                if (c, q4, "lt") in lt_tok:
                    pass
                prev = lt_tok.get((c, q4 - 1) if q4 >= 1 else (c - 1, 3))
                if prev is not None:
                    wl.append(prev)
                op("vector", lambda v, c=c, q4=q4, qp=qp: v.tensor_tensor(
                    T1B[qp][:].rearrange("p (h l) -> p h l", h=16),
                    CSREP[qp][:].rearrange("p (h l) -> p h l", h=16),
                    CST[:, 16 * q4:16 * q4 + 16].unsqueeze(2)
                    .broadcast_to([128, 16, 128]), ALU.subtract), waits=wl)
                t1_tok[(c, q4)] = op("vector", lambda v, qp=qp: v.tensor_tensor(
                    T1B[qp][:].rearrange("p (h l) -> p h l", l=128),
                    T1B[qp][:].rearrange("p (h l) -> p h l", l=128),
                    SLNEG[:].unsqueeze(1).broadcast_to([128, 16, 128]), ALU.add))
                lt_tok[(c, q4)] = op("scalar", lambda a, qp=qp: a.activation(
                    LTB[qp][:], T1B[qp][:], AF.Exp), waits=[t1_tok[(c, q4)]])

            def stage_gt(c, q4, p2, csl):
                qp = q4 % 2
                gt_mm = []
                for gg in range(8):
                    gc = q4 * 8 + gg
                    bank = gg % 2
                    slot = gg // 2
                    wl = list(cc_tok)
                    if slot == 0 and bank_tok[bank] is not None:
                        wl.append(bank_tok[bank])
                    gt_mm.append(op("tensor", lambda t, gc=gc, bank=bank, slot=slot, p2=p2, csl=csl: t.matmul(
                        PS[bank][:, slot * 128:(slot + 1) * 128],
                        BSL[p2][:, (gc // 4) % 8, csl:csl + 128],
                        CCB[:, gc, csl:csl + 128],
                        start=True, stop=True), waits=wl, inc=True))
                prevg = gts_tok.get((c, q4 - 2) if q4 >= 2 else (c - 1, q4 + 2))
                wl = [gt_mm[6]] + ([prevg] if prevg else [])
                aG1 = op("scalar", lambda a, qp=qp: a.activation(
                    GTS[qp][:].rearrange("p (k two l) -> p k two l", two=2, l=128)[:, :, 0, :],
                    PS[0][:].rearrange("p (k l) -> p k l", k=4), AF.Copy), waits=wl)
                aG2 = op("scalar", lambda a, qp=qp: a.activation(
                    GTS[qp][:].rearrange("p (k two l) -> p k two l", two=2, l=128)[:, :, 1, :],
                    PS[1][:].rearrange("p (k l) -> p k l", k=4), AF.Copy),
                    waits=[gt_mm[7]])
                bank_tok[0] = aG1
                bank_tok[1] = aG2
                gts_tok[(c, q4)] = aG2

            def stage_mid(c, q4, p2, csl, vSSB, vT2, trx):
                qp = q4 % 2
                wl = [lt_tok[(c, q4)], gts_tok[(c, q4)], bank_tok[0]]
                prev = glb_tok.get((c, q4 - 2) if q4 >= 2 else (c - 1, q4 + 2))
                if prev is not None:
                    wl.append(prev)
                op("vector", lambda v, qp=qp: v.tensor_tensor(
                    GLB[qp][:].rearrange("p (h two l) -> p h two l", two=2, l=128),
                    LTB[qp][:].rearrange("p (h two l) -> p h two l", two=2, l=128),
                    GTS[qp][:].rearrange("p (h l) -> p h l", h=8).unsqueeze(2)
                    .broadcast_to([128, 8, 2, 128]), ALU.mult), waits=wl)
                glb_tok[(c, q4)] = op("vector", lambda v, q4=q4, qp=qp: v.tensor_tensor(
                    GLB[qp][:], GLB[qp][:],
                    DIALL[:, q4 * 2048:(q4 + 1) * 2048], ALU.add))
                wl = rep_tok[(c, q4)][1:]
                prev = cs_q_tok.get((c, q4 - 2) if q4 >= 2 else (c - 1, q4 + 2))
                if prev is not None:
                    wl.append(prev)
                cs_q_tok[(c, q4)] = op("vector", lambda v, q4=q4, qp=qp, csl=csl: v.tensor_tensor(
                    CSBQ[qp][:].rearrange("p (h two l) -> p h two l", two=2, l=128),
                    CCB[:, q4 * 8:(q4 + 1) * 8, csl:csl + 128].unsqueeze(2)
                    .broadcast_to([128, 8, 2, 128]),
                    ECSREP[qp][:].rearrange("p (h two l) -> p h two l", two=2, l=128),
                    ALU.mult), waits=wl)
                st_toks = {}
                for gg in range(2):
                    gb = 2 * q4 + gg
                    bank = 2 + gg
                    xwp = gg
                    wl = [vT2, trx]
                    if xw_used[xwp] is not None:
                        wl.append(xw_used[xwp])
                    vXW = op("vector", lambda v, gb=gb, xwp=xwp, c=c: v.tensor_tensor(
                        XWS[xwp][:],
                        XTT[:, gb, :].rearrange("p (a q) -> p a q", a=2)
                        .unsqueeze(1).broadcast_to([128, 4, 2, 64]),
                        WTT[:, gb * 8:gb * 8 + 8]
                        .rearrange("p (a b) -> p a b", a=4)
                        .unsqueeze(3).broadcast_to([128, 4, 2, 64]),
                        ALU.mult), waits=wl)
                    wl = [vXW]
                    if bank_tok[bank] is not None:
                        wl.append(bank_tok[bank])
                    tST = op("tensor", lambda t, gb=gb, bank=bank, xwp=xwp: t.matmul(
                        PS[bank][:, :], BTT[:, gb, :],
                        XWS[xwp][:].rearrange("p a b q -> p (a b q)"),
                        start=True, stop=True), waits=wl, inc=True)
                    st_toks[gg] = tST
                    xw_used[xwp] = tST
                return st_toks

            def stage_y(c, q4, csl, vSSB, vDCB, st_toks):
                qp = q4 % 2
                y_toks = []
                for gg in range(8):
                    g = q4 * 8 + gg
                    bank = 6 + gg % 2
                    slot = gg // 2
                    h0, h1 = 2 * g, 2 * g + 1
                    wl = [glb_tok[(c, q4)], cs_q_tok[(c, q4)], vSSB]
                    if slot == 0:
                        wl.append(bank_tok[bank])

                    def ymm(t, gg=gg, bank=bank, slot=slot, h0=h0, h1=h1, qp=qp):
                        xs0 = XTT[:, h0 // 8, (h0 % 2) * 64:(h0 % 2) * 64 + 64]
                        xs1 = XTT[:, h1 // 8, (h1 % 2) * 64:(h1 % 2) * 64 + 64]
                        sl = slice(slot * 128, (slot + 1) * 128)
                        t.matmul(PS[bank][0:64, sl], xs0,
                                 GLB[qp][:, (gg * 2) * 128:(gg * 2 + 1) * 128],
                                 start=True, stop=False, tile_position=(0, 0))
                        t.matmul(PS[bank][0:64, sl], SSB[:, h0 * 64:h0 * 64 + 64],
                                 CSBQ[qp][:, (gg * 2) * 128:(gg * 2 + 1) * 128],
                                 start=False, stop=True, tile_position=(0, 0))
                        t.matmul(PS[bank][64:128, sl], xs1,
                                 GLB[qp][:, (gg * 2 + 1) * 128:(gg * 2 + 2) * 128],
                                 start=True, stop=False, tile_position=(0, 64))
                        return t.matmul(PS[bank][64:128, sl],
                                        SSB[:, h1 * 64:h1 * 64 + 64],
                                        CSBQ[qp][:, (gg * 2 + 1) * 128:(gg * 2 + 2) * 128],
                                        start=False, stop=True,
                                        tile_position=(0, 64))
                    y_toks.append(op("tensor", ymm, waits=wl, inc=True))
                for bb in range(2):
                    bank = 6 + bb
                    wl = [y_toks[6 + bb]]
                    if y4_used is not None:
                        wl.extend(y4_used)
                    aY = op("scalar", lambda a, q4=q4, bb=bb, bank=bank, csl=csl: a.activation(
                        Y4[:].rearrange("p (a b2) q -> p a b2 q", b2=2)
                        [:, q4 * 4:(q4 + 1) * 4, bb, csl:csl + 128],
                        PS[bank][:].rearrange("p (j q) -> p j q", j=4),
                        AF.Copy), waits=wl)
                    bank_tok[bank] = aY
                for gg in range(2):
                    gb = 2 * q4 + gg
                    bank = 2 + gg
                    wl = [vDCB, st_toks[gg], vSSB]
                    op("vector", lambda v, gb=gb: v.tensor_tensor(
                        SST[:, gb * 512:(gb + 1) * 512].rearrange("p (a q) -> p a q", a=8),
                        SST[:, gb * 512:(gb + 1) * 512].rearrange("p (a q) -> p a q", a=8),
                        DCB6[:, gb * 8:gb * 8 + 8].unsqueeze(2).broadcast_to([128, 8, 64]),
                        ALU.mult), waits=wl)
                    bank_tok[bank] = op("vector", lambda v, gb=gb, bank=bank: v.tensor_tensor(
                        SST[:, gb * 512:(gb + 1) * 512],
                        SST[:, gb * 512:(gb + 1) * 512],
                        PS[bank][:, :], ALU.add))
                return y_toks[-1]

            stage_rep(0, 0)
            for c in range(NCH):
                c2 = c // 2
                p2 = c2 % 2
                csl = 128 * (c % 2)
                if c % 2 == 0:
                    wl = [slab_used[p2]] if slab_used[p2] else []
                    for xg in range(8):
                        dma("gpsimd", XSL[p2][:, xg, :],
                            xbc_scr[xg * 128:(xg + 1) * 128, c * LC:(c + 2) * LC],
                            waits=wl if xg == 0 else [])
                    for xg in range(8):
                        dma("gpsimd", BSL[p2][:, xg, :],
                            xbc_scr[1024 + xg * 128:1024 + (xg + 1) * 128,
                                    c * LC:(c + 2) * LC])
                    wl = [cc_used] if cc_used else []
                    for gc in range(32):
                        dma("gpsimd", CCB[:, gc, :],
                            xbc_scr[2048 + gc * 128:2048 + (gc + 1) * 128,
                                    c * LC:(c + 2) * LC],
                            waits=wl if gc == 0 else [])
                    slab_all = B.dma_tokens("gpsimd")
                    slab_tok[p2] = slab_all
                    cc_tok = slab_all
                trev = []
                for half in range(4):
                    bank = 6 + half % 2
                    wl = list(slab_tok[p2])
                    if bank_tok[bank] is not None:
                        wl.append(bank_tok[bank])
                    if y_last_prev is not None:
                        wl.append(y_last_prev)

                    def tfn(t, half=half, p2=p2, csl=csl, bank=bank):
                        ins = None
                        src_t = XSL[p2] if half < 2 else BSL[p2]
                        for j in range(4):
                            xg = (half % 2) * 4 + j
                            ins = t.transpose(
                                ptb(bank)[:, j * 128:(j + 1) * 128],
                                src_t[:, xg, csl:csl + 128], IDENT[:])
                        return ins
                    tT = op("tensor", tfn, waits=wl, inc=True)

                    def evf(v, half=half, bank=bank):
                        dst = XTT if half < 2 else BTT
                        gi = (half % 2) * 4
                        return v.tensor_copy(
                            dst[:, gi:gi + 4, :],
                            ptb(bank).rearrange("p (j q) -> p j q", j=8)[:, 0:4, :])
                    bank_tok[bank] = op("vector", evf, waits=[tT])
                    trev.append(bank_tok[bank])
                trx = trev[-1]
                wl = [vWMB] + ([bank_tok[6]] if bank_tok[6] else [])
                t2 = op("tensor", lambda t, c=c: t.transpose(
                    ptb(6)[:, 128:192], WMB[:, c * LC:(c + 1) * LC], IDENT[0:64, 0:64]),
                    waits=wl, inc=True)
                vT2 = op("vector", lambda v: v.tensor_copy(WTT[:], ptb(6)[:, 128:192]),
                         waits=[t2])
                bank_tok[6] = vT2
                wl = [vCSD] + ([bank_tok[7]] if bank_tok[7] else [])
                tCS = op("tensor", lambda t, c=c: t.transpose(
                    PS[7][:, 0:64], CSD[:, c * LC:(c + 1) * LC], IDF[0:64, 0:64]),
                    waits=wl, inc=True)
                vT3 = op("vector", lambda v: v.tensor_copy(CST[:], PS[7][:, 0:64]),
                         waits=[tCS])
                bank_tok[7] = vT3
                wl = [vWMB] + ([dcc_prev] if dcc_prev else [])
                aDC = op("scalar", lambda a, c=c: a.copy(
                    DCC6[:], DCALL[:, c:c + 1]), waits=wl)
                wD0 = dma("sync", dcc_scr, DCC6[:], waits=[aDC])
                dcc_prev = wD0
                wl = [wD0] + ([dcbb_prev6] if dcbb_prev6 else [])
                wDB6 = dma("sync", DCBB[:],
                           dcc_scr.rearrange("a b -> b a").broadcast_to([128, 64]),
                           waits=wl)
                vDCB = op("vector", lambda v: v.tensor_copy(DCB6[:], DCBB[:]),
                          waits=[wDB6])
                dcbb_prev6 = vDCB
                wl = [(S["V"], S["V"].n)] + ([y_last_prev] if y_last_prev else [])
                vSSB = op("vector", lambda v: v.tensor_copy(SSB[:], SST[:]), waits=wl)
                stage_t1(c, 0, vT3)
                stage_gt(c, 0, p2, csl)
                for q4 in range(4):
                    if q4 + 1 < 4:
                        stage_rep(c, q4 + 1)
                        stage_t1(c, q4 + 1, vT3)
                        stage_gt(c, q4 + 1, p2, csl)
                    elif c + 1 < NCH:
                        stage_rep(c + 1, 0)
                    st_toks = stage_mid(c, q4, p2, csl, vSSB, vT2, trx)
                    y_last_prev = stage_y(c, q4, csl, vSSB, vDCB, st_toks)
                    qp = q4 % 2
                    rep_used[qp] = y_last_prev
                if c % 2 == 1:
                    wl = [(S["A"], S["A"].n)]
                    for j in range(32):
                        dma("sync", y_scr[j * 128:(j + 1) * 128,
                                          (c - 1) * LC:(c + 1) * LC],
                            Y4[:, j, :], waits=wl if j == 0 else [])
                    y4_used = B.dma_tokens("sync")
                    slab_used[p2] = y_last_prev
                    cc_used = y_last_prev
            barrier()

        es_mid.close()

        if "ys" in dbg:
            t = dbg_tensor("ys", [DI, TM], BF16)
            dma("sync", t, y_scr)
            barrier()

        SC82 = float(2.0 ** -3.5)
        with ExitStack() as es7:
            def sb7(name, shape, dtype):
                return es7.enter_context(nc.sbuf_tensor(name, shape, dtype))

            X2H = sb7("X2H", [128, 64, 1024], BF16)
            WDB = [sb7(f"WDB{i}", [128, 64, 128], BF16) for i in range(2)]
            G0T = [sb7(f"G0T{i}", [128, 512], BF16) for i in range(2)]
            h1v = h1_scr.rearrange("(q v) t -> v q t", v=64)
            bank_tok7 = {i: None for i in range(8)}
            wdb_free = {0: None, 1: None}
            g0_free = {0: None, 1: None}
            x2_used = None
            for half in range(2):
                ho = half * 1024
                for v64 in range(64):
                    wl = [x2_used] if x2_used else []
                    wH = dma("gpsimd", X2H[:, v64, :],
                             h1v[v64, :, ho:ho + 1024], waits=wl)
                    for s2 in range(2):
                        bank = (v64 * 2 + s2) % 6
                        wl = [wH]
                        if bank_tok7[bank] is not None:
                            wl.append(bank_tok7[bank])
                        tMM = op("tensor", lambda t, v64=v64, s2=s2, bank=bank: t.matmul(
                            PS[bank][:, :], H128[:],
                            X2H[:, v64, s2 * 512:(s2 + 1) * 512],
                            start=True, stop=True), waits=wl, inc=True)
                        bank_tok7[bank] = op("scalar", lambda a, v64=v64, s2=s2, bank=bank: a.activation(
                            X2H[:, v64, s2 * 512:(s2 + 1) * 512], PS[bank][:, :],
                            AF.Copy, scale=SC82), waits=[tMM])
                aX2 = (S["A"], S["A"].n)
                for m in range(16):
                    p = m % 2
                    wl = [wdb_free[p]] if wdb_free[p] else []
                    wW = dma("gpsimd", WDB[p][:], d_wdown[m], waits=wl)
                    for s2 in range(2):
                        bank = 6 + s2
                        gp = s2
                        wl = [wW, aX2]
                        if bank_tok7[bank] is not None:
                            wl.append(bank_tok7[bank])

                        def mmd(t, p=p, s2=s2, bank=bank):
                            ins = None
                            for v64 in range(64):
                                ins = t.matmul(PS[bank][:, :], WDB[p][:, v64, :],
                                               X2H[:, v64, s2 * 512:(s2 + 1) * 512],
                                               start=(v64 == 0), stop=(v64 == 63))
                            return ins
                        tMM = op("tensor", mmd, waits=wl, inc=True)
                        if s2 == 1:
                            wdb_free[p] = tMM
                        wl = [tMM]
                        if g0_free[gp] is not None:
                            wl.append(g0_free[gp])
                        aG0 = op("scalar", lambda a, m=m, gp=gp, bank=bank: a.activation(
                            G0T[gp][:], PS[bank][:, :], AF.Silu,
                            scale=WSCDP[:, m:m + 1]), waits=wl)
                        bank_tok7[bank] = aG0
                        g0_free[gp] = dma(
                            "sync", g_scr[m * 128:(m + 1) * 128,
                                          ho + s2 * 512:ho + (s2 + 1) * 512],
                            G0T[gp][:], waits=[aG0])
                x2_used = bank_tok7[7]
            barrier()

        if "gs" in dbg:
            t = dbg_tensor("gs", [DM, TM], BF16)
            dma("sync", t, g_scr)
            barrier()

        es_b = ExitStack()

        def sbb(name, shape, dtype):
            return es_b.enter_context(nc.sbuf_tensor(name, shape, dtype))

        YN = sbb("YN", [128, 32, TM], BF16)
        with ExitStack() as es8:
            def sb8(name, shape, dtype):
                return es8.enter_context(nc.sbuf_tensor(name, shape, dtype))

            OG8 = sb8("OG8", [128, 32, 32], BF16)
            EB8 = sb8("EB8", [32, 32, 128], BF16)
            YT = [sb8(f"YT{i}", [128, TM], BF16) for i in range(2)]
            ZT = [sb8(f"ZT{i}", [128, TM], BF16) for i in range(2)]
            ZS = sb8("ZS", [128, TM], BF16)
            SQG = sb8("SQG", [128, TM], BF16)
            SMS = sb8("SMS", [32, TM], F32)
            RSTB = sb8("RSTB", [32, TM], BF16)
            dma("sync", OG8[:], og_c.rearrange("(g p) m -> p g m", g=32))
            dma("sync", EB8[:], eb_c.rearrange("(g k) m -> k g m", g=32))
            wC8 = B.dma_tokens("sync")
            yz_free = {0: None, 1: None}
            sq_toks = []
            for grp in range(32):
                p = grp % 2
                wl = ([yz_free[p]] if yz_free[p] else [])
                wY = dma("sync", YT[p][:], y_scr[grp * 128:(grp + 1) * 128, :], waits=wl)
                wZ = dma("sync", ZT[p][:], z_scr[grp * 128:(grp + 1) * 128, HALO:], waits=wl)
                aZ = op("scalar", lambda a, p=p: a.activation(ZS[:], ZT[p][:], AF.Silu),
                        waits=[wZ, (S["V"], S["V"].n)])
                vYG = op("vector", lambda v, grp=grp, p=p: v.tensor_tensor(
                    YN[:, grp, :], YT[p][:], ZS[:], ALU.mult), waits=[wY, aZ])
                yz_free[p] = vYG
                aSQ = op("scalar", lambda a, grp=grp: a.activation(
                    SQG[:], YN[:, grp, :], AF.Square),
                    waits=[vYG] + (sq_toks[-4:] if grp >= 1 else []))
                sq4 = []
                for s4 in range(4):
                    wl = [aSQ] + (wC8 if grp == 0 else [])
                    sq4.append(op("tensor", lambda t, grp=grp, s4=s4: t.matmul(
                        PS[s4][0:32, :], OG8[:, grp, :],
                        SQG[:, s4 * 512:(s4 + 1) * 512],
                        start=(grp == 0), stop=(grp == 31)), waits=wl, inc=True))
                sq_toks = sq4
            ev4 = []
            for s4 in range(4):
                ev4.append(op("vector", lambda v, s4=s4: v.tensor_copy(
                    SMS[:, s4 * 512:(s4 + 1) * 512], PS[s4][0:32, :]),
                    waits=[sq_toks[s4]]))
            aL = op("scalar", lambda a: a.activation(
                SMS[:], SMS[:], AF.Ln, scale=1.0 / 128.0, bias=EPSC[0:32, :]),
                waits=ev4)
            aR8 = op("scalar", lambda a: a.activation(
                SMS[:], SMS[:], AF.Exp, scale=-0.5))
            vRB = op("vector", lambda v: v.tensor_copy(RSTB[:], SMS[:]), waits=[aR8])
            bank_tok8 = {i: None for i in range(8)}
            for grp in range(32):
                for s4 in range(4):
                    bank = (grp * 4 + s4) % 8
                    wl = [vRB]
                    if bank_tok8[bank] is not None:
                        wl.append(bank_tok8[bank])
                    tB = op("tensor", lambda t, grp=grp, s4=s4, bank=bank: t.matmul(
                        PS[bank][:, :], EB8[:, grp, :],
                        RSTB[0:32, s4 * 512:(s4 + 1) * 512],
                        start=True, stop=True), waits=wl, inc=True)
                    bank_tok8[bank] = op("vector", lambda v, grp=grp, s4=s4, bank=bank: v.scalar_tensor_tensor(
                        YN[:, grp, s4 * 512:(s4 + 1) * 512],
                        YN[:, grp, s4 * 512:(s4 + 1) * 512],
                        NW[:, grp:grp + 1], PS[bank][:, :],
                        ALU.mult, ALU.mult), waits=[tB])
            barrier()

        if "yn" in dbg:
            t = dbg_tensor("yn", [128, 32 * TM], BF16)
            dma("sync", t.rearrange("p (a b) -> p a b", a=32), YN[:])
            barrier()

        with ExitStack() as es9:
            def sb9(name, shape, dtype):
                return es9.enter_context(nc.sbuf_tensor(name, shape, dtype))

            WOB = [sb9(f"WOB{i}", [128, 32, 128], BF16) for i in range(2)]
            GT9 = [sb9(f"GT9{i}", [128, TM], BF16) for i in range(2)]
            SOT = sb9("SOT", [128, 512], BF16)
            SGT = [sb9(f"SGT{i}", [128, TM], BF16) for i in range(2)]
            bank_tok9 = {i: None for i in range(8)}
            w_free9 = {0: None, 1: None}
            sg_free = {0: None, 1: None}
            for m in range(16):
                p = m % 2
                wl = [w_free9[p]] if w_free9[p] else []
                wW = dma("gpsimd", WOB[p][:], d_wout[m], waits=wl)
                wG = dma("sync", GT9[p][:], g_scr[m * 128:(m + 1) * 128, :],
                         waits=[sg_free[p]] if sg_free[p] else [])
                for s4 in range(4):
                    bank = (m * 4 + s4) % 8

                    def mmo(t, p=p, s4=s4, bank=bank):
                        ins = None
                        for k in range(32):
                            ins = t.matmul(PS[bank][:, :], WOB[p][:, k, :],
                                           YN[:, k, s4 * 512:(s4 + 1) * 512],
                                           start=(k == 0), stop=(k == 31))
                        return ins
                    wl = [wW]
                    if bank_tok9[bank] is not None:
                        wl.append(bank_tok9[bank])
                    tMM = op("tensor", mmo, waits=wl, inc=True)
                    if s4 == 3:
                        w_free9[p] = tMM
                    aS = op("scalar", lambda a, s4=s4, bank=bank: a.activation(
                        SOT[:], PS[bank][:, :], AF.Copy),
                        waits=[tMM, (S["V"], S["V"].n)])
                    bank_tok9[bank] = aS
                    op("vector", lambda v, p=p, s4=s4: v.tensor_tensor(
                        SGT[p][:, s4 * 512:(s4 + 1) * 512], SOT[:],
                        GT9[p][:, s4 * 512:(s4 + 1) * 512], ALU.mult),
                        waits=[aS, wG])
                sg_free[p] = dma("sync", sg_scr[m * 128:(m + 1) * 128, :], SGT[p][:],
                                 waits=[(S["V"], S["V"].n)])
            barrier()
        es_b.close()

        with ExitStack() as es10:
            def sb10(name, shape, dtype):
                return es10.enter_context(nc.sbuf_tensor(name, shape, dtype))

            OUTS = sb10("OUTS", [128, 16, DM], BF16)
            SGB = [sb10(f"SGB{i}", [128, TM], BF16) for i in range(2)]
            UB = [sb10(f"UB{i}", [128, DM], F32) for i in range(2)]
            OUTF = [sb10(f"OUTF{i}", [128, DM], F32) for i in range(2)]
            bank_tokX = {6: None, 7: None}
            sgb_free = {0: None, 1: None}
            for m in range(16):
                p = m % 2
                wl = [sgb_free[p]] if sgb_free[p] else []
                wS9 = dma("sync", SGB[p][:], sg_scr[m * 128:(m + 1) * 128, :], waits=wl)
                for half in range(4):
                    bank = 6 + half % 2
                    wl = [wS9]
                    if bank_tokX[bank] is not None:
                        wl.append(bank_tokX[bank])

                    def tfn(t, p=p, half=half, bank=bank):
                        ins = None
                        for j in range(4):
                            tt = half * 4 + j
                            ins = t.transpose(
                                ptb(bank)[:, j * 128:(j + 1) * 128],
                                SGB[p][:, tt * 128:(tt + 1) * 128], IDENT[:])
                        return ins
                    tT = op("tensor", tfn, waits=wl, inc=True)
                    bank_tokX[bank] = op("vector", lambda v, m=m, half=half, bank=bank: v.tensor_copy(
                        OUTS[:, half * 4:half * 4 + 4, m * 128:(m + 1) * 128],
                        ptb(bank).rearrange("p (j q) -> p j q", j=8)[:, 0:4, :]),
                        waits=[tT])
                sgb_free[p] = (S["V"], S["V"].n)
            vTdone = (S["V"], S["V"].n)
            ub_free = {0: None, 1: None}
            for tt in range(16):
                p = tt % 2
                wl = [ub_free[p]] if ub_free[p] else []
                wU = dma("sync", UB[p][:],
                         d_u[HALO + tt * 128:HALO + (tt + 1) * 128, :], waits=wl)
                vO = op("vector", lambda v, tt=tt, p=p: v.tensor_tensor(
                    OUTF[p][:], UB[p][:], OUTS[:, tt, :], ALU.add),
                    waits=[wU, vTdone])
                ub_free[p] = dma("sync", d_out[tt * 128:(tt + 1) * 128, :],
                                 OUTF[p][:], waits=[vO])
            barrier()

        for (c, t) in B.dma_tokens("sync"):
            B.raw("sync", lambda e, c=c, t=t: e.wait_ge(c.h, t))
        for (c, t) in B.dma_tokens("gpsimd"):
            B.raw("gpsimd", lambda e, c=c, t=t: e.wait_ge(c.h, t))
        B.emit()
    return nc, dbg_outs


def _blk_weightT(wT, mt_w=128, kt=16):
    cin, cout = wT.shape
    assert cin == kt * 128
    blocks = []
    for m0 in range(0, cout, mt_w):
        w = min(mt_w, cout - m0)
        b = wT[:, m0:m0 + w].reshape(kt, 128, w).transpose(1, 0, 2)
        blocks.append(np.ascontiguousarray(b, dtype=np.float32))
    return blocks


def _host_prep(inputs):
    u = np.asarray(inputs["L0_attn_u"])
    WinT = np.ascontiguousarray(np.asarray(inputs["L0_attn_W_in"]).T)
    WupT = np.ascontiguousarray(np.asarray(inputs["L0_ffn_W_up"]).T)
    WgateT = np.ascontiguousarray(np.asarray(inputs["L0_ffn_W_gate"]).T)
    WoutT = np.ascontiguousarray(np.asarray(inputs["L0_attn_W_out"]).T)
    Wdown = np.asarray(inputs["L0_ffn_W_down"])

    win_blk = np.stack(_blk_weightT(WinT[:, :10240], 128, 16))
    win_dt_blk = _blk_weightT(WinT[:, 10240:10304], 64, 16)[0]
    wup_blk = np.stack(_blk_weightT(WupT, 128, 16))
    wgate_blk = np.stack(_blk_weightT(WgateT, 128, 16))
    wout_blk = np.stack(_blk_weightT(WoutT, 128, 32))

    qp = np.arange(128)
    vv = np.arange(64)
    cmap = (qp[None, :] >> 1) * 128 + (qp[None, :] & 1) * 64 + vv[:, None]
    WdT_perm = Wdown.T[cmap.reshape(-1), :].reshape(64, 128, 2048)
    wdown_blk = np.ascontiguousarray(
        WdT_perm.reshape(64, 128, 16, 128).transpose(2, 1, 0, 3), dtype=np.float32)

    def pack_pp(vec, ntile):
        v = np.asarray(vec, np.float32)
        return np.ascontiguousarray(v.reshape(ntile, 128).T)

    convw_pk = np.ascontiguousarray(
        np.asarray(inputs["L0_attn_conv_w"], np.float32).reshape(48, 128, 4).transpose(1, 0, 2))
    convb_pk = pack_pp(inputs["L0_attn_conv_b"], 48)
    wsc = np.asarray(inputs["L0_ffn_Wscale_ug"], np.float32)

    shared = dict(
        win_blk=win_blk, win_dt_blk=win_dt_blk, wup_blk=wup_blk,
        wgate_blk=wgate_blk, wdown_blk=wdown_blk, wout_blk=wout_blk,
        convw_pk=convw_pk, convb_pk=convb_pk,
        dtbias=np.asarray(inputs["L0_attn_dt_bias"], np.float32).reshape(64, 1),
        alog=np.asarray(inputs["L0_attn_A_log"], np.float32).reshape(64, 1),
        Drow=np.asarray(inputs["L0_attn_D"], np.float32).reshape(1, 64),
        nw_pk=pack_pp(inputs["L0_attn_norm_weight"], 32),
        suug_pk=pack_pp(inputs["L0_ffn_SU_ug"], 16),
        sudp_pk=pack_pp(inputs["L0_ffn_SU_dp"], 64),
        wscup_pk=pack_pp(wsc[:INTER], 64),
        wscgate_pk=pack_pp(wsc[INTER:], 64),
        wscdp_pk=pack_pp(inputs["L0_ffn_Wscale_dp"], 16),
    )

    in_maps = []
    for core in range(8):
        b, q = core // 4, core % 4
        start = q * TM
        u_loc = np.zeros((TL, DM), np.float32)
        lo = max(0, start - HALO)
        u_loc[HALO - (start - lo):] = u[b, lo:start + TM]
        u_pfx = np.zeros((PFX, DM), np.float32)
        plo = max(0, start - PFX)
        nreal = start - plo
        if nreal > 0:
            u_pfx[PFX - nreal:] = u[b, plo:start]
        dt_mask = np.zeros((NH, PFX), np.float32)
        if nreal > 0:
            dt_mask[:, PFX - nreal:] = 1.0
        m = dict(shared)
        m.update(u_loc=u_loc, u_pfx=u_pfx, dt_mask=dt_mask)
        in_maps.append(m)
    return in_maps


def kernel(**inputs):
    in_maps = _host_prep(inputs)
    nc, _ = build_program()
    res = run_bass_kernel_spmd(nc, in_maps, core_ids=list(range(8)))
    out = np.zeros((2, 8192, DM), np.float32)
    for core in range(8):
        b, q = core // 4, core % 4
        out[b, q * TM:(q + 1) * TM] = res.results[core]["out_loc"]
    return out


# version 18
# speedup vs baseline: 1.0868x; 1.0066x over previous
import numpy as np
import ml_dtypes
from collections import deque
from contextlib import ExitStack

import concourse.bass as bass
import concourse.mybir as mybir
from concourse.bass_utils import run_bass_kernel_spmd

F32 = mybir.dt.float32
BF16 = mybir.dt.bfloat16
AF = mybir.ActivationFunctionType
ALU = mybir.AluOpType
AXX = mybir.AxisListType.X

DM = 2048
DI = 4096
DXB = 1024
DSTATE = 128
HEADDIM = 64
NH = 64
NG = 32
CONVD = 6144
DPROJ = 10304
INTER = 8192
EPS = 1e-5

TM = 2048
HALO = 128
TL = TM + HALO
PFX = 6144
LC = 128
NCH = TM // LC
LCP = 512
NCHP = PFX // LCP
LCPB = 1024
NCHB = PFX // LCPB
PFXB_IN_PH3 = 999

ENGS = ("sync", "gpsimd", "tensor", "vector", "scalar")
ECH = {"sync": "dS", "gpsimd": "dG", "tensor": "T", "vector": "V", "scalar": "A"}


def _H(n):
    h = np.array([[1.0]], dtype=np.float64)
    while h.shape[0] < n:
        h = np.block([[h, h], [h, -h]])
    return h


class Ch:
    def __init__(self, handle):
        self.h = handle
        self.n = 0

    def inc(self, k=1):
        self.n += k
        return self.n


class Builder:

    NDMA = 8

    def __init__(self, nc, es):
        self.nc = nc
        self.q = {e: [] for e in ENGS}
        self.S = {k: Ch(es.enter_context(nc.semaphore("s_" + k)))
                  for k in ["T", "V", "A", "bar"]}
        self.dring = {
            "sync": [Ch(es.enter_context(nc.semaphore(f"s_dS{i}"))) for i in range(self.NDMA)],
            "gpsimd": [Ch(es.enter_context(nc.semaphore(f"s_dG{i}"))) for i in range(self.NDMA)],
        }
        self.dcnt = {"sync": 0, "gpsimd": 0}
        self.nbar = 0

    def op(self, eng, fn, waits=(), inc=False):
        S = self.S
        ch = S[ECH[eng]] if ECH[eng] in S else None
        auto = eng in ("vector", "scalar")
        pre = ch.n if auto else None
        tgt = ch.inc(1) if (auto or inc) else None
        wl = list(waits)

        def f(e, wl=wl, pre=pre, tgt=tgt, fn=fn, ch=ch):
            if pre:
                e.wait_ge(ch.h, pre)
            for (c, t) in wl:
                if t:
                    e.wait_ge(c.h, t)
            ins = fn(e)
            if tgt is not None:
                ins.then_inc(ch.h, 1)
        self.q[eng].append(f)
        return (ch, tgt) if tgt is not None else None

    def raw(self, eng, fn):
        self.q[eng].append(fn)

    def dma(self, eng, out, in_, waits=()):
        assert eng in ("sync", "gpsimd")
        i = self.dcnt[eng]
        self.dcnt[eng] += 1
        ch = self.dring[eng][i % self.NDMA]
        tgt = ch.inc(16)
        wl = list(waits)

        def f(e, wl=wl, tgt=tgt, ch=ch):
            for (c, t) in wl:
                if t:
                    e.wait_ge(c.h, t)
            if tgt > 16:
                e.wait_ge(ch.h, tgt - 16)
            e.dma_start(out=out, in_=in_).then_inc(ch.h, 16)
        self.q[eng].append(f)
        return (ch, tgt)

    def dma_tokens(self, eng):
        return [(c, c.n) for c in self.dring[eng] if c.n > 0]

    def barrier(self):
        S = self.S
        self.nbar += 1
        n = self.nbar
        marks = {e: S[ECH[e]].n if ECH[e] in S else None for e in ENGS}
        dS_tok = self.dma_tokens("sync")
        dG_tok = self.dma_tokens("gpsimd")

        def mk(ename):
            def f(e):
                if ename == "sync":
                    for (c, t) in dS_tok:
                        e.wait_ge(c.h, t)
                elif ename == "gpsimd":
                    for (c, t) in dG_tok:
                        e.wait_ge(c.h, t)
                elif marks[ename]:
                    e.wait_ge(S[ECH[ename]].h, marks[ename])
                e.nop().then_inc(S["bar"].h, 1)
                e.wait_ge(S["bar"].h, 5 * n)
            return f
        for e in ENGS:
            self.q[e].append(mk(e))

    def emit(self):
        with self.nc.Block() as block:
            qs = self.q

            @block.sync
            def _(e):
                for f in qs["sync"]:
                    f(e)

            @block.gpsimd
            def _(e):
                for f in qs["gpsimd"]:
                    f(e)

            @block.tensor
            def _(e):
                for f in qs["tensor"]:
                    f(e)

            @block.vector
            def _(e):
                for f in qs["vector"]:
                    f(e)

            @block.scalar
            def _(e):
                for f in qs["scalar"]:
                    f(e)


def build_program(dbg=()):
    dbg = set(dbg)
    nc = bass.Bass("TRN2", target_bir_lowering=False, debug=False)

    d_u = nc.dram_tensor("u_loc", [TL, DM], F32, kind="ExternalInput").ap()
    d_upfx = nc.dram_tensor("u_pfx", [PFX, DM], F32, kind="ExternalInput").ap()
    d_dtmask = nc.dram_tensor("dt_mask", [NH, PFX], F32, kind="ExternalInput").ap()
    d_win = nc.dram_tensor("win_blk", [80, 128, 16, 128], F32, kind="ExternalInput").ap()
    d_windt = nc.dram_tensor("win_dt_blk", [128, 16, 64], F32, kind="ExternalInput").ap()
    d_wup = nc.dram_tensor("wup_blk", [64, 128, 16, 128], F32, kind="ExternalInput").ap()
    d_wgate = nc.dram_tensor("wgate_blk", [64, 128, 16, 128], F32, kind="ExternalInput").ap()
    d_wdown = nc.dram_tensor("wdown_blk", [16, 128, 64, 128], F32, kind="ExternalInput").ap()
    d_wout = nc.dram_tensor("wout_blk", [16, 128, 32, 128], F32, kind="ExternalInput").ap()
    d_convw = nc.dram_tensor("convw_pk", [128, 48, 4], F32, kind="ExternalInput").ap()
    d_convb = nc.dram_tensor("convb_pk", [128, 48], F32, kind="ExternalInput").ap()
    d_dtbias = nc.dram_tensor("dtbias", [64, 1], F32, kind="ExternalInput").ap()
    d_alog = nc.dram_tensor("alog", [64, 1], F32, kind="ExternalInput").ap()
    d_Drow = nc.dram_tensor("Drow", [1, 64], F32, kind="ExternalInput").ap()
    d_nw = nc.dram_tensor("nw_pk", [128, 32], F32, kind="ExternalInput").ap()
    d_suug = nc.dram_tensor("suug_pk", [128, 16], F32, kind="ExternalInput").ap()
    d_sudp = nc.dram_tensor("sudp_pk", [128, 64], F32, kind="ExternalInput").ap()
    d_wscup = nc.dram_tensor("wscup_pk", [128, 64], F32, kind="ExternalInput").ap()
    d_wscgate = nc.dram_tensor("wscgate_pk", [128, 64], F32, kind="ExternalInput").ap()
    d_wscdp = nc.dram_tensor("wscdp_pk", [128, 16], F32, kind="ExternalInput").ap()
    d_out = nc.dram_tensor("out_loc", [TM, DM], F32, kind="ExternalOutput").ap()

    ident_c = nc.inline_tensor(np.eye(128, dtype=ml_dtypes.bfloat16), "c_ident").ap()
    identf_c = nc.inline_tensor(np.eye(128, dtype=np.float32), "c_identf").ap()
    h128_c = nc.inline_tensor(np.ascontiguousarray(_H(128).astype(ml_dtypes.bfloat16)), "c_h128").ap()
    m1_c = nc.inline_tensor(np.ascontiguousarray(
        np.kron(np.eye(2), _H(64)).astype(ml_dtypes.bfloat16)), "c_m1").ap()
    ones1_c = nc.inline_tensor(np.ones((1, 128), dtype=ml_dtypes.bfloat16), "c_ones1").ap()
    slneg_np = np.where(np.arange(128)[None, :] < np.arange(128)[:, None], -1e30, 0.0)
    slneg_c = nc.inline_tensor(np.ascontiguousarray(
        slneg_np.astype(ml_dtypes.bfloat16)), "c_slneg").ap()
    og_np = np.zeros((32, 128, 32), dtype=ml_dtypes.bfloat16)
    for g in range(32):
        og_np[g, :, g] = 1
    og_c = nc.inline_tensor(np.ascontiguousarray(og_np.reshape(32 * 128, 32)), "c_og").ap()
    eb_np = np.zeros((32, 32, 128), dtype=ml_dtypes.bfloat16)
    for g in range(32):
        eb_np[g, g, :] = 1
    eb_c = nc.inline_tensor(np.ascontiguousarray(eb_np.reshape(32 * 32, 128)), "c_eb").ap()

    z_scr = nc.dram_tensor("z_scr", [DI, TL], BF16, kind="Internal").ap()
    xbc_scr = nc.dram_tensor("xbc_scr", [CONVD, TM], BF16, kind="Internal").ap()
    h1_scr = nc.dram_tensor("h1_scr", [INTER, TM], BF16, kind="Internal").ap()
    y_scr = nc.dram_tensor("y_scr", [DI, TM], BF16, kind="Internal").ap()
    g_scr = nc.dram_tensor("g_scr", [DM, TM], BF16, kind="Internal").ap()
    sg_scr = nc.dram_tensor("sg_scr", [DM, TM], BF16, kind="Internal").ap()
    csm_scr = nc.dram_tensor("csm_scr", [NCH * 4, 2048], F32, kind="Internal").ap()
    dcc_scr = nc.dram_tensor("dcc_scr", [NH, 1], BF16, kind="Internal").ap()
    ecs_scr = nc.dram_tensor("ecs_scr", [NCH * 4, 2048], BF16, kind="Internal").ap()
    xbcpre_scr = nc.dram_tensor("xbcpre_scr", [2048, PFX], BF16, kind="Internal").ap()
    wpfx_scr = nc.dram_tensor("wpfx_scr", [16, 128, 16, 128], BF16, kind="Internal").ap()
    wbf_scr = nc.dram_tensor("wbf_scr", [64, PFX], BF16, kind="Internal").ap()

    dbg_outs = {}

    def dbg_tensor(name, shape, dtype=F32):
        t = nc.dram_tensor("dbg_" + name, shape, dtype, kind="ExternalOutput").ap()
        dbg_outs[name] = t
        return t

    es = ExitStack()
    with es:
        B = Builder(nc, es)
        S = B.S

        def sb(name, shape, dtype):
            return es.enter_context(nc.sbuf_tensor(name, shape, dtype))

        IDENT = sb("IDENT", [128, 128], BF16)
        IDF = sb("IDF", [128, 128], F32)
        H128 = sb("H128", [128, 128], BF16)
        M1C = sb("M1C", [128, 128], BF16)
        SLNEG = sb("SLNEG", [128, 128], BF16)
        ONES1 = sb("ONES1", [1, 128], BF16)
        CONVW = sb("CONVW", [128, 48, 4], F32)
        CONVB = sb("CONVB", [128, 48], F32)
        DTBIAS = sb("DTBIAS", [64, 1], F32)
        AH = sb("AH", [64, 1], F32)
        DROW = sb("DROW", [1, 64], BF16)
        NW = sb("NW", [128, 32], F32)
        SUUG = sb("SUUG", [128, 16], F32)
        SUDP = sb("SUDP", [128, 64], F32)
        WSCUP = sb("WSCUP", [128, 64], F32)
        WSCGATE = sb("WSCGATE", [128, 64], F32)
        WSCDP = sb("WSCDP", [128, 16], F32)
        EPSC = sb("EPSC", [128, 1], F32)
        ONEC = sb("ONEC", [128, 1], F32)
        DCBB = sb("DCBB", [128, 64], BF16)

        es_mid = ExitStack()

        def sbm(name, shape, dtype):
            return es_mid.enter_context(nc.sbuf_tensor(name, shape, dtype))

        SST = sbm("SST", [128, NH * HEADDIM], F32)

        PS = [es.enter_context(nc.psum_tensor(f"ps{i}", [128, 512], F32)) for i in range(8)]

        def ptb(i):
            return PS[i][:].bitcast(BF16)

        op, dma, barrier = B.op, B.dma, B.barrier

        for dst, src in [
            (IDENT[:], ident_c), (IDF[:], identf_c), (H128[:], h128_c), (M1C[:], m1_c),
            (ONES1[:], ones1_c), (SLNEG[:], slneg_c),
            (CONVW[:], d_convw), (CONVB[:], d_convb), (DTBIAS[:], d_dtbias),
            (AH[:], d_alog), (NW[:], d_nw),
            (SUUG[:], d_suug), (SUDP[:], d_sudp), (WSCUP[:], d_wscup),
            (WSCGATE[:], d_wscgate), (WSCDP[:], d_wscdp),
        ]:
            dma("sync", dst, src)
        wS = B.dma_tokens("sync")
        wG0 = dma("gpsimd", DROW[:], d_Drow)

        op("vector", lambda v: v.memset(EPSC[:], EPS))
        op("vector", lambda v: v.memset(ONEC[:], 1.0))
        aExp = op("scalar", lambda a: a.activation(AH[:], AH[:], AF.Exp),
                  waits=wS)
        op("vector", lambda v: v.tensor_scalar_mul(AH[:], AH[:], -1.0),
           waits=[aExp])
        barrier()

        def make_ln(tag, sba=None):
            sba = sba or sb
            U = [sba(f"lnU{p}_{tag}", [128, DM], F32) for p in range(3)]
            XT = [sba(f"lnX{p}_{tag}", [128, DM], BF16) for p in range(2)]
            SQ = sba(f"lnSQ_{tag}", [128, DM], BF16)
            ST = [sba(f"lnST{p}_{tag}", [128, 8], F32) for p in range(3)]
            st = dict(f=0, s=0, t=0)
            descs = {}
            tok = {}
            tG = {}
            vEv = {}

            def feed(src_dram, src_tok, dst_cm, dst_tile, dst_guard=None,
                     done=None):
                i = st["f"]
                st["f"] += 1
                descs[i] = (dst_cm, dst_tile, dst_guard, done)
                p = i % 3
                wl = []
                if (i - 3, "vN") in tok:
                    wl = [tok[(i - 3, "vN")], tok[(i - 3, "aSq")]]
                tok[(i, "dma")] = dma("sync", U[p][:],
                                      src_dram[src_tok:src_tok + 128, :],
                                      waits=wl)

            def stats():
                i = st["s"]
                st["s"] += 1
                p = i % 3
                px = i % 2
                tok[(i, "vSum")] = op("scalar", lambda a: a.activation(
                    SQ[:], U[p][:], AF.Copy, accum_out=ST[p][:, 0:1]),
                    waits=[tok[(i, "dma")]] +
                    ([tok[(i - 3, "vN")]] if (i - 3, "vN") in tok else []))
                tok[(i, "aSq")] = op("scalar", lambda a: a.activation(
                    SQ[:], U[p][:], AF.Square, accum_out=ST[p][:, 1:2]))
                tok[(i, "aM")] = op("scalar", lambda a: a.mul(
                    ST[p][:, 2:3], ST[p][:, 0:1], 1.0 / DM))
                op("vector", lambda v: v.tensor_tensor(
                    ST[p][:, 3:4], ST[p][:, 2:3], ST[p][:, 2:3], ALU.mult),
                    waits=[tok[(i, "aM")]])
                op("vector", lambda v: v.tensor_scalar(
                    ST[p][:, 4:5], ST[p][:, 1:2], 1.0 / DM, None, ALU.mult))
                tok[(i, "vVar")] = op("vector", lambda v: v.tensor_tensor(
                    ST[p][:, 5:6], ST[p][:, 4:5], ST[p][:, 3:4], ALU.subtract))
                op("scalar", lambda a: a.activation(
                    ST[p][:, 6:7], ST[p][:, 5:6], AF.Ln, bias=EPSC[:]),
                    waits=[tok[(i, "vVar")]])
                tok[(i, "aR")] = op("scalar", lambda a: a.activation(
                    ST[p][:, 7:8], ST[p][:, 6:7], AF.Exp, scale=-0.5))
                tok[(i, "vN")] = op("vector", lambda v: v.tensor_scalar(
                    XT[px][:], U[p][:], ST[p][:, 2:3], ST[p][:, 7:8],
                    ALU.subtract, ALU.mult),
                    waits=[tok[(i, "aR")]] +
                    ([tG[i - 2]] if (i - 2) in tG else []))

            def tp():
                i = st["t"]
                st["t"] += 1
                dst_cm, dt0, dst_guard, done = descs.pop(i)
                px = i % 2
                for jj in range(4):
                    bank = 6 + jj % 2
                    key = (i, jj - 2) if jj >= 2 else (i - 1, jj + 2)
                    wl = [tok[(i, "vN")]]
                    if key in vEv:
                        wl.append(vEv[key])
                    if dst_guard is not None:
                        wl.append(dst_guard)

                    def tfn(t, jj=jj, px=px, bank=bank):
                        ins = None
                        for j in range(4):
                            ins = t.transpose(
                                ptb(bank)[:, j * 128:(j + 1) * 128],
                                XT[px][:, (jj * 4 + j) * 128:(jj * 4 + j + 1) * 128],
                                IDENT[:])
                        return ins
                    tG[i] = op("tensor", tfn, waits=wl, inc=True)
                    vEv[(i, jj)] = op("vector", lambda v, jj=jj, dt0=dt0, bank=bank: v.tensor_copy(
                        dst_cm[:, jj * 4:jj * 4 + 4, dt0 * 128:(dt0 + 1) * 128],
                        ptb(bank).rearrange("p (j q) -> p j q", j=8)[:, 0:4, :]),
                        waits=[tG[i]])
                last = vEv[(i, 3)]
                if done is not None:
                    done(last)
                return last
            return st, feed, stats, tp

        def emit_softplus(dst, srcv, tmp1, tmp2, mask_view=None, waits=()):
            t1 = op("vector", lambda v: v.tensor_scalar(
                tmp1, srcv, DTBIAS[:], None, ALU.add), waits=list(waits))
            a1 = op("scalar", lambda a: a.activation(tmp2, tmp1, AF.Abs), waits=[t1])
            op("scalar", lambda a: a.activation(tmp2, tmp2, AF.Exp, scale=-1.0))
            t2 = op("scalar", lambda a: a.activation(tmp2, tmp2, AF.Ln, bias=ONEC[0:64, :]))
            op("vector", lambda v: v.tensor_scalar(tmp1, tmp1, 0.0, None, ALU.max),
               waits=[a1])
            if mask_view is None:
                aw = op("vector", lambda v: v.tensor_tensor(dst, tmp1, tmp2, ALU.add),
                        waits=[t2])
            else:
                op("vector", lambda v: v.tensor_tensor(tmp1, tmp1, tmp2, ALU.add),
                   waits=[t2])
                aw = op("vector", lambda v: v.tensor_tensor(dst, tmp1, mask_view, ALU.mult))
            return aw

        DTM = sbm("DTM", [64, TM], F32)
        es_a = ExitStack()

        def sba(name, shape, dtype):
            return es_a.enter_context(nc.sbuf_tensor(name, shape, dtype))

        XN = sba("XN", [128, 16, TL], BF16)
        X1 = XN[:, :, HALO:HALO + TM].rearrange("p a b -> p a b")
        WPDT = sba("WPDT", [128, 16, 64], BF16)
        DCBA = sba("DCBA", [128, NCHB, 64], F32)

        with ExitStack() as esp:
            def sbp(name, shape, dtype):
                return esp.enter_context(nc.sbuf_tensor(name, shape, dtype))

            lnst, ln_feed, ln_stats, ln_tp = make_ln("c", sbp)
            XNP = [sbp(f"XNP{i}", [128, 16, LCP], BF16) for i in range(2)]
            WPF = [sbp(f"WPF{i}", [128, 16, 128], BF16) for i in range(3)]
            ZEVP = [sbp(f"ZEVP{i}", [128, 512], BF16) for i in range(2)]
            DTP = sbp("DTP", [64, LCPB], F32)
            DAPC = sbp("DAPC", [64, LCPB], F32)
            CSP = sbp("CSP", [64, LCPB], F32)
            DSW = sbp("DSW", [64, LCPB], F32)
            DTMK = sbp("DTMK", [64, LCPB], BF16)
            ZER64 = sbp("ZER64", [64, LCPB], BF16)
            WBFP = sbp("WBFP", [64, LCPB], BF16)
            DCC = sbp("DCC", [64, 1], BF16)

            wPD = dma("gpsimd", WPDT[:], d_windt)
            op("vector", lambda v: v.memset(SST[:], 0.0))
            op("vector", lambda v: v.memset(ZER64[:], 0.0))

            bank_tok = {i: None for i in range(8)}
            abk = [0]
            wpf_free = {0: [], 1: [], 2: []}
            wb_tok = {}
            zevp_free = {0: None, 1: None}
            ln_guard = {0: None, 1: None}
            dtp_guard = [None]
            dtp_evac = {}
            aDt_prev = [None]
            dcc_prev = [None]
            dcbb_prev = [None]
            wbf_free = [None]
            lnv = {}

            tiles_pri = deque()
            tiles_main = deque()
            main_pos = [0]

            def push_lnc(c):
                pc = c % 2
                nt = LCP // 128
                for tix in range(nt):
                    def f(c=c, pc=pc, tix=tix, last=(tix == nt - 1)):
                        def done(tokl, c=c):
                            lnv[c] = tokl
                        ln_feed(d_upfx, c * LCP + tix * 128, XNP[pc], tix,
                                dst_guard=ln_guard[pc],
                                done=done if last else None)
                    tiles_pri.append(f)

            def push_main():
                def f():
                    ln_feed(d_u, main_pos[0] * 128, XN, main_pos[0])
                    main_pos[0] += 1
                tiles_main.append(f)

            def feed_one():
                if tiles_pri:
                    tiles_pri.popleft()()
                    return True
                if tiles_main:
                    tiles_main.popleft()()
                    return True
                return False

            def drain_one():
                if lnst["t"] < lnst["s"] - 1:
                    ln_tp()
                    return True
                if lnst["s"] < lnst["f"]:
                    ln_stats()
                    return True
                if feed_one():
                    return True
                if lnst["t"] < lnst["s"]:
                    ln_tp()
                    return True
                return False

            def pump():
                if lnst["t"] < lnst["s"] - 1:
                    ln_tp()
                if lnst["s"] < lnst["f"]:
                    ln_stats()
                feed_one()

            def emit_mms(c):
                pc = c % 2
                vLN = lnv[c]
                last_mm = None
                for idx, mi in enumerate([16] + list(range(16))):
                    if mi < 16:
                        wp = mi % 3
                        if c == 0:
                            wW = [dma("gpsimd", WPF[wp][:], d_win[32 + mi],
                                      waits=wpf_free[wp])]
                            wb_tok[mi] = dma("sync", wpfx_scr[mi], WPF[wp][:],
                                             waits=wW)
                        else:
                            wl = list(wpf_free[wp])
                            if c == 1:
                                wl.append(wb_tok[mi])
                            wW = [dma("sync", WPF[wp][:, 0:8, :],
                                      wpfx_scr[mi][:, 0:8, :], waits=wl),
                                  dma("sync", WPF[wp][:, 8:16, :],
                                      wpfx_scr[mi][:, 8:16, :], waits=wl)]
                    else:
                        wW = [wPD]
                    bank = abk[0] % 4
                    abk[0] += 1
                    wl = wW + [vLN]
                    if bank_tok[bank] is not None:
                        wl.append(bank_tok[bank])

                    def mmf(t, mi=mi, bank=bank, pc=pc):
                        ins = None
                        for k in range(16):
                            lhs = WPF[mi % 3][:, k, :] if mi < 16 else WPDT[:, k, :]
                            mrows = 128 if mi < 16 else 64
                            ins = t.matmul(PS[bank][0:mrows, :], lhs,
                                           XNP[pc][:, k, :],
                                           start=(k == 0), stop=(k == 15))
                        return ins
                    tMM = op("tensor", mmf, waits=wl, inc=True)
                    if mi < 16:
                        wpf_free[mi % 3] = [tMM]
                        if c == 0:
                            wpf_free[mi % 3].append(wb_tok[mi])
                        zp = mi % 2
                        wl2 = [tMM]
                        if zevp_free[zp] is not None:
                            wl2.append(zevp_free[zp])
                        aE = op("scalar", lambda a, zp=zp, bank=bank: a.copy(
                            ZEVP[zp][:], PS[bank][:, :]), waits=wl2)
                        bank_tok[bank] = aE
                        zevp_free[zp] = dma(
                            "sync" if c == 0 else "gpsimd",
                            xbcpre_scr[mi * 128:(mi + 1) * 128,
                                       c * LCP:(c + 1) * LCP],
                            ZEVP[zp][:], waits=[aE])
                    else:
                        ho = (c % 2) * 512
                        wl2 = [tMM]
                        if dtp_guard[0] is not None:
                            wl2.append(dtp_guard[0])
                        aD = op("scalar", lambda a, ho=ho, bank=bank: a.copy(
                            DTP[:, ho:ho + 512], PS[bank][0:64, :]), waits=wl2)
                        bank_tok[bank] = aD
                        dtp_evac[c] = aD
                    last_mm = tMM
                    if idx in (2, 5, 8, 11, 14, 16):
                        pump()
                ln_guard[pc] = last_mm

            def emit_dt(cp):
                wMK = dma("gpsimd", DTMK[:],
                          d_dtmask[:, cp * LCPB:(cp + 1) * LCPB],
                          waits=[aDt_prev[0]] if aDt_prev[0] else [])
                aDt = emit_softplus(
                    DTP[:], DTP[:], DAPC[:], CSP[:], mask_view=DTMK[:],
                    waits=[dtp_evac[2 * cp], dtp_evac[2 * cp + 1], wMK])
                aDt_prev[0] = aDt
                op("vector", lambda v: v.tensor_scalar(
                    DAPC[:], DTP[:], AH[:], None, ALU.mult))
                vCS = op("vector", lambda v: v.tensor_tensor_scan(
                    CSP[:], DAPC[:], ZER64[:], 0.0, ALU.add, ALU.add))
                vD1 = op("vector", lambda v: v.tensor_tensor(
                    DSW[:], CSP[:, LCPB - 1:LCPB].broadcast_to([64, LCPB]),
                    CSP[:], ALU.subtract))
                aD2 = op("scalar", lambda a: a.activation(DSW[:], DSW[:], AF.Exp),
                         waits=[vD1])
                vD3 = op("vector", lambda v: v.tensor_tensor(
                    DSW[:], DSW[:], DTP[:], ALU.mult), waits=[aD2])
                dtp_guard[0] = vD3
                vWBF = op("vector", lambda v: v.tensor_copy(WBFP[:], DSW[:]),
                          waits=[wbf_free[0]] if wbf_free[0] else [])
                wbf_free[0] = dma("gpsimd", wbf_scr[:, cp * LCPB:(cp + 1) * LCPB],
                                  WBFP[:], waits=[vWBF])
                aDC = op("scalar", lambda a: a.activation(
                    DCC[:], CSP[:, LCPB - 1:LCPB], AF.Exp),
                    waits=[vCS] + ([dcc_prev[0]] if dcc_prev[0] else []))
                wDC0 = dma("gpsimd", dcc_scr, DCC[:], waits=[aDC])
                dcc_prev[0] = wDC0
                wl = [wDC0]
                if dcbb_prev[0] is not None:
                    wl.append(dcbb_prev[0])
                wDB = dma("gpsimd", DCBB[:],
                          dcc_scr.rearrange("a b -> b a").broadcast_to([128, 64]),
                          waits=wl)
                dcbb_prev[0] = op("vector", lambda v, cp=cp: v.tensor_copy(
                    DCBA[:, cp, :], DCBB[:]), waits=[wDB])

            push_lnc(0)
            push_lnc(1)
            for _ in range(TL // 128):
                push_main()
            for _ in range(3):
                feed_one()
            for c in range(NCHP):
                while lnv.get(c) is None:
                    if not drain_one():
                        break
                emit_mms(c)
                if c % 2 == 1:
                    emit_dt(c // 2)
                if c + 2 < NCHP:
                    push_lnc(c + 2)
            while (tiles_pri or tiles_main or lnst["s"] < lnst["f"]
                   or lnst["t"] < lnst["s"]):
                if not drain_one():
                    break
            barrier()

        if "xn" in dbg:
            t = dbg_tensor("xn", [128, 16 * TL], BF16)
            dma("sync", t.rearrange("p (a b) -> p a b", a=16), XN[:])
            barrier()
        if "pfxa" in dbg:
            t = dbg_tensor("xbp", [2048, PFX], BF16)
            dma("sync", t, xbcpre_scr)
            t2 = dbg_tensor("wbfd", [64, PFX], BF16)
            dma("sync", t2, wbf_scr)
            t3 = dbg_tensor("dcba", [128, NCHB * 64])
            dma("sync", t3.rearrange("p (a b) -> p a b", a=NCHB), DCBA[:])
            barrier()

        es_b2 = ExitStack()

        def sbb2(name, shape, dtype):
            return es_b2.enter_context(nc.sbuf_tensor(name, shape, dtype))

        XBPRE = sbb2("XBPRE", [128, 16, LCPB + 4], BF16)
        ACCP0 = sbb2("ACCP0", [128, LCPB], F32)
        ACCP = [ACCP0, ACCP0]
        XTC8 = sbb2("XTC8", [128, LCPB // 128, 2, 128], BF16)
        XWG = sbb2("XWG", [128, LCPB // 128, 512], BF16)
        WTQ = sbb2("WTQ", [128, LCPB // 128, 64], BF16)
        WBF = sbb2("WBF", [64, LCPB], BF16)
        pbank = {i: None for i in range(8)}

        def pfxb_gen():
            vZB = op("vector", lambda v: v.memset(XBPRE[:, :, 0:4], 0.0))
            ld_guard = [vZB]
            wbf_guard = [None]
            wtq_tok = {i: None for i in range(LCPB // 128)}
            st_upd = {gb: None for gb in range(8)}
            accp_g = {0: None, 1: None}
            for cb in range(NCHB):
                ldt = []
                wl0 = [t for t in ld_guard if t is not None]
                for mi in range(16):
                    if cb == 0:
                        dst = XBPRE[:, mi, 4:4 + LCPB]
                        src = xbcpre_scr[mi * 128:(mi + 1) * 128, 0:LCPB]
                    else:
                        dst = XBPRE[:, mi, 1:4 + LCPB]
                        src = xbcpre_scr[mi * 128:(mi + 1) * 128,
                                         cb * LCPB - 3:(cb + 1) * LCPB]
                    ldt.append(dma("sync", dst, src,
                                   waits=wl0 if mi == 0 else []))
                wWBF = dma("sync", WBF[:],
                           wbf_scr[:, cb * LCPB:(cb + 1) * LCPB],
                           waits=[wbf_guard[0]] if wbf_guard[0] else [])
                yield
                vConvDone = []
                vCl = None
                for mi in range(16):
                    ap = mi % 2
                    wl = [ldt[mi]] + [t for t in accp_g.values() if t]
                    if cb == 0:
                        wl.append(vZB)
                    op("vector", lambda v, mi=mi, ap=ap: v.tensor_scalar(
                        ACCP[ap][:], XBPRE[:, mi, 1:1 + LCPB], CONVW[:, mi, 0:1],
                        None, ALU.mult), waits=wl)
                    for k in range(1, 4):
                        vCl = op("vector", lambda v, mi=mi, k=k, ap=ap: v.scalar_tensor_tensor(
                            ACCP[ap][:], XBPRE[:, mi, 1 + k:1 + k + LCPB],
                            CONVW[:, mi, k:k + 1], ACCP[ap][:], ALU.mult, ALU.add))
                    vConvDone.append(op("scalar", lambda a, mi=mi, ap=ap: a.activation(
                        XBPRE[:, mi, 4:4 + LCPB], ACCP[ap][:], AF.Silu,
                        bias=CONVB[:, mi:mi + 1]), waits=[vCl]))
                    accp_g[ap] = vConvDone[-1]
                    if mi % 4 == 3:
                        yield
                yield
                for blk in range(LCPB // 128):
                    tb = 6 + blk % 2
                    wl2 = [wWBF]
                    if wtq_tok[blk] is not None:
                        wl2.append(wtq_tok[blk])
                    if pbank[tb] is not None:
                        wl2.append(pbank[tb])
                    tW = op("tensor", lambda t, blk=blk, tb=tb: t.transpose(
                        ptb(tb)[:, 256:320],
                        WBF[:, blk * 128:(blk + 1) * 128],
                        IDENT[0:64, 0:64]), waits=wl2, inc=True)
                    wtq_tok[blk] = op("vector", lambda v, blk=blk, tb=tb: v.tensor_copy(
                        WTQ[:, blk, :], ptb(tb)[:, 256:320]), waits=[tW])
                    pbank[tb] = wtq_tok[blk]
                yield
                last_st = None
                for gb in range(8):
                    evs = []
                    for blk in range(LCPB // 128):
                        tb = 6 + blk % 2
                        wl = [vConvDone[gb], vConvDone[8 + gb]]
                        if pbank[tb] is not None:
                            wl.append(pbank[tb])

                        def tfn(t, gb=gb, blk=blk, tb=tb):
                            t.transpose(ptb(tb)[:, 0:128],
                                        XBPRE[:, gb, 4 + blk * 128:4 + (blk + 1) * 128],
                                        IDENT[:])
                            return t.transpose(
                                ptb(tb)[:, 128:256],
                                XBPRE[:, 8 + gb, 4 + blk * 128:4 + (blk + 1) * 128],
                                IDENT[:])
                        tT = op("tensor", tfn, waits=wl, inc=True)
                        pbank[tb] = op("vector", lambda v, blk=blk, tb=tb: v.tensor_copy(
                            XTC8[:, blk, :, :],
                            ptb(tb).rearrange("p (a q) -> p a q", a=8)[:, 0:2, :]),
                            waits=[tT])
                        evs.append(pbank[tb])
                    xw_toks = []
                    for blk in range(LCPB // 128):
                        xw_toks.append(op("vector", lambda v, gb=gb, blk=blk: v.tensor_tensor(
                            XWG[:, blk, :].rearrange("p (a b q) -> p a b q", a=4, b=2),
                            XTC8[:, blk, 0, :].rearrange("p (a q) -> p a q", a=2)
                            .unsqueeze(1).broadcast_to([128, 4, 2, 64]),
                            WTQ[:, blk, gb * 8:gb * 8 + 8]
                            .rearrange("p (a b) -> p a b", a=4)
                            .unsqueeze(3).broadcast_to([128, 4, 2, 64]),
                            ALU.mult), waits=[wtq_tok[blk], evs[blk]]))
                    bank = 4 + gb % 2
                    wl4 = [xw_toks[-1]]
                    if pbank[bank] is not None:
                        wl4.append(pbank[bank])

                    def mms(t, gb=gb, bank=bank):
                        ins = None
                        nbl = LCPB // 128
                        for blk in range(nbl):
                            ins = t.matmul(PS[bank][:, :], XTC8[:, blk, 1, :],
                                           XWG[:, blk, :],
                                           start=(blk == 0), stop=(blk == nbl - 1))
                        return ins
                    tSt = op("tensor", mms, waits=wl4, inc=True)
                    last_st = tSt
                    wl5 = [tSt]
                    if st_upd[gb] is not None:
                        wl5.append(st_upd[gb])
                    op("vector", lambda v, gb=gb, cb=cb: v.tensor_tensor(
                        SST[:, gb * 512:(gb + 1) * 512].rearrange("p (a q) -> p a q", a=8),
                        SST[:, gb * 512:(gb + 1) * 512].rearrange("p (a q) -> p a q", a=8),
                        DCBA[:, cb, gb * 8:gb * 8 + 8].unsqueeze(2).broadcast_to([128, 8, 64]),
                        ALU.mult), waits=wl5)
                    st_upd[gb] = op("vector", lambda v, gb=gb, bank=bank: v.tensor_tensor(
                        SST[:, gb * 512:(gb + 1) * 512],
                        SST[:, gb * 512:(gb + 1) * 512],
                        PS[bank][:, :], ALU.add))
                    pbank[bank] = st_upd[gb]
                    if gb % 2 == 1:
                        yield
                yield
                ld_guard[:] = [last_st, vCl]
                wbf_guard[0] = wtq_tok[LCPB // 128 - 1]

        gen = pfxb_gen()
        bpump = [0]

        with ExitStack() as es3:
            def sb3(name, shape, dtype):
                return es3.enter_context(nc.sbuf_tensor(name, shape, dtype))

            WB = [sb3(f"WB{i}", [128, 16, 128], BF16) for i in range(3)]
            ZEV = [sb3(f"ZEV{i}", [128, 512], BF16) for i in range(2)]
            XBR = [sb3(f"XBR{i}", [128, TL], BF16) for i in range(2)]
            ACC3 = [sb3(f"ACC3{i}", [128, TM], F32) for i in range(2)]
            CVO = [sb3(f"CVO{i}", [128, TM], BF16) for i in range(2)]
            DTRW = sb3("DTRW", [64, TL], F32)

            bank_tok = {i: None for i in range(8)}

            strips5 = [(0, 512), (512, 512), (1024, 512), (1536, 512), (2048, 128)]
            w_free = {0: None, 1: None, 2: None}
            zev_free = {0: None, 1: None}
            cvo_free = {0: None, 1: None}
            xbr_free = {0: None, 1: None}
            acc3_prev = {0: None, 1: None}
            dt_evs = []
            for m in range(81):
                if m < 80:
                    wl = [w_free[m % 3]] if w_free[m % 3] else []
                    wW = [dma("gpsimd", WB[m % 3][:, 0:8, :],
                              d_win[m][:, 0:8, :], waits=wl),
                          dma("gpsimd", WB[m % 3][:, 8:16, :],
                              d_win[m][:, 8:16, :], waits=wl)]
                else:
                    wW = [wPD]
                mm_toks = []
                ev_toks = []
                for si, (o, w) in enumerate(strips5):
                    bank = (m * 5 + si) % 4
                    wl = list(wW)
                    if bank_tok[bank] is not None:
                        wl.append(bank_tok[bank])
                    if m >= 32 and m < 80 and si == 0 and xbr_free[m % 2] is not None:
                        wl.append(xbr_free[m % 2])

                    def mmf(t, m=m, o=o, w=w, bank=bank):
                        ins = None
                        for k in range(16):
                            lhs = WB[m % 3][:, k, :] if m < 80 else WPDT[:, k, :]
                            mrows = 128 if m < 80 else 64
                            ins = t.matmul(PS[bank][0:mrows, 0:w], lhs,
                                           XN[:, k, o:o + w],
                                           start=(k == 0), stop=(k == 15))
                        return ins
                    tMM = op("tensor", mmf, waits=wl, inc=True)
                    mm_toks.append(tMM)
                    if m < 32:
                        zp = si % 2
                        wl2 = [tMM]
                        if zev_free[zp] is not None:
                            wl2.append(zev_free[zp])
                        aZ = op("scalar", lambda a, o=o, w=w, bank=bank, zp=zp: a.copy(
                            ZEV[zp][:, 0:w], PS[bank][:, 0:w]), waits=wl2)
                        bank_tok[bank] = aZ
                        zev_free[zp] = dma(
                            "sync", z_scr[m * 128:(m + 1) * 128, o:o + w],
                            ZEV[zp][:, 0:w], waits=[aZ])
                    elif m < 80:
                        aX = op("scalar", lambda a, m=m, o=o, w=w, bank=bank: a.copy(
                            XBR[m % 2][:, o:o + w], PS[bank][:, 0:w]), waits=[tMM])
                        bank_tok[bank] = aX
                        ev_toks.append(aX)
                    else:
                        aD = op("scalar", lambda a, o=o, w=w, bank=bank: a.copy(
                            DTRW[:, o:o + w], PS[bank][0:64, 0:w]), waits=[tMM])
                        bank_tok[bank] = aD
                        dt_evs.append(aD)
                w_free[m % 3] = mm_toks[-1]
                if 32 <= m < 80:
                    mi = m - 32
                    wl = list(ev_toks)
                    if cvo_free[m % 2] is not None:
                        wl.append(cvo_free[m % 2])
                    if acc3_prev[m % 2] is not None:
                        wl.append(acc3_prev[m % 2])
                    op("vector", lambda v, m=m, mi=mi: v.tensor_scalar(
                        ACC3[m % 2][:], XBR[m % 2][:, HALO - 3:HALO - 3 + TM],
                        CONVW[:, mi, 0:1], None, ALU.mult), waits=wl)
                    vC4 = None
                    for k in range(1, 4):
                        vC4 = op("vector", lambda v, m=m, mi=mi, k=k: v.scalar_tensor_tensor(
                            ACC3[m % 2][:], XBR[m % 2][:, HALO - 3 + k:HALO - 3 + k + TM],
                            CONVW[:, mi, k:k + 1], ACC3[m % 2][:], ALU.mult, ALU.add))
                    xbr_free[m % 2] = vC4
                    aCV = op("scalar", lambda a, m=m, mi=mi: a.activation(
                        CVO[m % 2][:, :], ACC3[m % 2][:], AF.Silu, bias=CONVB[:, mi:mi + 1]),
                        waits=[vC4])
                    acc3_prev[m % 2] = aCV
                    cvo_free[m % 2] = dma(
                        "sync", xbc_scr[mi * 128:(mi + 1) * 128, :], CVO[m % 2][:],
                        waits=[aCV])
                if bpump[0] < PFXB_IN_PH3:
                    next(gen, None)
                    bpump[0] += 1
            if PFXB_IN_PH3 > 100:
                for _ in gen:
                    pass
            op("vector", lambda v: v.tensor_copy(ACC3[0][0:64, :], DTRW[:, HALO:]),
               waits=dt_evs + [t for t in acc3_prev.values() if t])
            emit_softplus(DTM[:], ACC3[0][0:64, :], ACC3[0][0:64, :], DTRW[:, 0:TM])
            barrier()

        if "sin" in dbg:
            t = dbg_tensor("sin", [128, NH * HEADDIM])
            dma("sync", t, SST[:])
            barrier()

        if "xbc" in dbg:
            t = dbg_tensor("xbc", [CONVD, TM], BF16)
            dma("sync", t, xbc_scr)
            t2 = dbg_tensor("zd", [DI, TL], BF16)
            dma("sync", t2, z_scr)
            t3 = dbg_tensor("dtm", [64, TM])
            dma("sync", t3, DTM[:])
            barrier()

        with ExitStack() as es4:
            def sb4(name, shape, dtype):
                return es4.enter_context(nc.sbuf_tensor(name, shape, dtype))

            HSG = [sb4(f"HSG{i}", [128, 128], BF16) for i in range(2)]
            BTMP = sb4("BTMP", [128, TM], BF16)
            bank_tok = {i: None for i in range(8)}
            hs_free = {0: None, 1: None}
            SC2048 = 1.0 / float(np.sqrt(2048.0))
            for g in range(16):
                wl = [hs_free[g % 2]] if hs_free[g % 2] else []
                vH = op("vector", lambda v, g=g: v.tensor_scalar(
                    HSG[g % 2][:], H128[:], SUUG[:, g:g + 1], None, ALU.mult),
                    waits=wl)
                for s4 in range(4):
                    bank = (g * 4 + s4) % 4
                    wl = [vH]
                    if bank_tok[bank] is not None:
                        wl.append(bank_tok[bank])
                    tMM = op("tensor", lambda t, g=g, s4=s4, bank=bank: t.matmul(
                        PS[bank][:, :], HSG[g % 2][:],
                        XN[:, g, HALO + s4 * 512:HALO + (s4 + 1) * 512],
                        start=True, stop=True), waits=wl, inc=True)
                    if s4 == 3:
                        hs_free[g % 2] = tMM
                    bank_tok[bank] = op("scalar", lambda a, g=g, s4=s4, bank=bank: a.activation(
                        X1[:, g, s4 * 512:(s4 + 1) * 512], PS[bank][:, :],
                        AF.Copy, scale=SC2048), waits=[tMM])
            aH128 = (S["A"], S["A"].n)
            prev = aH128
            for st4 in range(4):
                d = 1 << st4
                for g0 in range(16):
                    if g0 & d:
                        continue
                    g1 = g0 + d
                    op("vector", lambda v, g0=g0: v.tensor_copy(
                        BTMP[:], X1[:, g0, :]), waits=[prev])
                    op("vector", lambda v, g0=g0, g1=g1: v.tensor_tensor(
                        X1[:, g0, :], BTMP[:], X1[:, g1, :], ALU.add))
                    op("vector", lambda v, g0=g0, g1=g1: v.tensor_tensor(
                        X1[:, g1, :], BTMP[:], X1[:, g1, :], ALU.subtract))
                prev = (S["V"], S["V"].n)
            barrier()

        if "x1" in dbg:
            t = dbg_tensor("x1", [128, 16 * TM], BF16)
            dma("sync", t.rearrange("p (a b) -> p a b", a=16), X1[:])
            barrier()

        with ExitStack() as es5:
            def sb5(name, shape, dtype):
                return es5.enter_context(nc.sbuf_tensor(name, shape, dtype))

            WUP = [sb5(f"WUP{i}", [128, 16, 128], BF16) for i in range(2)]
            WGT = [sb5(f"WGT{i}", [128, 16, 128], BF16) for i in range(2)]
            UPT = [sb5(f"UPT{i}", [128, TM], BF16) for i in range(2)]
            GTT = [sb5(f"GTT{i}", [128, TM], BF16) for i in range(2)]
            HPB0 = sb5("HPB0", [128, TM], BF16)
            HPB = [HPB0, HPB0]
            H1T = [sb5(f"H1T{i}", [128, TM], BF16) for i in range(2)]
            bank_tok = {i: None for i in range(8)}
            w_free = {0: None, 1: None}
            buf_free = {0: None, 1: None}
            SC81 = 0.125
            for i in range(64):
                p = i % 2
                wl = [w_free[p]] if w_free[p] else []
                wU = dma("gpsimd", WUP[p][:], d_wup[i], waits=wl)
                wG = dma("gpsimd", WGT[p][:], d_wgate[i], waits=wl)
                st_toks = []

                def evac(s4, tU, tG, i=i, p=p):
                    bU, bG = s4 % 2, 2 + s4 % 2
                    aU = op("vector", lambda v, i=i, p=p, s4=s4, bU=bU: v.tensor_scalar(
                        UPT[p][:, s4 * 512:(s4 + 1) * 512], PS[bU][:, :],
                        WSCUP[:, i:i + 1], None, ALU.mult), waits=[tU])
                    aG = op("scalar", lambda a, i=i, p=p, s4=s4, bG=bG: a.activation(
                        GTT[p][:, s4 * 512:(s4 + 1) * 512], PS[bG][:, :],
                        AF.Silu, scale=WSCGATE[:, i:i + 1]), waits=[tG])
                    bank_tok[bG] = aG
                    vHP = op("vector", lambda v, i=i, p=p, s4=s4: v.scalar_tensor_tensor(
                        HPB[p][:, s4 * 512:(s4 + 1) * 512],
                        GTT[p][:, s4 * 512:(s4 + 1) * 512], SUDP[:, i:i + 1],
                        UPT[p][:, s4 * 512:(s4 + 1) * 512], ALU.mult, ALU.mult),
                        waits=[aU, aG])
                    tS1 = op("tensor", lambda t, p=p, s4=s4, bU=bU: t.matmul(
                        PS[bU][:, :], M1C[:],
                        HPB[p][:, s4 * 512:(s4 + 1) * 512],
                        start=True, stop=True), waits=[vHP, aU], inc=True)
                    aH1 = op("scalar", lambda a, p=p, s4=s4, bU=bU: a.activation(
                        H1T[p][:, s4 * 512:(s4 + 1) * 512], PS[bU][:, :],
                        AF.Copy, scale=SC81), waits=[tS1])
                    bank_tok[bU] = aH1
                    st_toks.append(aH1)

                pend = None
                last_g = None
                for s4 in range(4):
                    bU, bG = s4 % 2, 2 + s4 % 2
                    wl = [wU]
                    if bank_tok[bU] is not None:
                        wl.append(bank_tok[bU])
                    if s4 == 0 and buf_free[p] is not None:
                        wl.append(buf_free[p])

                    def mmu(t, i=i, p=p, s4=s4, bU=bU):
                        ins = None
                        for k in range(16):
                            ins = t.matmul(PS[bU][:, :], WUP[p][:, k, :],
                                           X1[:, k, s4 * 512:(s4 + 1) * 512],
                                           start=(k == 0), stop=(k == 15))
                        return ins
                    tU = op("tensor", mmu, waits=wl, inc=True)
                    wl = [wG]
                    if bank_tok[bG] is not None:
                        wl.append(bank_tok[bG])

                    def mmg(t, i=i, p=p, s4=s4, bG=bG):
                        ins = None
                        for k in range(16):
                            ins = t.matmul(PS[bG][:, :], WGT[p][:, k, :],
                                           X1[:, k, s4 * 512:(s4 + 1) * 512],
                                           start=(k == 0), stop=(k == 15))
                        return ins
                    tG = op("tensor", mmg, waits=wl, inc=True)
                    last_g = tG
                    if pend is not None:
                        pend()
                    pend = (lambda s4=s4, tU=tU, tG=tG: evac(s4, tU, tG))
                pend()
                w_free[p] = last_g
                buf_free[p] = dma("sync", h1_scr[i * 128:(i + 1) * 128, :],
                                  H1T[p][:], waits=[st_toks[-1]])
                next(gen, None)
            for _ in gen:
                pass
            barrier()
        es_b2.close()
        es_a.close()

        if "h1" in dbg:
            t = dbg_tensor("h1", [INTER, TM], BF16)
            dma("sync", t, h1_scr)
            barrier()

        with ExitStack() as es6:
            def sb6(name, shape, dtype):
                return es6.enter_context(nc.sbuf_tensor(name, shape, dtype))

            CSD = sb6("CSD", [64, TM], F32)
            ECSR = sb6("ECSR", [64, TM], BF16)
            WMB = sb6("WMB", [64, TM], BF16)
            CTMP = sb6("CTMP", [64, LC], F32)
            CTM2 = sb6("CTM2", [64, LC], F32)
            DCALL = sb6("DCALL", [64, NCH], BF16)
            Z64 = sb6("Z64", [64, LC], F32)
            XSL = [sb6(f"XSL{i}", [128, 8, 256], BF16) for i in range(2)]
            BSL = [sb6(f"BSL{i}", [128, 8, 256], BF16) for i in range(2)]
            CCB = sb6("CCB", [128, 32, 256], BF16)
            XTT = sb6("XTT", [128, 8, 128], BF16)
            BTT = sb6("BTT", [128, 8, 128], BF16)
            WTT = sb6("WTT", [128, 64], BF16)
            CST = sb6("CST", [128, 64], F32)
            CSREP0 = sb6("CSREP0", [128, 2048], F32)
            CSREP = [CSREP0, CSREP0]
            ECSREP = [sb6(f"ECSREP{i}", [128, 2048], BF16) for i in range(2)]
            T1B0 = sb6("T1B0", [128, 2048], F32)
            T1B = [T1B0, T1B0]
            LTB = [sb6(f"LTB{i}", [128, 2048], BF16) for i in range(2)]
            GTS = [sb6(f"GTS{i}", [128, 1024], BF16) for i in range(2)]
            GLB = [sb6(f"GLB{i}", [128, 2048], BF16) for i in range(2)]
            CSBQ = [sb6(f"CSBQ{i}", [128, 2048], BF16) for i in range(2)]
            XWS = [sb6(f"XWS{i}", [128, 4, 2, 64], BF16) for i in range(2)]
            SSB = sb6("SSB", [128, NH * HEADDIM], BF16)
            Y4 = sb6("Y4", [128, 32, 256], BF16)
            DCC6 = sb6("DCC6", [64, 1], BF16)
            DCB6 = sb6("DCB6", [128, 64], F32)
            DIALL = sb6("DIALL", [128, NH * 128], BF16)
            DBRD = sb6("DBRD", [128, 64], BF16)

            tD = op("tensor", lambda t: t.matmul(PS[0][:, 0:64], ONES1[:], DROW[:],
                                                 start=True, stop=True),
                    waits=[wG0], inc=True)
            vDB = op("vector", lambda v: v.tensor_copy(DBRD[:], PS[0][:, 0:64]),
                     waits=[tD])
            op("vector", lambda v: v.tensor_tensor(
                DIALL[:].rearrange("p (h l) -> p h l", h=64),
                IDENT[:].unsqueeze(1).broadcast_to([128, 64, 128]),
                DBRD[:].unsqueeze(2).broadcast_to([128, 64, 128]),
                ALU.mult))

            vZ = op("vector", lambda v: v.memset(Z64[:], 0.0))
            aLDT = op("scalar", lambda a: a.activation(CSD[:], DTM[:], AF.Ln))
            wCS = None
            cs_spill = {}
            ecs_spill = {}
            for c in range(NCH):
                slc = slice(c * LC, (c + 1) * LC)
                op("vector", lambda v, slc=slc: v.tensor_scalar(
                    CTM2[:], DTM[:, slc], AH[:], None, ALU.mult))
                vC = op("vector", lambda v: v.tensor_tensor_scan(
                    CTMP[:], CTM2[:], Z64[:], 0.0, ALU.add, ALU.add),
                    waits=[vZ] + ([wCS] if wCS else []))
                wCS = dma("sync",
                          csm_scr[c * 4:c * 4 + 4, :].rearrange("q (h l) -> (q h) l", h=16),
                          CTMP[:], waits=[vC])
                cs_spill[c] = wCS
                op("scalar", lambda a, slc=slc: a.activation(
                    ECSR[:, slc], CTMP[:], AF.Exp), waits=[vC])
                wEC = dma("sync",
                          ecs_scr[c * 4:c * 4 + 4, :].rearrange("q (h l) -> (q h) l", h=16),
                          ECSR[:, slc], waits=[(S["A"], S["A"].n)])
                ecs_spill[c] = wEC
                op("scalar", lambda a, c=c: a.activation(
                    DCALL[:, c:c + 1], CTMP[:, LC - 1:LC], AF.Exp))
                op("vector", lambda v: v.tensor_tensor(
                    CTM2[:], CTMP[:, LC - 1:LC].broadcast_to([64, LC]),
                    CTMP[:], ALU.subtract))
                aD = op("scalar", lambda a: a.activation(CTM2[:], CTM2[:], AF.Exp),
                        waits=[(S["V"], S["V"].n)])
                op("vector", lambda v, slc=slc: v.tensor_tensor(
                    WMB[:, slc], CTM2[:], DTM[:, slc], ALU.mult), waits=[aD])
                op("vector", lambda v, slc=slc: v.tensor_tensor(
                    CSD[:, slc], CTMP[:], CSD[:, slc], ALU.subtract), waits=[aLDT])
            vWMB = (S["V"], S["V"].n)
            vCSD = vWMB
            wECS = wEC

            bank_tok = {i: None for i in range(8)}
            bank_tok[0] = vDB
            slab_tok = {0: None, 1: None}
            slab_used = {0: None, 1: None}
            cc_tok = None
            cc_used = None
            rep_used = {0: None, 1: None}
            y4_used = None
            xw_used = {0: None, 1: None}
            dcc_prev = None
            dcbb_prev6 = None
            y_last_prev = None
            glb_tok = {}
            cs_q_tok = {}
            lt_tok = {}
            gts_tok = {}
            rep_tok = {}
            t1_tok = {}

            def stage_rep(c, q4):
                qp = q4 % 2
                prev_t1 = t1_tok.get((c, q4 - 1) if q4 >= 1 else (c - 1, 3))
                wl = [cs_spill[c]] + ([prev_t1] if prev_t1 else [])
                r1 = dma("sync", CSREP[qp][:],
                         csm_scr[c * 4 + q4:c * 4 + q4 + 1, :]
                         .broadcast_to([128, 2048]), waits=wl)
                prev_cs = cs_q_tok.get((c, q4 - 2) if q4 >= 2 else (c - 1, q4 + 2))
                wl2 = [ecs_spill[c]] + ([prev_cs] if prev_cs else [])
                r2 = dma("sync", ECSREP[qp][:],
                         ecs_scr[c * 4 + q4:c * 4 + q4 + 1, :]
                         .broadcast_to([128, 2048]), waits=wl2)
                rep_tok[(c, q4)] = [r1, r2]

            def stage_t1(c, q4, vT3):
                qp = q4 % 2
                wl = rep_tok[(c, q4)] + [vT3]
                if (c, q4, "lt") in lt_tok:
                    pass
                prev = lt_tok.get((c, q4 - 1) if q4 >= 1 else (c - 1, 3))
                if prev is not None:
                    wl.append(prev)
                op("vector", lambda v, c=c, q4=q4, qp=qp: v.tensor_tensor(
                    T1B[qp][:].rearrange("p (h l) -> p h l", h=16),
                    CSREP[qp][:].rearrange("p (h l) -> p h l", h=16),
                    CST[:, 16 * q4:16 * q4 + 16].unsqueeze(2)
                    .broadcast_to([128, 16, 128]), ALU.subtract), waits=wl)
                t1_tok[(c, q4)] = op("vector", lambda v, qp=qp: v.tensor_tensor(
                    T1B[qp][:].rearrange("p (h l) -> p h l", l=128),
                    T1B[qp][:].rearrange("p (h l) -> p h l", l=128),
                    SLNEG[:].unsqueeze(1).broadcast_to([128, 16, 128]), ALU.add))
                lt_tok[(c, q4)] = op("scalar", lambda a, qp=qp: a.activation(
                    LTB[qp][:], T1B[qp][:], AF.Exp), waits=[t1_tok[(c, q4)]])

            def stage_gt(c, q4, p2, csl):
                qp = q4 % 2
                gt_mm = []
                for gg in range(8):
                    gc = q4 * 8 + gg
                    bank = gg % 2
                    slot = gg // 2
                    wl = list(cc_tok)
                    if slot == 0 and bank_tok[bank] is not None:
                        wl.append(bank_tok[bank])
                    gt_mm.append(op("tensor", lambda t, gc=gc, bank=bank, slot=slot, p2=p2, csl=csl: t.matmul(
                        PS[bank][:, slot * 128:(slot + 1) * 128],
                        BSL[p2][:, (gc // 4) % 8, csl:csl + 128],
                        CCB[:, gc, csl:csl + 128],
                        start=True, stop=True), waits=wl, inc=True))
                prevg = gts_tok.get((c, q4 - 2) if q4 >= 2 else (c - 1, q4 + 2))
                wl = [gt_mm[6]] + ([prevg] if prevg else [])
                aG1 = op("scalar", lambda a, qp=qp: a.activation(
                    GTS[qp][:].rearrange("p (k two l) -> p k two l", two=2, l=128)[:, :, 0, :],
                    PS[0][:].rearrange("p (k l) -> p k l", k=4), AF.Copy), waits=wl)
                aG2 = op("scalar", lambda a, qp=qp: a.activation(
                    GTS[qp][:].rearrange("p (k two l) -> p k two l", two=2, l=128)[:, :, 1, :],
                    PS[1][:].rearrange("p (k l) -> p k l", k=4), AF.Copy),
                    waits=[gt_mm[7]])
                bank_tok[0] = aG1
                bank_tok[1] = aG2
                gts_tok[(c, q4)] = aG2

            def stage_mid(c, q4, p2, csl, vSSB, vT2, trx):
                qp = q4 % 2
                wl = [lt_tok[(c, q4)], gts_tok[(c, q4)], bank_tok[0]]
                prev = glb_tok.get((c, q4 - 2) if q4 >= 2 else (c - 1, q4 + 2))
                if prev is not None:
                    wl.append(prev)
                op("vector", lambda v, qp=qp: v.tensor_tensor(
                    GLB[qp][:].rearrange("p (h two l) -> p h two l", two=2, l=128),
                    LTB[qp][:].rearrange("p (h two l) -> p h two l", two=2, l=128),
                    GTS[qp][:].rearrange("p (h l) -> p h l", h=8).unsqueeze(2)
                    .broadcast_to([128, 8, 2, 128]), ALU.mult), waits=wl)
                glb_tok[(c, q4)] = op("vector", lambda v, q4=q4, qp=qp: v.tensor_tensor(
                    GLB[qp][:], GLB[qp][:],
                    DIALL[:, q4 * 2048:(q4 + 1) * 2048], ALU.add))
                wl = rep_tok[(c, q4)][1:]
                prev = cs_q_tok.get((c, q4 - 2) if q4 >= 2 else (c - 1, q4 + 2))
                if prev is not None:
                    wl.append(prev)
                cs_q_tok[(c, q4)] = op("vector", lambda v, q4=q4, qp=qp, csl=csl: v.tensor_tensor(
                    CSBQ[qp][:].rearrange("p (h two l) -> p h two l", two=2, l=128),
                    CCB[:, q4 * 8:(q4 + 1) * 8, csl:csl + 128].unsqueeze(2)
                    .broadcast_to([128, 8, 2, 128]),
                    ECSREP[qp][:].rearrange("p (h two l) -> p h two l", two=2, l=128),
                    ALU.mult), waits=wl)
                st_toks = {}
                for gg in range(2):
                    gb = 2 * q4 + gg
                    bank = 2 + gg
                    xwp = gg
                    wl = [vT2, trx]
                    if xw_used[xwp] is not None:
                        wl.append(xw_used[xwp])
                    vXW = op("vector", lambda v, gb=gb, xwp=xwp, c=c: v.tensor_tensor(
                        XWS[xwp][:],
                        XTT[:, gb, :].rearrange("p (a q) -> p a q", a=2)
                        .unsqueeze(1).broadcast_to([128, 4, 2, 64]),
                        WTT[:, gb * 8:gb * 8 + 8]
                        .rearrange("p (a b) -> p a b", a=4)
                        .unsqueeze(3).broadcast_to([128, 4, 2, 64]),
                        ALU.mult), waits=wl)
                    wl = [vXW]
                    if bank_tok[bank] is not None:
                        wl.append(bank_tok[bank])
                    tST = op("tensor", lambda t, gb=gb, bank=bank, xwp=xwp: t.matmul(
                        PS[bank][:, :], BTT[:, gb, :],
                        XWS[xwp][:].rearrange("p a b q -> p (a b q)"),
                        start=True, stop=True), waits=wl, inc=True)
                    st_toks[gg] = tST
                    xw_used[xwp] = tST
                return st_toks

            def stage_y(c, q4, csl, vSSB, vDCB, st_toks):
                qp = q4 % 2
                y_toks = []
                for gg in range(8):
                    g = q4 * 8 + gg
                    bank = 6 + gg % 2
                    slot = gg // 2
                    h0, h1 = 2 * g, 2 * g + 1
                    wl = [glb_tok[(c, q4)], cs_q_tok[(c, q4)], vSSB]
                    if slot == 0:
                        wl.append(bank_tok[bank])

                    def ymm(t, gg=gg, bank=bank, slot=slot, h0=h0, h1=h1, qp=qp):
                        xs0 = XTT[:, h0 // 8, (h0 % 2) * 64:(h0 % 2) * 64 + 64]
                        xs1 = XTT[:, h1 // 8, (h1 % 2) * 64:(h1 % 2) * 64 + 64]
                        sl = slice(slot * 128, (slot + 1) * 128)
                        t.matmul(PS[bank][0:64, sl], xs0,
                                 GLB[qp][:, (gg * 2) * 128:(gg * 2 + 1) * 128],
                                 start=True, stop=False, tile_position=(0, 0))
                        t.matmul(PS[bank][0:64, sl], SSB[:, h0 * 64:h0 * 64 + 64],
                                 CSBQ[qp][:, (gg * 2) * 128:(gg * 2 + 1) * 128],
                                 start=False, stop=True, tile_position=(0, 0))
                        t.matmul(PS[bank][64:128, sl], xs1,
                                 GLB[qp][:, (gg * 2 + 1) * 128:(gg * 2 + 2) * 128],
                                 start=True, stop=False, tile_position=(0, 64))
                        return t.matmul(PS[bank][64:128, sl],
                                        SSB[:, h1 * 64:h1 * 64 + 64],
                                        CSBQ[qp][:, (gg * 2 + 1) * 128:(gg * 2 + 2) * 128],
                                        start=False, stop=True,
                                        tile_position=(0, 64))
                    y_toks.append(op("tensor", ymm, waits=wl, inc=True))
                for bb in range(2):
                    bank = 6 + bb
                    wl = [y_toks[6 + bb]]
                    if y4_used is not None:
                        wl.extend(y4_used)
                    aY = op("scalar", lambda a, q4=q4, bb=bb, bank=bank, csl=csl: a.activation(
                        Y4[:].rearrange("p (a b2) q -> p a b2 q", b2=2)
                        [:, q4 * 4:(q4 + 1) * 4, bb, csl:csl + 128],
                        PS[bank][:].rearrange("p (j q) -> p j q", j=4),
                        AF.Copy), waits=wl)
                    bank_tok[bank] = aY
                for gg in range(2):
                    gb = 2 * q4 + gg
                    bank = 2 + gg
                    wl = [vDCB, st_toks[gg], vSSB]
                    op("vector", lambda v, gb=gb: v.tensor_tensor(
                        SST[:, gb * 512:(gb + 1) * 512].rearrange("p (a q) -> p a q", a=8),
                        SST[:, gb * 512:(gb + 1) * 512].rearrange("p (a q) -> p a q", a=8),
                        DCB6[:, gb * 8:gb * 8 + 8].unsqueeze(2).broadcast_to([128, 8, 64]),
                        ALU.mult), waits=wl)
                    bank_tok[bank] = op("vector", lambda v, gb=gb, bank=bank: v.tensor_tensor(
                        SST[:, gb * 512:(gb + 1) * 512],
                        SST[:, gb * 512:(gb + 1) * 512],
                        PS[bank][:, :], ALU.add))
                return y_toks[-1]

            stage_rep(0, 0)
            for c in range(NCH):
                c2 = c // 2
                p2 = c2 % 2
                csl = 128 * (c % 2)
                if c % 2 == 0:
                    wl = [slab_used[p2]] if slab_used[p2] else []
                    for xg in range(8):
                        dma("gpsimd", XSL[p2][:, xg, :],
                            xbc_scr[xg * 128:(xg + 1) * 128, c * LC:(c + 2) * LC],
                            waits=wl if xg == 0 else [])
                    for xg in range(8):
                        dma("gpsimd", BSL[p2][:, xg, :],
                            xbc_scr[1024 + xg * 128:1024 + (xg + 1) * 128,
                                    c * LC:(c + 2) * LC])
                    wl = [cc_used] if cc_used else []
                    for gc in range(32):
                        dma("gpsimd", CCB[:, gc, :],
                            xbc_scr[2048 + gc * 128:2048 + (gc + 1) * 128,
                                    c * LC:(c + 2) * LC],
                            waits=wl if gc == 0 else [])
                    slab_all = B.dma_tokens("gpsimd")
                    slab_tok[p2] = slab_all
                    cc_tok = slab_all
                trev = []
                for half in range(4):
                    bank = 6 + half % 2
                    wl = list(slab_tok[p2])
                    if bank_tok[bank] is not None:
                        wl.append(bank_tok[bank])
                    if y_last_prev is not None:
                        wl.append(y_last_prev)

                    def tfn(t, half=half, p2=p2, csl=csl, bank=bank):
                        ins = None
                        src_t = XSL[p2] if half < 2 else BSL[p2]
                        for j in range(4):
                            xg = (half % 2) * 4 + j
                            ins = t.transpose(
                                ptb(bank)[:, j * 128:(j + 1) * 128],
                                src_t[:, xg, csl:csl + 128], IDENT[:])
                        return ins
                    tT = op("tensor", tfn, waits=wl, inc=True)

                    def evf(v, half=half, bank=bank):
                        dst = XTT if half < 2 else BTT
                        gi = (half % 2) * 4
                        return v.tensor_copy(
                            dst[:, gi:gi + 4, :],
                            ptb(bank).rearrange("p (j q) -> p j q", j=8)[:, 0:4, :])
                    bank_tok[bank] = op("vector", evf, waits=[tT])
                    trev.append(bank_tok[bank])
                trx = trev[-1]
                wl = [vWMB] + ([bank_tok[6]] if bank_tok[6] else [])
                t2 = op("tensor", lambda t, c=c: t.transpose(
                    ptb(6)[:, 128:192], WMB[:, c * LC:(c + 1) * LC], IDENT[0:64, 0:64]),
                    waits=wl, inc=True)
                vT2 = op("vector", lambda v: v.tensor_copy(WTT[:], ptb(6)[:, 128:192]),
                         waits=[t2])
                bank_tok[6] = vT2
                wl = [vCSD] + ([bank_tok[7]] if bank_tok[7] else [])
                tCS = op("tensor", lambda t, c=c: t.transpose(
                    PS[7][:, 0:64], CSD[:, c * LC:(c + 1) * LC], IDF[0:64, 0:64]),
                    waits=wl, inc=True)
                vT3 = op("vector", lambda v: v.tensor_copy(CST[:], PS[7][:, 0:64]),
                         waits=[tCS])
                bank_tok[7] = vT3
                wl = [vWMB] + ([dcc_prev] if dcc_prev else [])
                aDC = op("scalar", lambda a, c=c: a.copy(
                    DCC6[:], DCALL[:, c:c + 1]), waits=wl)
                wD0 = dma("sync", dcc_scr, DCC6[:], waits=[aDC])
                dcc_prev = wD0
                wl = [wD0] + ([dcbb_prev6] if dcbb_prev6 else [])
                wDB6 = dma("sync", DCBB[:],
                           dcc_scr.rearrange("a b -> b a").broadcast_to([128, 64]),
                           waits=wl)
                vDCB = op("vector", lambda v: v.tensor_copy(DCB6[:], DCBB[:]),
                          waits=[wDB6])
                dcbb_prev6 = vDCB
                wl = [(S["V"], S["V"].n)] + ([y_last_prev] if y_last_prev else [])
                vSSB = op("vector", lambda v: v.tensor_copy(SSB[:], SST[:]), waits=wl)
                stage_t1(c, 0, vT3)
                stage_gt(c, 0, p2, csl)
                for q4 in range(4):
                    if q4 + 1 < 4:
                        stage_rep(c, q4 + 1)
                        stage_t1(c, q4 + 1, vT3)
                        stage_gt(c, q4 + 1, p2, csl)
                    elif c + 1 < NCH:
                        stage_rep(c + 1, 0)
                    st_toks = stage_mid(c, q4, p2, csl, vSSB, vT2, trx)
                    y_last_prev = stage_y(c, q4, csl, vSSB, vDCB, st_toks)
                    qp = q4 % 2
                    rep_used[qp] = y_last_prev
                if c % 2 == 1:
                    wl = [(S["A"], S["A"].n)]
                    for j in range(32):
                        dma("sync", y_scr[j * 128:(j + 1) * 128,
                                          (c - 1) * LC:(c + 1) * LC],
                            Y4[:, j, :], waits=wl if j == 0 else [])
                    y4_used = B.dma_tokens("sync")
                    slab_used[p2] = y_last_prev
                    cc_used = y_last_prev
            barrier()

        es_mid.close()

        if "ys" in dbg:
            t = dbg_tensor("ys", [DI, TM], BF16)
            dma("sync", t, y_scr)
            barrier()

        SC82 = float(2.0 ** -3.5)
        with ExitStack() as es7:
            def sb7(name, shape, dtype):
                return es7.enter_context(nc.sbuf_tensor(name, shape, dtype))

            X2H = sb7("X2H", [128, 64, 1024], BF16)
            WDB = [sb7(f"WDB{i}", [128, 64, 128], BF16) for i in range(2)]
            G0T = [sb7(f"G0T{i}", [128, 512], BF16) for i in range(2)]
            h1v = h1_scr.rearrange("(q v) t -> v q t", v=64)
            bank_tok7 = {i: None for i in range(8)}
            wdb_free = {0: None, 1: None}
            g0_free = {0: None, 1: None}
            x2_used = None
            for half in range(2):
                ho = half * 1024
                for v64 in range(64):
                    wl = [x2_used] if x2_used else []
                    wH = dma("gpsimd", X2H[:, v64, :],
                             h1v[v64, :, ho:ho + 1024], waits=wl)
                    for s2 in range(2):
                        bank = (v64 * 2 + s2) % 6
                        wl = [wH]
                        if bank_tok7[bank] is not None:
                            wl.append(bank_tok7[bank])
                        tMM = op("tensor", lambda t, v64=v64, s2=s2, bank=bank: t.matmul(
                            PS[bank][:, :], H128[:],
                            X2H[:, v64, s2 * 512:(s2 + 1) * 512],
                            start=True, stop=True), waits=wl, inc=True)
                        bank_tok7[bank] = op("scalar", lambda a, v64=v64, s2=s2, bank=bank: a.activation(
                            X2H[:, v64, s2 * 512:(s2 + 1) * 512], PS[bank][:, :],
                            AF.Copy, scale=SC82), waits=[tMM])
                aX2 = (S["A"], S["A"].n)
                for m in range(16):
                    p = m % 2
                    wl = [wdb_free[p]] if wdb_free[p] else []
                    wW = dma("gpsimd", WDB[p][:], d_wdown[m], waits=wl)
                    for s2 in range(2):
                        bank = 6 + s2
                        gp = s2
                        wl = [wW, aX2]
                        if bank_tok7[bank] is not None:
                            wl.append(bank_tok7[bank])

                        def mmd(t, p=p, s2=s2, bank=bank):
                            ins = None
                            for v64 in range(64):
                                ins = t.matmul(PS[bank][:, :], WDB[p][:, v64, :],
                                               X2H[:, v64, s2 * 512:(s2 + 1) * 512],
                                               start=(v64 == 0), stop=(v64 == 63))
                            return ins
                        tMM = op("tensor", mmd, waits=wl, inc=True)
                        if s2 == 1:
                            wdb_free[p] = tMM
                        wl = [tMM]
                        if g0_free[gp] is not None:
                            wl.append(g0_free[gp])
                        aG0 = op("scalar", lambda a, m=m, gp=gp, bank=bank: a.activation(
                            G0T[gp][:], PS[bank][:, :], AF.Silu,
                            scale=WSCDP[:, m:m + 1]), waits=wl)
                        bank_tok7[bank] = aG0
                        g0_free[gp] = dma(
                            "sync", g_scr[m * 128:(m + 1) * 128,
                                          ho + s2 * 512:ho + (s2 + 1) * 512],
                            G0T[gp][:], waits=[aG0])
                x2_used = bank_tok7[7]
            barrier()

        if "gs" in dbg:
            t = dbg_tensor("gs", [DM, TM], BF16)
            dma("sync", t, g_scr)
            barrier()

        es_b = ExitStack()

        def sbb(name, shape, dtype):
            return es_b.enter_context(nc.sbuf_tensor(name, shape, dtype))

        YN = sbb("YN", [128, 32, TM], BF16)
        with ExitStack() as es8:
            def sb8(name, shape, dtype):
                return es8.enter_context(nc.sbuf_tensor(name, shape, dtype))

            OG8 = sb8("OG8", [128, 32, 32], BF16)
            EB8 = sb8("EB8", [32, 32, 128], BF16)
            YT = [sb8(f"YT{i}", [128, TM], BF16) for i in range(2)]
            ZT = [sb8(f"ZT{i}", [128, TM], BF16) for i in range(2)]
            ZS = sb8("ZS", [128, TM], BF16)
            SQG = sb8("SQG", [128, TM], BF16)
            SMS = sb8("SMS", [32, TM], F32)
            RSTB = sb8("RSTB", [32, TM], BF16)
            dma("sync", OG8[:], og_c.rearrange("(g p) m -> p g m", g=32))
            dma("sync", EB8[:], eb_c.rearrange("(g k) m -> k g m", g=32))
            wC8 = B.dma_tokens("sync")
            yz_free = {0: None, 1: None}
            sq_toks = []
            for grp in range(32):
                p = grp % 2
                wl = ([yz_free[p]] if yz_free[p] else [])
                wY = dma("sync", YT[p][:], y_scr[grp * 128:(grp + 1) * 128, :], waits=wl)
                wZ = dma("sync", ZT[p][:], z_scr[grp * 128:(grp + 1) * 128, HALO:], waits=wl)
                aZ = op("scalar", lambda a, p=p: a.activation(ZS[:], ZT[p][:], AF.Silu),
                        waits=[wZ, (S["V"], S["V"].n)])
                vYG = op("vector", lambda v, grp=grp, p=p: v.tensor_tensor(
                    YN[:, grp, :], YT[p][:], ZS[:], ALU.mult), waits=[wY, aZ])
                yz_free[p] = vYG
                aSQ = op("scalar", lambda a, grp=grp: a.activation(
                    SQG[:], YN[:, grp, :], AF.Square),
                    waits=[vYG] + (sq_toks[-4:] if grp >= 1 else []))
                sq4 = []
                for s4 in range(4):
                    wl = [aSQ] + (wC8 if grp == 0 else [])
                    sq4.append(op("tensor", lambda t, grp=grp, s4=s4: t.matmul(
                        PS[s4][0:32, :], OG8[:, grp, :],
                        SQG[:, s4 * 512:(s4 + 1) * 512],
                        start=(grp == 0), stop=(grp == 31)), waits=wl, inc=True))
                sq_toks = sq4
            ev4 = []
            for s4 in range(4):
                ev4.append(op("vector", lambda v, s4=s4: v.tensor_copy(
                    SMS[:, s4 * 512:(s4 + 1) * 512], PS[s4][0:32, :]),
                    waits=[sq_toks[s4]]))
            aL = op("scalar", lambda a: a.activation(
                SMS[:], SMS[:], AF.Ln, scale=1.0 / 128.0, bias=EPSC[0:32, :]),
                waits=ev4)
            aR8 = op("scalar", lambda a: a.activation(
                SMS[:], SMS[:], AF.Exp, scale=-0.5))
            vRB = op("vector", lambda v: v.tensor_copy(RSTB[:], SMS[:]), waits=[aR8])
            bank_tok8 = {i: None for i in range(8)}
            for grp in range(32):
                for s4 in range(4):
                    bank = (grp * 4 + s4) % 8
                    wl = [vRB]
                    if bank_tok8[bank] is not None:
                        wl.append(bank_tok8[bank])
                    tB = op("tensor", lambda t, grp=grp, s4=s4, bank=bank: t.matmul(
                        PS[bank][:, :], EB8[:, grp, :],
                        RSTB[0:32, s4 * 512:(s4 + 1) * 512],
                        start=True, stop=True), waits=wl, inc=True)
                    bank_tok8[bank] = op("vector", lambda v, grp=grp, s4=s4, bank=bank: v.scalar_tensor_tensor(
                        YN[:, grp, s4 * 512:(s4 + 1) * 512],
                        YN[:, grp, s4 * 512:(s4 + 1) * 512],
                        NW[:, grp:grp + 1], PS[bank][:, :],
                        ALU.mult, ALU.mult), waits=[tB])
            barrier()

        if "yn" in dbg:
            t = dbg_tensor("yn", [128, 32 * TM], BF16)
            dma("sync", t.rearrange("p (a b) -> p a b", a=32), YN[:])
            barrier()

        with ExitStack() as es9:
            def sb9(name, shape, dtype):
                return es9.enter_context(nc.sbuf_tensor(name, shape, dtype))

            WOB = [sb9(f"WOB{i}", [128, 32, 128], BF16) for i in range(2)]
            GT9 = [sb9(f"GT9{i}", [128, TM], BF16) for i in range(2)]
            SOT = sb9("SOT", [128, 512], BF16)
            SGT = [sb9(f"SGT{i}", [128, TM], BF16) for i in range(2)]
            bank_tok9 = {i: None for i in range(8)}
            w_free9 = {0: None, 1: None}
            sg_free = {0: None, 1: None}
            for m in range(16):
                p = m % 2
                wl = [w_free9[p]] if w_free9[p] else []
                wW = dma("gpsimd", WOB[p][:], d_wout[m], waits=wl)
                wG = dma("sync", GT9[p][:], g_scr[m * 128:(m + 1) * 128, :],
                         waits=[sg_free[p]] if sg_free[p] else [])
                for s4 in range(4):
                    bank = (m * 4 + s4) % 8

                    def mmo(t, p=p, s4=s4, bank=bank):
                        ins = None
                        for k in range(32):
                            ins = t.matmul(PS[bank][:, :], WOB[p][:, k, :],
                                           YN[:, k, s4 * 512:(s4 + 1) * 512],
                                           start=(k == 0), stop=(k == 31))
                        return ins
                    wl = [wW]
                    if bank_tok9[bank] is not None:
                        wl.append(bank_tok9[bank])
                    tMM = op("tensor", mmo, waits=wl, inc=True)
                    if s4 == 3:
                        w_free9[p] = tMM
                    aS = op("scalar", lambda a, s4=s4, bank=bank: a.activation(
                        SOT[:], PS[bank][:, :], AF.Copy),
                        waits=[tMM, (S["V"], S["V"].n)])
                    bank_tok9[bank] = aS
                    op("vector", lambda v, p=p, s4=s4: v.tensor_tensor(
                        SGT[p][:, s4 * 512:(s4 + 1) * 512], SOT[:],
                        GT9[p][:, s4 * 512:(s4 + 1) * 512], ALU.mult),
                        waits=[aS, wG])
                sg_free[p] = dma("sync", sg_scr[m * 128:(m + 1) * 128, :], SGT[p][:],
                                 waits=[(S["V"], S["V"].n)])
            barrier()
        es_b.close()

        with ExitStack() as es10:
            def sb10(name, shape, dtype):
                return es10.enter_context(nc.sbuf_tensor(name, shape, dtype))

            OUTS = sb10("OUTS", [128, 16, DM], BF16)
            SGB = [sb10(f"SGB{i}", [128, TM], BF16) for i in range(2)]
            UB = [sb10(f"UB{i}", [128, DM], F32) for i in range(2)]
            OUTF = [sb10(f"OUTF{i}", [128, DM], F32) for i in range(2)]
            bank_tokX = {6: None, 7: None}
            sgb_free = {0: None, 1: None}
            for m in range(16):
                p = m % 2
                wl = [sgb_free[p]] if sgb_free[p] else []
                wS9 = dma("sync", SGB[p][:], sg_scr[m * 128:(m + 1) * 128, :], waits=wl)
                for half in range(4):
                    bank = 6 + half % 2
                    wl = [wS9]
                    if bank_tokX[bank] is not None:
                        wl.append(bank_tokX[bank])

                    def tfn(t, p=p, half=half, bank=bank):
                        ins = None
                        for j in range(4):
                            tt = half * 4 + j
                            ins = t.transpose(
                                ptb(bank)[:, j * 128:(j + 1) * 128],
                                SGB[p][:, tt * 128:(tt + 1) * 128], IDENT[:])
                        return ins
                    tT = op("tensor", tfn, waits=wl, inc=True)
                    bank_tokX[bank] = op("vector", lambda v, m=m, half=half, bank=bank: v.tensor_copy(
                        OUTS[:, half * 4:half * 4 + 4, m * 128:(m + 1) * 128],
                        ptb(bank).rearrange("p (j q) -> p j q", j=8)[:, 0:4, :]),
                        waits=[tT])
                sgb_free[p] = (S["V"], S["V"].n)
            vTdone = (S["V"], S["V"].n)
            ub_free = {0: None, 1: None}
            for tt in range(16):
                p = tt % 2
                wl = [ub_free[p]] if ub_free[p] else []
                wU = dma("sync", UB[p][:],
                         d_u[HALO + tt * 128:HALO + (tt + 1) * 128, :], waits=wl)
                vO = op("vector", lambda v, tt=tt, p=p: v.tensor_tensor(
                    OUTF[p][:], UB[p][:], OUTS[:, tt, :], ALU.add),
                    waits=[wU, vTdone])
                ub_free[p] = dma("sync", d_out[tt * 128:(tt + 1) * 128, :],
                                 OUTF[p][:], waits=[vO])
            barrier()

        for (c, t) in B.dma_tokens("sync"):
            B.raw("sync", lambda e, c=c, t=t: e.wait_ge(c.h, t))
        for (c, t) in B.dma_tokens("gpsimd"):
            B.raw("gpsimd", lambda e, c=c, t=t: e.wait_ge(c.h, t))
        B.emit()
    return nc, dbg_outs


def _blk_weightT(wT, mt_w=128, kt=16):
    cin, cout = wT.shape
    assert cin == kt * 128
    blocks = []
    for m0 in range(0, cout, mt_w):
        w = min(mt_w, cout - m0)
        b = wT[:, m0:m0 + w].reshape(kt, 128, w).transpose(1, 0, 2)
        blocks.append(np.ascontiguousarray(b, dtype=np.float32))
    return blocks


def _host_prep(inputs):
    u = np.asarray(inputs["L0_attn_u"])
    WinT = np.ascontiguousarray(np.asarray(inputs["L0_attn_W_in"]).T)
    WupT = np.ascontiguousarray(np.asarray(inputs["L0_ffn_W_up"]).T)
    WgateT = np.ascontiguousarray(np.asarray(inputs["L0_ffn_W_gate"]).T)
    WoutT = np.ascontiguousarray(np.asarray(inputs["L0_attn_W_out"]).T)
    Wdown = np.asarray(inputs["L0_ffn_W_down"])

    win_blk = np.stack(_blk_weightT(WinT[:, :10240], 128, 16))
    win_dt_blk = _blk_weightT(WinT[:, 10240:10304], 64, 16)[0]
    wup_blk = np.stack(_blk_weightT(WupT, 128, 16))
    wgate_blk = np.stack(_blk_weightT(WgateT, 128, 16))
    wout_blk = np.stack(_blk_weightT(WoutT, 128, 32))

    qp = np.arange(128)
    vv = np.arange(64)
    cmap = (qp[None, :] >> 1) * 128 + (qp[None, :] & 1) * 64 + vv[:, None]
    WdT_perm = Wdown.T[cmap.reshape(-1), :].reshape(64, 128, 2048)
    wdown_blk = np.ascontiguousarray(
        WdT_perm.reshape(64, 128, 16, 128).transpose(2, 1, 0, 3), dtype=np.float32)

    def pack_pp(vec, ntile):
        v = np.asarray(vec, np.float32)
        return np.ascontiguousarray(v.reshape(ntile, 128).T)

    convw_pk = np.ascontiguousarray(
        np.asarray(inputs["L0_attn_conv_w"], np.float32).reshape(48, 128, 4).transpose(1, 0, 2))
    convb_pk = pack_pp(inputs["L0_attn_conv_b"], 48)
    wsc = np.asarray(inputs["L0_ffn_Wscale_ug"], np.float32)

    shared = dict(
        win_blk=win_blk, win_dt_blk=win_dt_blk, wup_blk=wup_blk,
        wgate_blk=wgate_blk, wdown_blk=wdown_blk, wout_blk=wout_blk,
        convw_pk=convw_pk, convb_pk=convb_pk,
        dtbias=np.asarray(inputs["L0_attn_dt_bias"], np.float32).reshape(64, 1),
        alog=np.asarray(inputs["L0_attn_A_log"], np.float32).reshape(64, 1),
        Drow=np.asarray(inputs["L0_attn_D"], np.float32).reshape(1, 64),
        nw_pk=pack_pp(inputs["L0_attn_norm_weight"], 32),
        suug_pk=pack_pp(inputs["L0_ffn_SU_ug"], 16),
        sudp_pk=pack_pp(inputs["L0_ffn_SU_dp"], 64),
        wscup_pk=pack_pp(wsc[:INTER], 64),
        wscgate_pk=pack_pp(wsc[INTER:], 64),
        wscdp_pk=pack_pp(inputs["L0_ffn_Wscale_dp"], 16),
    )

    in_maps = []
    for core in range(8):
        b, q = core // 4, core % 4
        start = q * TM
        u_loc = np.zeros((TL, DM), np.float32)
        lo = max(0, start - HALO)
        u_loc[HALO - (start - lo):] = u[b, lo:start + TM]
        u_pfx = np.zeros((PFX, DM), np.float32)
        plo = max(0, start - PFX)
        nreal = start - plo
        if nreal > 0:
            u_pfx[PFX - nreal:] = u[b, plo:start]
        dt_mask = np.zeros((NH, PFX), np.float32)
        if nreal > 0:
            dt_mask[:, PFX - nreal:] = 1.0
        m = dict(shared)
        m.update(u_loc=u_loc, u_pfx=u_pfx, dt_mask=dt_mask)
        in_maps.append(m)
    return in_maps


def kernel(**inputs):
    in_maps = _host_prep(inputs)
    nc, _ = build_program()
    res = run_bass_kernel_spmd(nc, in_maps, core_ids=list(range(8)))
    out = np.zeros((2, 8192, DM), np.float32)
    for core in range(8):
        b, q = core // 4, core % 4
        out[b, q * TM:(q + 1) * TM] = res.results[core]["out_loc"]
    return out


# version 19
# speedup vs baseline: 1.0905x; 1.0035x over previous
import numpy as np
import ml_dtypes
from collections import deque
from contextlib import ExitStack

import concourse.bass as bass
import concourse.mybir as mybir
from concourse.bass_utils import run_bass_kernel_spmd

F32 = mybir.dt.float32
BF16 = mybir.dt.bfloat16
AF = mybir.ActivationFunctionType
ALU = mybir.AluOpType
AXX = mybir.AxisListType.X

DM = 2048
DI = 4096
DXB = 1024
DSTATE = 128
HEADDIM = 64
NH = 64
NG = 32
CONVD = 6144
DPROJ = 10304
INTER = 8192
EPS = 1e-5

TM = 2048
HALO = 128
TL = TM + HALO
PFX = 6144
LC = 128
NCH = TM // LC
LCP = 512
NCHP = PFX // LCP
LCPB = 1024
NCHB = PFX // LCPB
PFXB_IN_PH3 = 999

ENGS = ("sync", "gpsimd", "tensor", "vector", "scalar")
ECH = {"sync": "dS", "gpsimd": "dG", "tensor": "T", "vector": "V", "scalar": "A"}


def _H(n):
    h = np.array([[1.0]], dtype=np.float64)
    while h.shape[0] < n:
        h = np.block([[h, h], [h, -h]])
    return h


class Ch:
    def __init__(self, handle):
        self.h = handle
        self.n = 0

    def inc(self, k=1):
        self.n += k
        return self.n


class Builder:

    NDMA = 8

    def __init__(self, nc, es):
        self.nc = nc
        self.q = {e: [] for e in ENGS}
        self.S = {k: Ch(es.enter_context(nc.semaphore("s_" + k)))
                  for k in ["T", "V", "A", "bar"]}
        self.dring = {
            "sync": [Ch(es.enter_context(nc.semaphore(f"s_dS{i}"))) for i in range(self.NDMA)],
            "gpsimd": [Ch(es.enter_context(nc.semaphore(f"s_dG{i}"))) for i in range(self.NDMA)],
        }
        self.dcnt = {"sync": 0, "gpsimd": 0}
        self.nbar = 0

    def op(self, eng, fn, waits=(), inc=False):
        S = self.S
        ch = S[ECH[eng]] if ECH[eng] in S else None
        auto = eng in ("vector", "scalar")
        pre = ch.n if auto else None
        tgt = ch.inc(1) if (auto or inc) else None
        wl = list(waits)

        def f(e, wl=wl, pre=pre, tgt=tgt, fn=fn, ch=ch):
            if pre:
                e.wait_ge(ch.h, pre)
            for (c, t) in wl:
                if t:
                    e.wait_ge(c.h, t)
            ins = fn(e)
            if tgt is not None:
                ins.then_inc(ch.h, 1)
        self.q[eng].append(f)
        return (ch, tgt) if tgt is not None else None

    def raw(self, eng, fn):
        self.q[eng].append(fn)

    def dma(self, eng, out, in_, waits=()):
        assert eng in ("sync", "gpsimd")
        i = self.dcnt[eng]
        self.dcnt[eng] += 1
        ch = self.dring[eng][i % self.NDMA]
        tgt = ch.inc(16)
        wl = list(waits)

        def f(e, wl=wl, tgt=tgt, ch=ch):
            for (c, t) in wl:
                if t:
                    e.wait_ge(c.h, t)
            if tgt > 16:
                e.wait_ge(ch.h, tgt - 16)
            e.dma_start(out=out, in_=in_).then_inc(ch.h, 16)
        self.q[eng].append(f)
        return (ch, tgt)

    def dma_tokens(self, eng):
        return [(c, c.n) for c in self.dring[eng] if c.n > 0]

    def barrier(self):
        S = self.S
        self.nbar += 1
        n = self.nbar
        marks = {e: S[ECH[e]].n if ECH[e] in S else None for e in ENGS}
        dS_tok = self.dma_tokens("sync")
        dG_tok = self.dma_tokens("gpsimd")

        def mk(ename):
            def f(e):
                if ename == "sync":
                    for (c, t) in dS_tok:
                        e.wait_ge(c.h, t)
                elif ename == "gpsimd":
                    for (c, t) in dG_tok:
                        e.wait_ge(c.h, t)
                elif marks[ename]:
                    e.wait_ge(S[ECH[ename]].h, marks[ename])
                e.nop().then_inc(S["bar"].h, 1)
                e.wait_ge(S["bar"].h, 5 * n)
            return f
        for e in ENGS:
            self.q[e].append(mk(e))

    def emit(self):
        with self.nc.Block() as block:
            qs = self.q

            @block.sync
            def _(e):
                for f in qs["sync"]:
                    f(e)

            @block.gpsimd
            def _(e):
                for f in qs["gpsimd"]:
                    f(e)

            @block.tensor
            def _(e):
                for f in qs["tensor"]:
                    f(e)

            @block.vector
            def _(e):
                for f in qs["vector"]:
                    f(e)

            @block.scalar
            def _(e):
                for f in qs["scalar"]:
                    f(e)


def build_program(dbg=()):
    dbg = set(dbg)
    nc = bass.Bass("TRN2", target_bir_lowering=False, debug=False)

    d_u = nc.dram_tensor("u_loc", [TL, DM], F32, kind="ExternalInput").ap()
    d_upfx = nc.dram_tensor("u_pfx", [PFX, DM], F32, kind="ExternalInput").ap()
    d_dtmask = nc.dram_tensor("dt_mask", [NH, PFX], F32, kind="ExternalInput").ap()
    d_win = nc.dram_tensor("win_blk", [80, 128, 16, 128], F32, kind="ExternalInput").ap()
    d_windt = nc.dram_tensor("win_dt_blk", [128, 16, 64], F32, kind="ExternalInput").ap()
    d_wup = nc.dram_tensor("wup_blk", [64, 128, 16, 128], F32, kind="ExternalInput").ap()
    d_wgate = nc.dram_tensor("wgate_blk", [64, 128, 16, 128], F32, kind="ExternalInput").ap()
    d_wdown = nc.dram_tensor("wdown_blk", [16, 128, 64, 128], F32, kind="ExternalInput").ap()
    d_wout = nc.dram_tensor("wout_blk", [16, 128, 32, 128], F32, kind="ExternalInput").ap()
    d_convw = nc.dram_tensor("convw_pk", [128, 48, 4], F32, kind="ExternalInput").ap()
    d_convb = nc.dram_tensor("convb_pk", [128, 48], F32, kind="ExternalInput").ap()
    d_dtbias = nc.dram_tensor("dtbias", [64, 1], F32, kind="ExternalInput").ap()
    d_alog = nc.dram_tensor("alog", [64, 1], F32, kind="ExternalInput").ap()
    d_Drow = nc.dram_tensor("Drow", [1, 64], F32, kind="ExternalInput").ap()
    d_nw = nc.dram_tensor("nw_pk", [128, 32], F32, kind="ExternalInput").ap()
    d_suug = nc.dram_tensor("suug_pk", [128, 16], F32, kind="ExternalInput").ap()
    d_sudp = nc.dram_tensor("sudp_pk", [128, 64], F32, kind="ExternalInput").ap()
    d_wscup = nc.dram_tensor("wscup_pk", [128, 64], F32, kind="ExternalInput").ap()
    d_wscgate = nc.dram_tensor("wscgate_pk", [128, 64], F32, kind="ExternalInput").ap()
    d_wscdp = nc.dram_tensor("wscdp_pk", [128, 16], F32, kind="ExternalInput").ap()
    d_out = nc.dram_tensor("out_loc", [TM, DM], F32, kind="ExternalOutput").ap()

    ident_c = nc.inline_tensor(np.eye(128, dtype=ml_dtypes.bfloat16), "c_ident").ap()
    identf_c = nc.inline_tensor(np.eye(128, dtype=np.float32), "c_identf").ap()
    h128_c = nc.inline_tensor(np.ascontiguousarray(_H(128).astype(ml_dtypes.bfloat16)), "c_h128").ap()
    m1_c = nc.inline_tensor(np.ascontiguousarray(
        np.kron(np.eye(2), _H(64)).astype(ml_dtypes.bfloat16)), "c_m1").ap()
    ones1_c = nc.inline_tensor(np.ones((1, 128), dtype=ml_dtypes.bfloat16), "c_ones1").ap()
    slneg_np = np.where(np.arange(128)[None, :] < np.arange(128)[:, None], -1e30, 0.0)
    slneg_c = nc.inline_tensor(np.ascontiguousarray(
        slneg_np.astype(ml_dtypes.bfloat16)), "c_slneg").ap()
    og_np = np.zeros((32, 128, 32), dtype=ml_dtypes.bfloat16)
    for g in range(32):
        og_np[g, :, g] = 1
    og_c = nc.inline_tensor(np.ascontiguousarray(og_np.reshape(32 * 128, 32)), "c_og").ap()
    eb_np = np.zeros((32, 32, 128), dtype=ml_dtypes.bfloat16)
    for g in range(32):
        eb_np[g, g, :] = 1
    eb_c = nc.inline_tensor(np.ascontiguousarray(eb_np.reshape(32 * 32, 128)), "c_eb").ap()

    z_scr = nc.dram_tensor("z_scr", [DI, TL], BF16, kind="Internal").ap()
    xbc_scr = nc.dram_tensor("xbc_scr", [CONVD, TM], BF16, kind="Internal").ap()
    h1_scr = nc.dram_tensor("h1_scr", [INTER, TM], BF16, kind="Internal").ap()
    y_scr = nc.dram_tensor("y_scr", [DI, TM], BF16, kind="Internal").ap()
    g_scr = nc.dram_tensor("g_scr", [DM, TM], BF16, kind="Internal").ap()
    sg_scr = nc.dram_tensor("sg_scr", [DM, TM], BF16, kind="Internal").ap()
    csm_scr = nc.dram_tensor("csm_scr", [NCH * 4, 2048], F32, kind="Internal").ap()
    dcc_scr = nc.dram_tensor("dcc_scr", [NH, 1], BF16, kind="Internal").ap()
    ecs_scr = nc.dram_tensor("ecs_scr", [NCH * 4, 2048], BF16, kind="Internal").ap()
    xbcpre_scr = nc.dram_tensor("xbcpre_scr", [2048, PFX], BF16, kind="Internal").ap()
    wpfx_scr = nc.dram_tensor("wpfx_scr", [16, 128, 16, 128], BF16, kind="Internal").ap()
    wbf_scr = nc.dram_tensor("wbf_scr", [64, PFX], BF16, kind="Internal").ap()

    dbg_outs = {}

    def dbg_tensor(name, shape, dtype=F32):
        t = nc.dram_tensor("dbg_" + name, shape, dtype, kind="ExternalOutput").ap()
        dbg_outs[name] = t
        return t

    es = ExitStack()
    with es:
        B = Builder(nc, es)
        S = B.S

        def sb(name, shape, dtype):
            return es.enter_context(nc.sbuf_tensor(name, shape, dtype))

        IDENT = sb("IDENT", [128, 128], BF16)
        IDF = sb("IDF", [128, 128], F32)
        H128 = sb("H128", [128, 128], BF16)
        M1C = sb("M1C", [128, 128], BF16)
        SLNEG = sb("SLNEG", [128, 128], BF16)
        ONES1 = sb("ONES1", [1, 128], BF16)
        CONVW = sb("CONVW", [128, 48, 4], F32)
        CONVB = sb("CONVB", [128, 48], F32)
        DTBIAS = sb("DTBIAS", [64, 1], F32)
        AH = sb("AH", [64, 1], F32)
        DROW = sb("DROW", [1, 64], BF16)
        NW = sb("NW", [128, 32], F32)
        SUUG = sb("SUUG", [128, 16], F32)
        SUDP = sb("SUDP", [128, 64], F32)
        WSCUP = sb("WSCUP", [128, 64], F32)
        WSCGATE = sb("WSCGATE", [128, 64], F32)
        WSCDP = sb("WSCDP", [128, 16], F32)
        EPSC = sb("EPSC", [128, 1], F32)
        ONEC = sb("ONEC", [128, 1], F32)
        DCBB = sb("DCBB", [128, 64], BF16)

        es_mid = ExitStack()

        def sbm(name, shape, dtype):
            return es_mid.enter_context(nc.sbuf_tensor(name, shape, dtype))

        SST = sbm("SST", [128, NH * HEADDIM], F32)

        PS = [es.enter_context(nc.psum_tensor(f"ps{i}", [128, 512], F32)) for i in range(8)]

        def ptb(i):
            return PS[i][:].bitcast(BF16)

        op, dma, barrier = B.op, B.dma, B.barrier

        for dst, src in [
            (IDENT[:], ident_c), (IDF[:], identf_c), (H128[:], h128_c), (M1C[:], m1_c),
            (ONES1[:], ones1_c), (SLNEG[:], slneg_c),
            (CONVW[:], d_convw), (CONVB[:], d_convb), (DTBIAS[:], d_dtbias),
            (AH[:], d_alog), (NW[:], d_nw),
            (SUUG[:], d_suug), (SUDP[:], d_sudp), (WSCUP[:], d_wscup),
            (WSCGATE[:], d_wscgate), (WSCDP[:], d_wscdp),
        ]:
            dma("sync", dst, src)
        wS = B.dma_tokens("sync")
        wG0 = dma("gpsimd", DROW[:], d_Drow)

        op("vector", lambda v: v.memset(EPSC[:], EPS))
        op("vector", lambda v: v.memset(ONEC[:], 1.0))
        aExp = op("scalar", lambda a: a.activation(AH[:], AH[:], AF.Exp),
                  waits=wS)
        op("vector", lambda v: v.tensor_scalar_mul(AH[:], AH[:], -1.0),
           waits=[aExp])
        barrier()

        def make_ln(tag, sba=None):
            sba = sba or sb
            U = [sba(f"lnU{p}_{tag}", [128, DM], F32) for p in range(3)]
            XT = [sba(f"lnX{p}_{tag}", [128, DM], BF16) for p in range(2)]
            SQ = sba(f"lnSQ_{tag}", [128, DM], BF16)
            ST = [sba(f"lnST{p}_{tag}", [128, 8], F32) for p in range(3)]
            st = dict(f=0, s=0, t=0)
            descs = {}
            tok = {}
            tG = {}
            vEv = {}

            def feed(src_dram, src_tok, dst_cm, dst_tile, dst_guard=None,
                     done=None):
                i = st["f"]
                st["f"] += 1
                descs[i] = (dst_cm, dst_tile, dst_guard, done)
                p = i % 3
                wl = []
                if (i - 3, "vN") in tok:
                    wl = [tok[(i - 3, "vN")], tok[(i - 3, "aSq")]]
                tok[(i, "dma")] = dma("sync", U[p][:],
                                      src_dram[src_tok:src_tok + 128, :],
                                      waits=wl)

            def stats():
                i = st["s"]
                st["s"] += 1
                p = i % 3
                px = i % 2
                tok[(i, "vSum")] = op("scalar", lambda a: a.activation(
                    SQ[:], U[p][:], AF.Copy, accum_out=ST[p][:, 0:1]),
                    waits=[tok[(i, "dma")]] +
                    ([tok[(i - 3, "vN")]] if (i - 3, "vN") in tok else []))
                tok[(i, "aSq")] = op("scalar", lambda a: a.activation(
                    SQ[:], U[p][:], AF.Square, accum_out=ST[p][:, 1:2]))
                tok[(i, "aM")] = op("scalar", lambda a: a.mul(
                    ST[p][:, 2:3], ST[p][:, 0:1], 1.0 / DM))
                op("vector", lambda v: v.tensor_tensor(
                    ST[p][:, 3:4], ST[p][:, 2:3], ST[p][:, 2:3], ALU.mult),
                    waits=[tok[(i, "aM")]])
                op("vector", lambda v: v.tensor_scalar(
                    ST[p][:, 4:5], ST[p][:, 1:2], 1.0 / DM, None, ALU.mult))
                tok[(i, "vVar")] = op("vector", lambda v: v.tensor_tensor(
                    ST[p][:, 5:6], ST[p][:, 4:5], ST[p][:, 3:4], ALU.subtract))
                op("scalar", lambda a: a.activation(
                    ST[p][:, 6:7], ST[p][:, 5:6], AF.Ln, bias=EPSC[:]),
                    waits=[tok[(i, "vVar")]])
                tok[(i, "aR")] = op("scalar", lambda a: a.activation(
                    ST[p][:, 7:8], ST[p][:, 6:7], AF.Exp, scale=-0.5))
                tok[(i, "vN")] = op("vector", lambda v: v.tensor_scalar(
                    XT[px][:], U[p][:], ST[p][:, 2:3], ST[p][:, 7:8],
                    ALU.subtract, ALU.mult),
                    waits=[tok[(i, "aR")]] +
                    ([tG[i - 2]] if (i - 2) in tG else []))

            def tp():
                i = st["t"]
                st["t"] += 1
                dst_cm, dt0, dst_guard, done = descs.pop(i)
                px = i % 2
                for jj in range(4):
                    bank = 6 + jj % 2
                    key = (i, jj - 2) if jj >= 2 else (i - 1, jj + 2)
                    wl = [tok[(i, "vN")]]
                    if key in vEv:
                        wl.append(vEv[key])
                    if dst_guard is not None:
                        wl.append(dst_guard)

                    def tfn(t, jj=jj, px=px, bank=bank):
                        ins = None
                        for j in range(4):
                            ins = t.transpose(
                                ptb(bank)[:, j * 128:(j + 1) * 128],
                                XT[px][:, (jj * 4 + j) * 128:(jj * 4 + j + 1) * 128],
                                IDENT[:])
                        return ins
                    tG[i] = op("tensor", tfn, waits=wl, inc=True)
                    vEv[(i, jj)] = op("vector", lambda v, jj=jj, dt0=dt0, bank=bank: v.tensor_copy(
                        dst_cm[:, jj * 4:jj * 4 + 4, dt0 * 128:(dt0 + 1) * 128],
                        ptb(bank).rearrange("p (j q) -> p j q", j=8)[:, 0:4, :]),
                        waits=[tG[i]])
                last = vEv[(i, 3)]
                if done is not None:
                    done(last)
                return last
            return st, feed, stats, tp

        def emit_softplus(dst, srcv, tmp1, tmp2, mask_view=None, waits=()):
            t1 = op("vector", lambda v: v.tensor_scalar(
                tmp1, srcv, DTBIAS[:], None, ALU.add), waits=list(waits))
            a1 = op("scalar", lambda a: a.activation(tmp2, tmp1, AF.Abs), waits=[t1])
            op("scalar", lambda a: a.activation(tmp2, tmp2, AF.Exp, scale=-1.0))
            t2 = op("scalar", lambda a: a.activation(tmp2, tmp2, AF.Ln, bias=ONEC[0:64, :]))
            op("vector", lambda v: v.tensor_scalar(tmp1, tmp1, 0.0, None, ALU.max),
               waits=[a1])
            if mask_view is None:
                aw = op("vector", lambda v: v.tensor_tensor(dst, tmp1, tmp2, ALU.add),
                        waits=[t2])
            else:
                op("vector", lambda v: v.tensor_tensor(tmp1, tmp1, tmp2, ALU.add),
                   waits=[t2])
                aw = op("vector", lambda v: v.tensor_tensor(dst, tmp1, mask_view, ALU.mult))
            return aw

        DTM = sbm("DTM", [64, TM], F32)
        es_a = ExitStack()

        def sba(name, shape, dtype):
            return es_a.enter_context(nc.sbuf_tensor(name, shape, dtype))

        XN = sba("XN", [128, 16, TL], BF16)
        X1 = XN[:, :, HALO:HALO + TM].rearrange("p a b -> p a b")
        WPDT = sba("WPDT", [128, 16, 64], BF16)
        DCBA = sba("DCBA", [128, NCHB, 64], F32)

        with ExitStack() as esp:
            def sbp(name, shape, dtype):
                return esp.enter_context(nc.sbuf_tensor(name, shape, dtype))

            lnst, ln_feed, ln_stats, ln_tp = make_ln("c", sbp)
            XNP = [sbp(f"XNP{i}", [128, 16, LCP], BF16) for i in range(2)]
            WPF = [sbp(f"WPF{i}", [128, 16, 128], BF16) for i in range(3)]
            ZEVP = [sbp(f"ZEVP{i}", [128, 512], BF16) for i in range(2)]
            DTP = sbp("DTP", [64, LCPB], F32)
            DAPC = sbp("DAPC", [64, LCPB], F32)
            CSP = sbp("CSP", [64, LCPB], F32)
            DSW = sbp("DSW", [64, LCPB], F32)
            DTMK = sbp("DTMK", [64, LCPB], BF16)
            ZER64 = sbp("ZER64", [64, LCPB], BF16)
            WBFP = sbp("WBFP", [64, LCPB], BF16)
            DCC = sbp("DCC", [64, 1], BF16)

            wPD = dma("gpsimd", WPDT[:], d_windt)
            op("vector", lambda v: v.memset(SST[:], 0.0))
            op("vector", lambda v: v.memset(ZER64[:], 0.0))

            bank_tok = {i: None for i in range(8)}
            abk = [0]
            wpf_free = {0: [], 1: [], 2: []}
            wb_tok = {}
            zevp_free = {0: None, 1: None}
            ln_guard = {0: None, 1: None}
            dtp_guard = [None]
            dtp_evac = {}
            aDt_prev = [None]
            dcc_prev = [None]
            dcbb_prev = [None]
            wbf_free = [None]
            lnv = {}

            tiles_pri = deque()
            tiles_main = deque()
            main_pos = [0]

            def push_lnc(c):
                pc = c % 2
                nt = LCP // 128
                for tix in range(nt):
                    def f(c=c, pc=pc, tix=tix, last=(tix == nt - 1)):
                        def done(tokl, c=c):
                            lnv[c] = tokl
                        ln_feed(d_upfx, c * LCP + tix * 128, XNP[pc], tix,
                                dst_guard=ln_guard[pc],
                                done=done if last else None)
                    tiles_pri.append(f)

            def push_main():
                def f():
                    ln_feed(d_u, main_pos[0] * 128, XN, main_pos[0])
                    main_pos[0] += 1
                tiles_main.append(f)

            def feed_one():
                if tiles_pri:
                    tiles_pri.popleft()()
                    return True
                if tiles_main:
                    tiles_main.popleft()()
                    return True
                return False

            def drain_one():
                if lnst["t"] < lnst["s"] - 1:
                    ln_tp()
                    return True
                if lnst["s"] < lnst["f"]:
                    ln_stats()
                    return True
                if feed_one():
                    return True
                if lnst["t"] < lnst["s"]:
                    ln_tp()
                    return True
                return False

            def pump():
                if lnst["t"] < lnst["s"] - 1:
                    ln_tp()
                if lnst["s"] < lnst["f"]:
                    ln_stats()
                feed_one()

            def emit_mms(c):
                pc = c % 2
                vLN = lnv[c]
                last_mm = None
                for idx, mi in enumerate([16] + list(range(16))):
                    if mi < 16:
                        wp = mi % 3
                        if c == 0:
                            wW = [dma("gpsimd", WPF[wp][:], d_win[32 + mi],
                                      waits=wpf_free[wp])]
                            wb_tok[mi] = dma("sync", wpfx_scr[mi], WPF[wp][:],
                                             waits=wW)
                        else:
                            wl = list(wpf_free[wp])
                            if c == 1:
                                wl.append(wb_tok[mi])
                            wW = [dma("sync", WPF[wp][:, 0:8, :],
                                      wpfx_scr[mi][:, 0:8, :], waits=wl),
                                  dma("sync", WPF[wp][:, 8:16, :],
                                      wpfx_scr[mi][:, 8:16, :], waits=wl)]
                    else:
                        wW = [wPD]
                    bank = abk[0] % 4
                    abk[0] += 1
                    wl = wW + [vLN]
                    if bank_tok[bank] is not None:
                        wl.append(bank_tok[bank])

                    def mmf(t, mi=mi, bank=bank, pc=pc):
                        ins = None
                        for k in range(16):
                            lhs = WPF[mi % 3][:, k, :] if mi < 16 else WPDT[:, k, :]
                            mrows = 128 if mi < 16 else 64
                            ins = t.matmul(PS[bank][0:mrows, :], lhs,
                                           XNP[pc][:, k, :],
                                           start=(k == 0), stop=(k == 15))
                        return ins
                    tMM = op("tensor", mmf, waits=wl, inc=True)
                    if mi < 16:
                        wpf_free[mi % 3] = [tMM]
                        if c == 0:
                            wpf_free[mi % 3].append(wb_tok[mi])
                        zp = mi % 2
                        wl2 = [tMM]
                        if zevp_free[zp] is not None:
                            wl2.append(zevp_free[zp])
                        aE = op("scalar", lambda a, zp=zp, bank=bank: a.copy(
                            ZEVP[zp][:], PS[bank][:, :]), waits=wl2)
                        bank_tok[bank] = aE
                        zevp_free[zp] = dma(
                            "sync" if c == 0 else "gpsimd",
                            xbcpre_scr[mi * 128:(mi + 1) * 128,
                                       c * LCP:(c + 1) * LCP],
                            ZEVP[zp][:], waits=[aE])
                    else:
                        ho = (c % 2) * 512
                        wl2 = [tMM]
                        if dtp_guard[0] is not None:
                            wl2.append(dtp_guard[0])
                        aD = op("scalar", lambda a, ho=ho, bank=bank: a.copy(
                            DTP[:, ho:ho + 512], PS[bank][0:64, :]), waits=wl2)
                        bank_tok[bank] = aD
                        dtp_evac[c] = aD
                    last_mm = tMM
                    if idx in (2, 5, 8, 11, 14, 16):
                        pump()
                ln_guard[pc] = last_mm

            def emit_dt(cp):
                wMK = dma("gpsimd", DTMK[:],
                          d_dtmask[:, cp * LCPB:(cp + 1) * LCPB],
                          waits=[aDt_prev[0]] if aDt_prev[0] else [])
                aDt = emit_softplus(
                    DTP[:], DTP[:], DAPC[:], CSP[:], mask_view=DTMK[:],
                    waits=[dtp_evac[2 * cp], dtp_evac[2 * cp + 1], wMK])
                aDt_prev[0] = aDt
                op("vector", lambda v: v.tensor_scalar(
                    DAPC[:], DTP[:], AH[:], None, ALU.mult))
                vCS = op("vector", lambda v: v.tensor_tensor_scan(
                    CSP[:], DAPC[:], ZER64[:], 0.0, ALU.add, ALU.add))
                vD1 = op("vector", lambda v: v.tensor_tensor(
                    DSW[:], CSP[:, LCPB - 1:LCPB].broadcast_to([64, LCPB]),
                    CSP[:], ALU.subtract))
                aD2 = op("scalar", lambda a: a.activation(DSW[:], DSW[:], AF.Exp),
                         waits=[vD1])
                vD3 = op("vector", lambda v: v.tensor_tensor(
                    DSW[:], DSW[:], DTP[:], ALU.mult), waits=[aD2])
                dtp_guard[0] = vD3
                vWBF = op("vector", lambda v: v.tensor_copy(WBFP[:], DSW[:]),
                          waits=[wbf_free[0]] if wbf_free[0] else [])
                wbf_free[0] = dma("gpsimd", wbf_scr[:, cp * LCPB:(cp + 1) * LCPB],
                                  WBFP[:], waits=[vWBF])
                aDC = op("scalar", lambda a: a.activation(
                    DCC[:], CSP[:, LCPB - 1:LCPB], AF.Exp),
                    waits=[vCS] + ([dcc_prev[0]] if dcc_prev[0] else []))
                wDC0 = dma("gpsimd", dcc_scr, DCC[:], waits=[aDC])
                dcc_prev[0] = wDC0
                wl = [wDC0]
                if dcbb_prev[0] is not None:
                    wl.append(dcbb_prev[0])
                wDB = dma("gpsimd", DCBB[:],
                          dcc_scr.rearrange("a b -> b a").broadcast_to([128, 64]),
                          waits=wl)
                dcbb_prev[0] = op("vector", lambda v, cp=cp: v.tensor_copy(
                    DCBA[:, cp, :], DCBB[:]), waits=[wDB])

            push_lnc(0)
            push_lnc(1)
            for _ in range(TL // 128):
                push_main()
            for _ in range(3):
                feed_one()
            for c in range(NCHP):
                while lnv.get(c) is None:
                    if not drain_one():
                        break
                emit_mms(c)
                if c % 2 == 1:
                    emit_dt(c // 2)
                if c + 2 < NCHP:
                    push_lnc(c + 2)
            while (tiles_pri or tiles_main or lnst["s"] < lnst["f"]
                   or lnst["t"] < lnst["s"]):
                if not drain_one():
                    break
            barrier()

        if "xn" in dbg:
            t = dbg_tensor("xn", [128, 16 * TL], BF16)
            dma("sync", t.rearrange("p (a b) -> p a b", a=16), XN[:])
            barrier()
        if "pfxa" in dbg:
            t = dbg_tensor("xbp", [2048, PFX], BF16)
            dma("sync", t, xbcpre_scr)
            t2 = dbg_tensor("wbfd", [64, PFX], BF16)
            dma("sync", t2, wbf_scr)
            t3 = dbg_tensor("dcba", [128, NCHB * 64])
            dma("sync", t3.rearrange("p (a b) -> p a b", a=NCHB), DCBA[:])
            barrier()

        es_b2 = ExitStack()

        def sbb2(name, shape, dtype):
            return es_b2.enter_context(nc.sbuf_tensor(name, shape, dtype))

        XBPRE = sbb2("XBPRE", [128, 16, LCPB + 4], BF16)
        ACCP = [sbb2(f"ACCP{i}", [128, LCPB], F32) for i in range(2)]
        XTC8 = sbb2("XTC8", [128, LCPB // 128, 2, 128], BF16)
        XWG = sbb2("XWG", [128, LCPB // 128, 512], BF16)
        WTQ = sbb2("WTQ", [128, LCPB // 128, 64], BF16)
        WBF = sbb2("WBF", [64, LCPB], BF16)
        pbank = {i: None for i in range(8)}

        def pfxb_gen():
            vZB = op("vector", lambda v: v.memset(XBPRE[:, :, 0:4], 0.0))
            ld_guard = [vZB]
            wbf_guard = [None]
            wtq_tok = {i: None for i in range(LCPB // 128)}
            st_upd = {gb: None for gb in range(8)}
            accp_g = {0: None, 1: None}
            for cb in range(NCHB):
                wWBF = dma("sync", WBF[:],
                           wbf_scr[:, cb * LCPB:(cb + 1) * LCPB],
                           waits=[wbf_guard[0]] if wbf_guard[0] else [])
                ldt = []
                wl0 = [t for t in ld_guard if t is not None]
                for mi in range(16):
                    if cb == 0:
                        dst = XBPRE[:, mi, 4:4 + LCPB]
                        src = xbcpre_scr[mi * 128:(mi + 1) * 128, 0:LCPB]
                    else:
                        dst = XBPRE[:, mi, 1:4 + LCPB]
                        src = xbcpre_scr[mi * 128:(mi + 1) * 128,
                                         cb * LCPB - 3:(cb + 1) * LCPB]
                    ldt.append(dma("sync", dst, src,
                                   waits=wl0 if mi == 0 else []))
                yield
                for blk in range(LCPB // 128):
                    tb = 6 + blk % 2
                    wl2 = [wWBF]
                    if wtq_tok[blk] is not None:
                        wl2.append(wtq_tok[blk])
                    if pbank[tb] is not None:
                        wl2.append(pbank[tb])
                    tW = op("tensor", lambda t, blk=blk, tb=tb: t.transpose(
                        ptb(tb)[:, 256:320],
                        WBF[:, blk * 128:(blk + 1) * 128],
                        IDENT[0:64, 0:64]), waits=wl2, inc=True)
                    wtq_tok[blk] = op("vector", lambda v, blk=blk, tb=tb: v.tensor_copy(
                        WTQ[:, blk, :], ptb(tb)[:, 256:320]), waits=[tW])
                    pbank[tb] = wtq_tok[blk]
                vConvDone = {}
                cst = {"vCl": None}

                def conv_tile(mi, cb=cb, ldt=ldt):
                    ap = mi % 2
                    wl = [ldt[mi]]
                    if accp_g[ap] is not None:
                        wl.append(accp_g[ap])
                    if cb == 0:
                        wl.append(vZB)
                    op("vector", lambda v, mi=mi, ap=ap: v.tensor_scalar(
                        ACCP[ap][:], XBPRE[:, mi, 1:1 + LCPB], CONVW[:, mi, 0:1],
                        None, ALU.mult), waits=wl)
                    vCl = None
                    for k in range(1, 4):
                        vCl = op("vector", lambda v, mi=mi, k=k, ap=ap: v.scalar_tensor_tensor(
                            ACCP[ap][:], XBPRE[:, mi, 1 + k:1 + k + LCPB],
                            CONVW[:, mi, k:k + 1], ACCP[ap][:], ALU.mult, ALU.add))
                    vConvDone[mi] = op("scalar", lambda a, mi=mi, ap=ap: a.activation(
                        XBPRE[:, mi, 4:4 + LCPB], ACCP[ap][:], AF.Silu,
                        bias=CONVB[:, mi:mi + 1]), waits=[vCl])
                    accp_g[ap] = vConvDone[mi]
                    cst["vCl"] = vCl

                conv_tile(0)
                conv_tile(8)
                yield
                last_st = None
                for gb in range(8):
                    if gb + 1 < 8:
                        conv_tile(gb + 1)
                        conv_tile(9 + gb)
                    evs = []
                    for blk in range(LCPB // 128):
                        tb = 6 + blk % 2
                        wl = [vConvDone[gb], vConvDone[8 + gb]]
                        if pbank[tb] is not None:
                            wl.append(pbank[tb])

                        def tfn(t, gb=gb, blk=blk, tb=tb):
                            t.transpose(ptb(tb)[:, 0:128],
                                        XBPRE[:, gb, 4 + blk * 128:4 + (blk + 1) * 128],
                                        IDENT[:])
                            return t.transpose(
                                ptb(tb)[:, 128:256],
                                XBPRE[:, 8 + gb, 4 + blk * 128:4 + (blk + 1) * 128],
                                IDENT[:])
                        tT = op("tensor", tfn, waits=wl, inc=True)
                        pbank[tb] = op("vector", lambda v, blk=blk, tb=tb: v.tensor_copy(
                            XTC8[:, blk, :, :],
                            ptb(tb).rearrange("p (a q) -> p a q", a=8)[:, 0:2, :]),
                            waits=[tT])
                        evs.append(pbank[tb])
                    xw_toks = []
                    for blk in range(LCPB // 128):
                        xw_toks.append(op("vector", lambda v, gb=gb, blk=blk: v.tensor_tensor(
                            XWG[:, blk, :].rearrange("p (a b q) -> p a b q", a=4, b=2),
                            XTC8[:, blk, 0, :].rearrange("p (a q) -> p a q", a=2)
                            .unsqueeze(1).broadcast_to([128, 4, 2, 64]),
                            WTQ[:, blk, gb * 8:gb * 8 + 8]
                            .rearrange("p (a b) -> p a b", a=4)
                            .unsqueeze(3).broadcast_to([128, 4, 2, 64]),
                            ALU.mult), waits=[wtq_tok[blk], evs[blk]]))
                    bank = 4 + gb % 2
                    wl4 = [xw_toks[-1]]
                    if pbank[bank] is not None:
                        wl4.append(pbank[bank])

                    def mms(t, gb=gb, bank=bank):
                        ins = None
                        nbl = LCPB // 128
                        for blk in range(nbl):
                            ins = t.matmul(PS[bank][:, :], XTC8[:, blk, 1, :],
                                           XWG[:, blk, :],
                                           start=(blk == 0), stop=(blk == nbl - 1))
                        return ins
                    tSt = op("tensor", mms, waits=wl4, inc=True)
                    last_st = tSt
                    wl5 = [tSt]
                    if st_upd[gb] is not None:
                        wl5.append(st_upd[gb])
                    op("vector", lambda v, gb=gb, cb=cb: v.tensor_tensor(
                        SST[:, gb * 512:(gb + 1) * 512].rearrange("p (a q) -> p a q", a=8),
                        SST[:, gb * 512:(gb + 1) * 512].rearrange("p (a q) -> p a q", a=8),
                        DCBA[:, cb, gb * 8:gb * 8 + 8].unsqueeze(2).broadcast_to([128, 8, 64]),
                        ALU.mult), waits=wl5)
                    st_upd[gb] = op("vector", lambda v, gb=gb, bank=bank: v.tensor_tensor(
                        SST[:, gb * 512:(gb + 1) * 512],
                        SST[:, gb * 512:(gb + 1) * 512],
                        PS[bank][:, :], ALU.add))
                    pbank[bank] = st_upd[gb]
                    yield
                ld_guard[:] = [last_st, cst["vCl"]]
                wbf_guard[0] = wtq_tok[LCPB // 128 - 1]

        gen = pfxb_gen()
        bpump = [0]

        with ExitStack() as es3:
            def sb3(name, shape, dtype):
                return es3.enter_context(nc.sbuf_tensor(name, shape, dtype))

            WB = [sb3(f"WB{i}", [128, 16, 128], BF16) for i in range(2)]
            ZEV = [sb3(f"ZEV{i}", [128, 512], BF16) for i in range(4)]
            XBR = [sb3(f"XBR{i}", [128, TL], BF16) for i in range(2)]
            ACC3 = sb3("ACC3", [128, TM], F32)
            CVO = [sb3(f"CVO{i}", [128, TM], BF16) for i in range(2)]
            DTRW = sb3("DTRW", [64, TL], F32)

            bank_tok = {i: None for i in range(8)}

            strips5 = [(0, 512), (512, 512), (1024, 512), (1536, 512), (2048, 128)]
            w_free = {0: None, 1: None}
            zev_free = {0: None, 1: None, 2: None, 3: None}
            cvo_free = {0: None, 1: None}
            xbr_free = {0: None, 1: None}
            acc3_prev = [None]
            dt_evs = []
            for m in range(81):
                if m < 80:
                    wl = [w_free[m % 2]] if w_free[m % 2] else []
                    wW = dma("gpsimd", WB[m % 2][:], d_win[m], waits=wl)
                else:
                    wW = wPD
                mm_toks = []
                ev_toks = []
                for si, (o, w) in enumerate(strips5):
                    bank = (m * 5 + si) % 4
                    wl = [wW]
                    if bank_tok[bank] is not None:
                        wl.append(bank_tok[bank])
                    if m >= 32 and m < 80 and si == 0 and xbr_free[m % 2] is not None:
                        wl.append(xbr_free[m % 2])

                    def mmf(t, m=m, o=o, w=w, bank=bank):
                        ins = None
                        for k in range(16):
                            lhs = WB[m % 2][:, k, :] if m < 80 else WPDT[:, k, :]
                            mrows = 128 if m < 80 else 64
                            ins = t.matmul(PS[bank][0:mrows, 0:w], lhs,
                                           XN[:, k, o:o + w],
                                           start=(k == 0), stop=(k == 15))
                        return ins
                    tMM = op("tensor", mmf, waits=wl, inc=True)
                    mm_toks.append(tMM)
                    if m < 32:
                        zp = (m * 5 + si) % 4
                        wl2 = [tMM]
                        if zev_free[zp] is not None:
                            wl2.append(zev_free[zp])
                        aZ = op("scalar", lambda a, o=o, w=w, bank=bank, zp=zp: a.copy(
                            ZEV[zp][:, 0:w], PS[bank][:, 0:w]), waits=wl2)
                        bank_tok[bank] = aZ
                        zev_free[zp] = dma(
                            "sync", z_scr[m * 128:(m + 1) * 128, o:o + w],
                            ZEV[zp][:, 0:w], waits=[aZ])
                    elif m < 80:
                        aX = op("scalar", lambda a, m=m, o=o, w=w, bank=bank: a.copy(
                            XBR[m % 2][:, o:o + w], PS[bank][:, 0:w]), waits=[tMM])
                        bank_tok[bank] = aX
                        ev_toks.append(aX)
                    else:
                        aD = op("scalar", lambda a, o=o, w=w, bank=bank: a.copy(
                            DTRW[:, o:o + w], PS[bank][0:64, 0:w]), waits=[tMM])
                        bank_tok[bank] = aD
                        dt_evs.append(aD)
                w_free[m % 2] = mm_toks[-1]
                if 32 <= m < 80:
                    mi = m - 32
                    wl = list(ev_toks)
                    if cvo_free[m % 2] is not None:
                        wl.extend(cvo_free[m % 2])
                    if acc3_prev[0] is not None:
                        wl.append(acc3_prev[0])
                    op("vector", lambda v, m=m, mi=mi: v.tensor_scalar(
                        ACC3[:], XBR[m % 2][:, HALO - 3:HALO - 3 + TM],
                        CONVW[:, mi, 0:1], None, ALU.mult), waits=wl)
                    vC4 = None
                    for k in range(1, 4):
                        vC4 = op("vector", lambda v, m=m, mi=mi, k=k: v.scalar_tensor_tensor(
                            ACC3[:], XBR[m % 2][:, HALO - 3 + k:HALO - 3 + k + TM],
                            CONVW[:, mi, k:k + 1], ACC3[:], ALU.mult, ALU.add))
                    xbr_free[m % 2] = vC4
                    aCV = op("scalar", lambda a, m=m, mi=mi: a.activation(
                        CVO[m % 2][:, :], ACC3[:], AF.Silu, bias=CONVB[:, mi:mi + 1]),
                        waits=[vC4])
                    acc3_prev[0] = aCV
                    cvo_free[m % 2] = [
                        dma("sync", xbc_scr[mi * 128:(mi + 1) * 128, 0:1024],
                            CVO[m % 2][:, 0:1024], waits=[aCV]),
                        dma("sync", xbc_scr[mi * 128:(mi + 1) * 128, 1024:2048],
                            CVO[m % 2][:, 1024:2048], waits=[aCV])]
                if bpump[0] < PFXB_IN_PH3:
                    next(gen, None)
                    bpump[0] += 1
            if PFXB_IN_PH3 > 100:
                for _ in gen:
                    pass
            op("vector", lambda v: v.tensor_copy(ACC3[0:64, :], DTRW[:, HALO:]),
               waits=dt_evs + ([acc3_prev[0]] if acc3_prev[0] else []))
            emit_softplus(DTM[:], ACC3[0:64, :], ACC3[0:64, :], DTRW[:, 0:TM])
            barrier()

        if "sin" in dbg:
            t = dbg_tensor("sin", [128, NH * HEADDIM])
            dma("sync", t, SST[:])
            barrier()

        if "xbc" in dbg:
            t = dbg_tensor("xbc", [CONVD, TM], BF16)
            dma("sync", t, xbc_scr)
            t2 = dbg_tensor("zd", [DI, TL], BF16)
            dma("sync", t2, z_scr)
            t3 = dbg_tensor("dtm", [64, TM])
            dma("sync", t3, DTM[:])
            barrier()

        with ExitStack() as es4:
            def sb4(name, shape, dtype):
                return es4.enter_context(nc.sbuf_tensor(name, shape, dtype))

            HSG = [sb4(f"HSG{i}", [128, 128], BF16) for i in range(2)]
            BTMP = sb4("BTMP", [128, TM], BF16)
            bank_tok = {i: None for i in range(8)}
            hs_free = {0: None, 1: None}
            SC2048 = 1.0 / float(np.sqrt(2048.0))
            for g in range(16):
                wl = [hs_free[g % 2]] if hs_free[g % 2] else []
                vH = op("vector", lambda v, g=g: v.tensor_scalar(
                    HSG[g % 2][:], H128[:], SUUG[:, g:g + 1], None, ALU.mult),
                    waits=wl)
                for s4 in range(4):
                    bank = (g * 4 + s4) % 4
                    wl = [vH]
                    if bank_tok[bank] is not None:
                        wl.append(bank_tok[bank])
                    tMM = op("tensor", lambda t, g=g, s4=s4, bank=bank: t.matmul(
                        PS[bank][:, :], HSG[g % 2][:],
                        XN[:, g, HALO + s4 * 512:HALO + (s4 + 1) * 512],
                        start=True, stop=True), waits=wl, inc=True)
                    if s4 == 3:
                        hs_free[g % 2] = tMM
                    bank_tok[bank] = op("scalar", lambda a, g=g, s4=s4, bank=bank: a.activation(
                        X1[:, g, s4 * 512:(s4 + 1) * 512], PS[bank][:, :],
                        AF.Copy, scale=SC2048), waits=[tMM])
            aH128 = (S["A"], S["A"].n)
            prev = aH128
            for st4 in range(4):
                d = 1 << st4
                for g0 in range(16):
                    if g0 & d:
                        continue
                    g1 = g0 + d
                    op("vector", lambda v, g0=g0: v.tensor_copy(
                        BTMP[:], X1[:, g0, :]), waits=[prev])
                    op("vector", lambda v, g0=g0, g1=g1: v.tensor_tensor(
                        X1[:, g0, :], BTMP[:], X1[:, g1, :], ALU.add))
                    op("vector", lambda v, g0=g0, g1=g1: v.tensor_tensor(
                        X1[:, g1, :], BTMP[:], X1[:, g1, :], ALU.subtract))
                prev = (S["V"], S["V"].n)
            barrier()

        if "x1" in dbg:
            t = dbg_tensor("x1", [128, 16 * TM], BF16)
            dma("sync", t.rearrange("p (a b) -> p a b", a=16), X1[:])
            barrier()

        with ExitStack() as es5:
            def sb5(name, shape, dtype):
                return es5.enter_context(nc.sbuf_tensor(name, shape, dtype))

            WUP = [sb5(f"WUP{i}", [128, 16, 128], BF16) for i in range(2)]
            WGT = [sb5(f"WGT{i}", [128, 16, 128], BF16) for i in range(2)]
            UPT = [sb5(f"UPT{i}", [128, TM], BF16) for i in range(2)]
            GTT = [sb5(f"GTT{i}", [128, TM], BF16) for i in range(2)]
            HPB0 = sb5("HPB0", [128, TM], BF16)
            HPB = [HPB0, HPB0]
            H1T = [sb5(f"H1T{i}", [128, TM], BF16) for i in range(2)]
            bank_tok = {i: None for i in range(8)}
            w_free = {0: None, 1: None}
            buf_free = {0: None, 1: None}
            SC81 = 0.125
            for i in range(64):
                p = i % 2
                wl = [w_free[p]] if w_free[p] else []
                wU = dma("gpsimd", WUP[p][:], d_wup[i], waits=wl)
                wG = dma("gpsimd", WGT[p][:], d_wgate[i], waits=wl)
                st_toks = []

                def evac(s4, tU, tG, i=i, p=p):
                    bU, bG = s4 % 2, 2 + s4 % 2
                    aU = op("vector", lambda v, i=i, p=p, s4=s4, bU=bU: v.tensor_scalar(
                        UPT[p][:, s4 * 512:(s4 + 1) * 512], PS[bU][:, :],
                        WSCUP[:, i:i + 1], None, ALU.mult), waits=[tU])
                    aG = op("scalar", lambda a, i=i, p=p, s4=s4, bG=bG: a.activation(
                        GTT[p][:, s4 * 512:(s4 + 1) * 512], PS[bG][:, :],
                        AF.Silu, scale=WSCGATE[:, i:i + 1]), waits=[tG])
                    bank_tok[bG] = aG
                    vHP = op("vector", lambda v, i=i, p=p, s4=s4: v.scalar_tensor_tensor(
                        HPB[p][:, s4 * 512:(s4 + 1) * 512],
                        GTT[p][:, s4 * 512:(s4 + 1) * 512], SUDP[:, i:i + 1],
                        UPT[p][:, s4 * 512:(s4 + 1) * 512], ALU.mult, ALU.mult),
                        waits=[aU, aG])
                    tS1 = op("tensor", lambda t, p=p, s4=s4, bU=bU: t.matmul(
                        PS[bU][:, :], M1C[:],
                        HPB[p][:, s4 * 512:(s4 + 1) * 512],
                        start=True, stop=True), waits=[vHP, aU], inc=True)
                    aH1 = op("scalar", lambda a, p=p, s4=s4, bU=bU: a.activation(
                        H1T[p][:, s4 * 512:(s4 + 1) * 512], PS[bU][:, :],
                        AF.Copy, scale=SC81), waits=[tS1])
                    bank_tok[bU] = aH1
                    st_toks.append(aH1)

                pend = None
                last_g = None
                for s4 in range(4):
                    bU, bG = s4 % 2, 2 + s4 % 2
                    wl = [wU]
                    if bank_tok[bU] is not None:
                        wl.append(bank_tok[bU])
                    if s4 == 0 and buf_free[p] is not None:
                        wl.append(buf_free[p])

                    def mmu(t, i=i, p=p, s4=s4, bU=bU):
                        ins = None
                        for k in range(16):
                            ins = t.matmul(PS[bU][:, :], WUP[p][:, k, :],
                                           X1[:, k, s4 * 512:(s4 + 1) * 512],
                                           start=(k == 0), stop=(k == 15))
                        return ins
                    tU = op("tensor", mmu, waits=wl, inc=True)
                    wl = [wG]
                    if bank_tok[bG] is not None:
                        wl.append(bank_tok[bG])

                    def mmg(t, i=i, p=p, s4=s4, bG=bG):
                        ins = None
                        for k in range(16):
                            ins = t.matmul(PS[bG][:, :], WGT[p][:, k, :],
                                           X1[:, k, s4 * 512:(s4 + 1) * 512],
                                           start=(k == 0), stop=(k == 15))
                        return ins
                    tG = op("tensor", mmg, waits=wl, inc=True)
                    last_g = tG
                    if pend is not None:
                        pend()
                    pend = (lambda s4=s4, tU=tU, tG=tG: evac(s4, tU, tG))
                pend()
                w_free[p] = last_g
                buf_free[p] = dma("sync", h1_scr[i * 128:(i + 1) * 128, :],
                                  H1T[p][:], waits=[st_toks[-1]])
                next(gen, None)
            for _ in gen:
                pass
            barrier()
        es_b2.close()
        es_a.close()

        if "h1" in dbg:
            t = dbg_tensor("h1", [INTER, TM], BF16)
            dma("sync", t, h1_scr)
            barrier()

        with ExitStack() as es6:
            def sb6(name, shape, dtype):
                return es6.enter_context(nc.sbuf_tensor(name, shape, dtype))

            CSD = sb6("CSD", [64, TM], F32)
            ECSR = sb6("ECSR", [64, TM], BF16)
            WMB = sb6("WMB", [64, TM], BF16)
            CTMP = sb6("CTMP", [64, LC], F32)
            CTM2 = sb6("CTM2", [64, LC], F32)
            DCALL = sb6("DCALL", [64, NCH], BF16)
            Z64 = sb6("Z64", [64, LC], F32)
            XSL = [sb6(f"XSL{i}", [128, 8, 256], BF16) for i in range(2)]
            BSL = [sb6(f"BSL{i}", [128, 8, 256], BF16) for i in range(2)]
            CCB = sb6("CCB", [128, 32, 256], BF16)
            XTT = sb6("XTT", [128, 8, 128], BF16)
            BTT = sb6("BTT", [128, 8, 128], BF16)
            WTT = sb6("WTT", [128, 64], BF16)
            CST = sb6("CST", [128, 64], F32)
            CSREP0 = sb6("CSREP0", [128, 2048], F32)
            CSREP = [CSREP0, CSREP0]
            ECSREP = [sb6(f"ECSREP{i}", [128, 2048], BF16) for i in range(2)]
            T1B0 = sb6("T1B0", [128, 2048], F32)
            T1B = [T1B0, T1B0]
            LTB = [sb6(f"LTB{i}", [128, 2048], BF16) for i in range(2)]
            GTS = [sb6(f"GTS{i}", [128, 1024], BF16) for i in range(2)]
            GLB = [sb6(f"GLB{i}", [128, 2048], BF16) for i in range(2)]
            CSBQ = [sb6(f"CSBQ{i}", [128, 2048], BF16) for i in range(2)]
            XWS = [sb6(f"XWS{i}", [128, 4, 2, 64], BF16) for i in range(2)]
            SSB = sb6("SSB", [128, NH * HEADDIM], BF16)
            Y4 = sb6("Y4", [128, 32, 256], BF16)
            DCC6 = sb6("DCC6", [64, 1], BF16)
            DCB6 = sb6("DCB6", [128, 64], F32)
            DIALL = sb6("DIALL", [128, NH * 128], BF16)
            DBRD = sb6("DBRD", [128, 64], BF16)

            tD = op("tensor", lambda t: t.matmul(PS[0][:, 0:64], ONES1[:], DROW[:],
                                                 start=True, stop=True),
                    waits=[wG0], inc=True)
            vDB = op("vector", lambda v: v.tensor_copy(DBRD[:], PS[0][:, 0:64]),
                     waits=[tD])
            op("vector", lambda v: v.tensor_tensor(
                DIALL[:].rearrange("p (h l) -> p h l", h=64),
                IDENT[:].unsqueeze(1).broadcast_to([128, 64, 128]),
                DBRD[:].unsqueeze(2).broadcast_to([128, 64, 128]),
                ALU.mult))

            vZ = op("vector", lambda v: v.memset(Z64[:], 0.0))
            aLDT = op("scalar", lambda a: a.activation(CSD[:], DTM[:], AF.Ln))
            wCS = None
            cs_spill = {}
            ecs_spill = {}
            for c in range(NCH):
                slc = slice(c * LC, (c + 1) * LC)
                op("vector", lambda v, slc=slc: v.tensor_scalar(
                    CTM2[:], DTM[:, slc], AH[:], None, ALU.mult))
                vC = op("vector", lambda v: v.tensor_tensor_scan(
                    CTMP[:], CTM2[:], Z64[:], 0.0, ALU.add, ALU.add),
                    waits=[vZ] + ([wCS] if wCS else []))
                wCS = dma("sync",
                          csm_scr[c * 4:c * 4 + 4, :].rearrange("q (h l) -> (q h) l", h=16),
                          CTMP[:], waits=[vC])
                cs_spill[c] = wCS
                op("scalar", lambda a, slc=slc: a.activation(
                    ECSR[:, slc], CTMP[:], AF.Exp), waits=[vC])
                wEC = dma("sync",
                          ecs_scr[c * 4:c * 4 + 4, :].rearrange("q (h l) -> (q h) l", h=16),
                          ECSR[:, slc], waits=[(S["A"], S["A"].n)])
                ecs_spill[c] = wEC
                op("scalar", lambda a, c=c: a.activation(
                    DCALL[:, c:c + 1], CTMP[:, LC - 1:LC], AF.Exp))
                op("vector", lambda v: v.tensor_tensor(
                    CTM2[:], CTMP[:, LC - 1:LC].broadcast_to([64, LC]),
                    CTMP[:], ALU.subtract))
                aD = op("scalar", lambda a: a.activation(CTM2[:], CTM2[:], AF.Exp),
                        waits=[(S["V"], S["V"].n)])
                op("vector", lambda v, slc=slc: v.tensor_tensor(
                    WMB[:, slc], CTM2[:], DTM[:, slc], ALU.mult), waits=[aD])
                op("vector", lambda v, slc=slc: v.tensor_tensor(
                    CSD[:, slc], CTMP[:], CSD[:, slc], ALU.subtract), waits=[aLDT])
            vWMB = (S["V"], S["V"].n)
            vCSD = vWMB
            wECS = wEC

            bank_tok = {i: None for i in range(8)}
            bank_tok[0] = vDB
            slab_tok = {0: None, 1: None}
            slab_used = {0: None, 1: None}
            cc_tok = None
            cc_used = None
            rep_used = {0: None, 1: None}
            y4_used = None
            xw_used = {0: None, 1: None}
            dcc_prev = None
            dcbb_prev6 = None
            y_last_prev = None
            glb_tok = {}
            cs_q_tok = {}
            lt_tok = {}
            gts_tok = {}
            rep_tok = {}
            t1_tok = {}

            def stage_rep(c, q4):
                qp = q4 % 2
                prev_t1 = t1_tok.get((c, q4 - 1) if q4 >= 1 else (c - 1, 3))
                wl = [cs_spill[c]] + ([prev_t1] if prev_t1 else [])
                r1 = dma("sync", CSREP[qp][:],
                         csm_scr[c * 4 + q4:c * 4 + q4 + 1, :]
                         .broadcast_to([128, 2048]), waits=wl)
                prev_cs = cs_q_tok.get((c, q4 - 2) if q4 >= 2 else (c - 1, q4 + 2))
                wl2 = [ecs_spill[c]] + ([prev_cs] if prev_cs else [])
                r2 = dma("sync", ECSREP[qp][:],
                         ecs_scr[c * 4 + q4:c * 4 + q4 + 1, :]
                         .broadcast_to([128, 2048]), waits=wl2)
                rep_tok[(c, q4)] = [r1, r2]

            def stage_t1(c, q4, vT3):
                qp = q4 % 2
                wl = rep_tok[(c, q4)] + [vT3]
                if (c, q4, "lt") in lt_tok:
                    pass
                prev = lt_tok.get((c, q4 - 1) if q4 >= 1 else (c - 1, 3))
                if prev is not None:
                    wl.append(prev)
                op("vector", lambda v, c=c, q4=q4, qp=qp: v.tensor_tensor(
                    T1B[qp][:].rearrange("p (h l) -> p h l", h=16),
                    CSREP[qp][:].rearrange("p (h l) -> p h l", h=16),
                    CST[:, 16 * q4:16 * q4 + 16].unsqueeze(2)
                    .broadcast_to([128, 16, 128]), ALU.subtract), waits=wl)
                t1_tok[(c, q4)] = op("vector", lambda v, qp=qp: v.tensor_tensor(
                    T1B[qp][:].rearrange("p (h l) -> p h l", l=128),
                    T1B[qp][:].rearrange("p (h l) -> p h l", l=128),
                    SLNEG[:].unsqueeze(1).broadcast_to([128, 16, 128]), ALU.add))
                lt_tok[(c, q4)] = op("scalar", lambda a, qp=qp: a.activation(
                    LTB[qp][:], T1B[qp][:], AF.Exp), waits=[t1_tok[(c, q4)]])

            def stage_gt(c, q4, p2, csl):
                qp = q4 % 2
                gt_mm = []
                for gg in range(8):
                    gc = q4 * 8 + gg
                    bank = gg % 2
                    slot = gg // 2
                    wl = list(cc_tok)
                    if slot == 0 and bank_tok[bank] is not None:
                        wl.append(bank_tok[bank])
                    gt_mm.append(op("tensor", lambda t, gc=gc, bank=bank, slot=slot, p2=p2, csl=csl: t.matmul(
                        PS[bank][:, slot * 128:(slot + 1) * 128],
                        BSL[p2][:, (gc // 4) % 8, csl:csl + 128],
                        CCB[:, gc, csl:csl + 128],
                        start=True, stop=True), waits=wl, inc=True))
                prevg = gts_tok.get((c, q4 - 2) if q4 >= 2 else (c - 1, q4 + 2))
                wl = [gt_mm[6]] + ([prevg] if prevg else [])
                aG1 = op("scalar", lambda a, qp=qp: a.activation(
                    GTS[qp][:].rearrange("p (k two l) -> p k two l", two=2, l=128)[:, :, 0, :],
                    PS[0][:].rearrange("p (k l) -> p k l", k=4), AF.Copy), waits=wl)
                aG2 = op("scalar", lambda a, qp=qp: a.activation(
                    GTS[qp][:].rearrange("p (k two l) -> p k two l", two=2, l=128)[:, :, 1, :],
                    PS[1][:].rearrange("p (k l) -> p k l", k=4), AF.Copy),
                    waits=[gt_mm[7]])
                bank_tok[0] = aG1
                bank_tok[1] = aG2
                gts_tok[(c, q4)] = aG2

            def stage_mid(c, q4, p2, csl, vSSB, vT2, trx):
                qp = q4 % 2
                wl = [lt_tok[(c, q4)], gts_tok[(c, q4)], bank_tok[0]]
                prev = glb_tok.get((c, q4 - 2) if q4 >= 2 else (c - 1, q4 + 2))
                if prev is not None:
                    wl.append(prev)
                op("vector", lambda v, qp=qp: v.tensor_tensor(
                    GLB[qp][:].rearrange("p (h two l) -> p h two l", two=2, l=128),
                    LTB[qp][:].rearrange("p (h two l) -> p h two l", two=2, l=128),
                    GTS[qp][:].rearrange("p (h l) -> p h l", h=8).unsqueeze(2)
                    .broadcast_to([128, 8, 2, 128]), ALU.mult), waits=wl)
                glb_tok[(c, q4)] = op("vector", lambda v, q4=q4, qp=qp: v.tensor_tensor(
                    GLB[qp][:], GLB[qp][:],
                    DIALL[:, q4 * 2048:(q4 + 1) * 2048], ALU.add))
                wl = rep_tok[(c, q4)][1:]
                prev = cs_q_tok.get((c, q4 - 2) if q4 >= 2 else (c - 1, q4 + 2))
                if prev is not None:
                    wl.append(prev)
                cs_q_tok[(c, q4)] = op("vector", lambda v, q4=q4, qp=qp, csl=csl: v.tensor_tensor(
                    CSBQ[qp][:].rearrange("p (h two l) -> p h two l", two=2, l=128),
                    CCB[:, q4 * 8:(q4 + 1) * 8, csl:csl + 128].unsqueeze(2)
                    .broadcast_to([128, 8, 2, 128]),
                    ECSREP[qp][:].rearrange("p (h two l) -> p h two l", two=2, l=128),
                    ALU.mult), waits=wl)
                st_toks = {}
                for gg in range(2):
                    gb = 2 * q4 + gg
                    bank = 2 + gg
                    xwp = gg
                    wl = [vT2, trx]
                    if xw_used[xwp] is not None:
                        wl.append(xw_used[xwp])
                    vXW = op("vector", lambda v, gb=gb, xwp=xwp, c=c: v.tensor_tensor(
                        XWS[xwp][:],
                        XTT[:, gb, :].rearrange("p (a q) -> p a q", a=2)
                        .unsqueeze(1).broadcast_to([128, 4, 2, 64]),
                        WTT[:, gb * 8:gb * 8 + 8]
                        .rearrange("p (a b) -> p a b", a=4)
                        .unsqueeze(3).broadcast_to([128, 4, 2, 64]),
                        ALU.mult), waits=wl)
                    wl = [vXW]
                    if bank_tok[bank] is not None:
                        wl.append(bank_tok[bank])
                    tST = op("tensor", lambda t, gb=gb, bank=bank, xwp=xwp: t.matmul(
                        PS[bank][:, :], BTT[:, gb, :],
                        XWS[xwp][:].rearrange("p a b q -> p (a b q)"),
                        start=True, stop=True), waits=wl, inc=True)
                    st_toks[gg] = tST
                    xw_used[xwp] = tST
                return st_toks

            def stage_y(c, q4, csl, vSSB, vDCB, st_toks):
                qp = q4 % 2
                y_toks = []
                for gg in range(8):
                    g = q4 * 8 + gg
                    bank = 6 + gg % 2
                    slot = gg // 2
                    h0, h1 = 2 * g, 2 * g + 1
                    wl = [glb_tok[(c, q4)], cs_q_tok[(c, q4)], vSSB]
                    if slot == 0:
                        wl.append(bank_tok[bank])

                    def ymm(t, gg=gg, bank=bank, slot=slot, h0=h0, h1=h1, qp=qp):
                        xs0 = XTT[:, h0 // 8, (h0 % 2) * 64:(h0 % 2) * 64 + 64]
                        xs1 = XTT[:, h1 // 8, (h1 % 2) * 64:(h1 % 2) * 64 + 64]
                        sl = slice(slot * 128, (slot + 1) * 128)
                        t.matmul(PS[bank][0:64, sl], xs0,
                                 GLB[qp][:, (gg * 2) * 128:(gg * 2 + 1) * 128],
                                 start=True, stop=False, tile_position=(0, 0))
                        t.matmul(PS[bank][0:64, sl], SSB[:, h0 * 64:h0 * 64 + 64],
                                 CSBQ[qp][:, (gg * 2) * 128:(gg * 2 + 1) * 128],
                                 start=False, stop=True, tile_position=(0, 0))
                        t.matmul(PS[bank][64:128, sl], xs1,
                                 GLB[qp][:, (gg * 2 + 1) * 128:(gg * 2 + 2) * 128],
                                 start=True, stop=False, tile_position=(0, 64))
                        return t.matmul(PS[bank][64:128, sl],
                                        SSB[:, h1 * 64:h1 * 64 + 64],
                                        CSBQ[qp][:, (gg * 2 + 1) * 128:(gg * 2 + 2) * 128],
                                        start=False, stop=True,
                                        tile_position=(0, 64))
                    y_toks.append(op("tensor", ymm, waits=wl, inc=True))
                for bb in range(2):
                    bank = 6 + bb
                    wl = [y_toks[6 + bb]]
                    if y4_used is not None:
                        wl.extend(y4_used)
                    aY = op("scalar", lambda a, q4=q4, bb=bb, bank=bank, csl=csl: a.activation(
                        Y4[:].rearrange("p (a b2) q -> p a b2 q", b2=2)
                        [:, q4 * 4:(q4 + 1) * 4, bb, csl:csl + 128],
                        PS[bank][:].rearrange("p (j q) -> p j q", j=4),
                        AF.Copy), waits=wl)
                    bank_tok[bank] = aY
                for gg in range(2):
                    gb = 2 * q4 + gg
                    bank = 2 + gg
                    wl = [vDCB, st_toks[gg], vSSB]
                    op("vector", lambda v, gb=gb: v.tensor_tensor(
                        SST[:, gb * 512:(gb + 1) * 512].rearrange("p (a q) -> p a q", a=8),
                        SST[:, gb * 512:(gb + 1) * 512].rearrange("p (a q) -> p a q", a=8),
                        DCB6[:, gb * 8:gb * 8 + 8].unsqueeze(2).broadcast_to([128, 8, 64]),
                        ALU.mult), waits=wl)
                    bank_tok[bank] = op("vector", lambda v, gb=gb, bank=bank: v.tensor_tensor(
                        SST[:, gb * 512:(gb + 1) * 512],
                        SST[:, gb * 512:(gb + 1) * 512],
                        PS[bank][:, :], ALU.add))
                return y_toks[-1]

            stage_rep(0, 0)
            for c in range(NCH):
                c2 = c // 2
                p2 = c2 % 2
                csl = 128 * (c % 2)
                if c % 2 == 0:
                    wl = [slab_used[p2]] if slab_used[p2] else []
                    for xg in range(8):
                        dma("gpsimd", XSL[p2][:, xg, :],
                            xbc_scr[xg * 128:(xg + 1) * 128, c * LC:(c + 2) * LC],
                            waits=wl if xg == 0 else [])
                    for xg in range(8):
                        dma("gpsimd", BSL[p2][:, xg, :],
                            xbc_scr[1024 + xg * 128:1024 + (xg + 1) * 128,
                                    c * LC:(c + 2) * LC])
                    wl = [cc_used] if cc_used else []
                    for gc in range(32):
                        dma("gpsimd", CCB[:, gc, :],
                            xbc_scr[2048 + gc * 128:2048 + (gc + 1) * 128,
                                    c * LC:(c + 2) * LC],
                            waits=wl if gc == 0 else [])
                    slab_all = B.dma_tokens("gpsimd")
                    slab_tok[p2] = slab_all
                    cc_tok = slab_all
                trev = []
                for half in range(4):
                    bank = 6 + half % 2
                    wl = list(slab_tok[p2])
                    if bank_tok[bank] is not None:
                        wl.append(bank_tok[bank])
                    if y_last_prev is not None:
                        wl.append(y_last_prev)

                    def tfn(t, half=half, p2=p2, csl=csl, bank=bank):
                        ins = None
                        src_t = XSL[p2] if half < 2 else BSL[p2]
                        for j in range(4):
                            xg = (half % 2) * 4 + j
                            ins = t.transpose(
                                ptb(bank)[:, j * 128:(j + 1) * 128],
                                src_t[:, xg, csl:csl + 128], IDENT[:])
                        return ins
                    tT = op("tensor", tfn, waits=wl, inc=True)

                    def evf(v, half=half, bank=bank):
                        dst = XTT if half < 2 else BTT
                        gi = (half % 2) * 4
                        return v.tensor_copy(
                            dst[:, gi:gi + 4, :],
                            ptb(bank).rearrange("p (j q) -> p j q", j=8)[:, 0:4, :])
                    bank_tok[bank] = op("vector", evf, waits=[tT])
                    trev.append(bank_tok[bank])
                trx = trev[-1]
                wl = [vWMB] + ([bank_tok[6]] if bank_tok[6] else [])
                t2 = op("tensor", lambda t, c=c: t.transpose(
                    ptb(6)[:, 128:192], WMB[:, c * LC:(c + 1) * LC], IDENT[0:64, 0:64]),
                    waits=wl, inc=True)
                vT2 = op("vector", lambda v: v.tensor_copy(WTT[:], ptb(6)[:, 128:192]),
                         waits=[t2])
                bank_tok[6] = vT2
                wl = [vCSD] + ([bank_tok[7]] if bank_tok[7] else [])
                tCS = op("tensor", lambda t, c=c: t.transpose(
                    PS[7][:, 0:64], CSD[:, c * LC:(c + 1) * LC], IDF[0:64, 0:64]),
                    waits=wl, inc=True)
                vT3 = op("vector", lambda v: v.tensor_copy(CST[:], PS[7][:, 0:64]),
                         waits=[tCS])
                bank_tok[7] = vT3
                wl = [vWMB] + ([dcc_prev] if dcc_prev else [])
                aDC = op("scalar", lambda a, c=c: a.copy(
                    DCC6[:], DCALL[:, c:c + 1]), waits=wl)
                wD0 = dma("sync", dcc_scr, DCC6[:], waits=[aDC])
                dcc_prev = wD0
                wl = [wD0] + ([dcbb_prev6] if dcbb_prev6 else [])
                wDB6 = dma("sync", DCBB[:],
                           dcc_scr.rearrange("a b -> b a").broadcast_to([128, 64]),
                           waits=wl)
                vDCB = op("vector", lambda v: v.tensor_copy(DCB6[:], DCBB[:]),
                          waits=[wDB6])
                dcbb_prev6 = vDCB
                wl = [(S["V"], S["V"].n)] + ([y_last_prev] if y_last_prev else [])
                vSSB = op("vector", lambda v: v.tensor_copy(SSB[:], SST[:]), waits=wl)
                stage_t1(c, 0, vT3)
                stage_gt(c, 0, p2, csl)
                for q4 in range(4):
                    if q4 + 1 < 4:
                        stage_rep(c, q4 + 1)
                        stage_t1(c, q4 + 1, vT3)
                        stage_gt(c, q4 + 1, p2, csl)
                    elif c + 1 < NCH:
                        stage_rep(c + 1, 0)
                    st_toks = stage_mid(c, q4, p2, csl, vSSB, vT2, trx)
                    y_last_prev = stage_y(c, q4, csl, vSSB, vDCB, st_toks)
                    qp = q4 % 2
                    rep_used[qp] = y_last_prev
                if c % 2 == 1:
                    wl = [(S["A"], S["A"].n)]
                    for j in range(32):
                        dma("sync", y_scr[j * 128:(j + 1) * 128,
                                          (c - 1) * LC:(c + 1) * LC],
                            Y4[:, j, :], waits=wl if j == 0 else [])
                    y4_used = B.dma_tokens("sync")
                    slab_used[p2] = y_last_prev
                    cc_used = y_last_prev
            barrier()

        es_mid.close()

        if "ys" in dbg:
            t = dbg_tensor("ys", [DI, TM], BF16)
            dma("sync", t, y_scr)
            barrier()

        SC82 = float(2.0 ** -3.5)
        with ExitStack() as es7:
            def sb7(name, shape, dtype):
                return es7.enter_context(nc.sbuf_tensor(name, shape, dtype))

            X2H = sb7("X2H", [128, 64, 1024], BF16)
            WDB = [sb7(f"WDB{i}", [128, 64, 128], BF16) for i in range(2)]
            G0T = [sb7(f"G0T{i}", [128, 512], BF16) for i in range(2)]
            h1v = h1_scr.rearrange("(q v) t -> v q t", v=64)
            bank_tok7 = {i: None for i in range(8)}
            wdb_free = {0: None, 1: None}
            g0_free = {0: None, 1: None}
            x2_used = None
            for half in range(2):
                ho = half * 1024
                for v64 in range(64):
                    wl = [x2_used] if x2_used else []
                    wH = dma("gpsimd", X2H[:, v64, :],
                             h1v[v64, :, ho:ho + 1024], waits=wl)
                    for s2 in range(2):
                        bank = (v64 * 2 + s2) % 6
                        wl = [wH]
                        if bank_tok7[bank] is not None:
                            wl.append(bank_tok7[bank])
                        tMM = op("tensor", lambda t, v64=v64, s2=s2, bank=bank: t.matmul(
                            PS[bank][:, :], H128[:],
                            X2H[:, v64, s2 * 512:(s2 + 1) * 512],
                            start=True, stop=True), waits=wl, inc=True)
                        bank_tok7[bank] = op("scalar", lambda a, v64=v64, s2=s2, bank=bank: a.activation(
                            X2H[:, v64, s2 * 512:(s2 + 1) * 512], PS[bank][:, :],
                            AF.Copy, scale=SC82), waits=[tMM])
                aX2 = (S["A"], S["A"].n)
                for m in range(16):
                    p = m % 2
                    wl = [wdb_free[p]] if wdb_free[p] else []
                    wW = dma("gpsimd", WDB[p][:], d_wdown[m], waits=wl)
                    for s2 in range(2):
                        bank = 6 + s2
                        gp = s2
                        wl = [wW, aX2]
                        if bank_tok7[bank] is not None:
                            wl.append(bank_tok7[bank])

                        def mmd(t, p=p, s2=s2, bank=bank):
                            ins = None
                            for v64 in range(64):
                                ins = t.matmul(PS[bank][:, :], WDB[p][:, v64, :],
                                               X2H[:, v64, s2 * 512:(s2 + 1) * 512],
                                               start=(v64 == 0), stop=(v64 == 63))
                            return ins
                        tMM = op("tensor", mmd, waits=wl, inc=True)
                        if s2 == 1:
                            wdb_free[p] = tMM
                        wl = [tMM]
                        if g0_free[gp] is not None:
                            wl.append(g0_free[gp])
                        aG0 = op("scalar", lambda a, m=m, gp=gp, bank=bank: a.activation(
                            G0T[gp][:], PS[bank][:, :], AF.Silu,
                            scale=WSCDP[:, m:m + 1]), waits=wl)
                        bank_tok7[bank] = aG0
                        g0_free[gp] = dma(
                            "sync", g_scr[m * 128:(m + 1) * 128,
                                          ho + s2 * 512:ho + (s2 + 1) * 512],
                            G0T[gp][:], waits=[aG0])
                x2_used = bank_tok7[7]
            barrier()

        if "gs" in dbg:
            t = dbg_tensor("gs", [DM, TM], BF16)
            dma("sync", t, g_scr)
            barrier()

        es_b = ExitStack()

        def sbb(name, shape, dtype):
            return es_b.enter_context(nc.sbuf_tensor(name, shape, dtype))

        YN = sbb("YN", [128, 32, TM], BF16)
        with ExitStack() as es8:
            def sb8(name, shape, dtype):
                return es8.enter_context(nc.sbuf_tensor(name, shape, dtype))

            OG8 = sb8("OG8", [128, 32, 32], BF16)
            EB8 = sb8("EB8", [32, 32, 128], BF16)
            YT = [sb8(f"YT{i}", [128, TM], BF16) for i in range(2)]
            ZT = [sb8(f"ZT{i}", [128, TM], BF16) for i in range(2)]
            ZS = sb8("ZS", [128, TM], BF16)
            SQG = sb8("SQG", [128, TM], BF16)
            SMS = sb8("SMS", [32, TM], F32)
            RSTB = sb8("RSTB", [32, TM], BF16)
            dma("sync", OG8[:], og_c.rearrange("(g p) m -> p g m", g=32))
            dma("sync", EB8[:], eb_c.rearrange("(g k) m -> k g m", g=32))
            wC8 = B.dma_tokens("sync")
            yz_free = {0: None, 1: None}
            sq_toks = []
            for grp in range(32):
                p = grp % 2
                wl = ([yz_free[p]] if yz_free[p] else [])
                wY = dma("sync", YT[p][:], y_scr[grp * 128:(grp + 1) * 128, :], waits=wl)
                wZ = dma("sync", ZT[p][:], z_scr[grp * 128:(grp + 1) * 128, HALO:], waits=wl)
                aZ = op("scalar", lambda a, p=p: a.activation(ZS[:], ZT[p][:], AF.Silu),
                        waits=[wZ, (S["V"], S["V"].n)])
                vYG = op("vector", lambda v, grp=grp, p=p: v.tensor_tensor(
                    YN[:, grp, :], YT[p][:], ZS[:], ALU.mult), waits=[wY, aZ])
                yz_free[p] = vYG
                aSQ = op("scalar", lambda a, grp=grp: a.activation(
                    SQG[:], YN[:, grp, :], AF.Square),
                    waits=[vYG] + (sq_toks[-4:] if grp >= 1 else []))
                sq4 = []
                for s4 in range(4):
                    wl = [aSQ] + (wC8 if grp == 0 else [])
                    sq4.append(op("tensor", lambda t, grp=grp, s4=s4: t.matmul(
                        PS[s4][0:32, :], OG8[:, grp, :],
                        SQG[:, s4 * 512:(s4 + 1) * 512],
                        start=(grp == 0), stop=(grp == 31)), waits=wl, inc=True))
                sq_toks = sq4
            ev4 = []
            for s4 in range(4):
                ev4.append(op("vector", lambda v, s4=s4: v.tensor_copy(
                    SMS[:, s4 * 512:(s4 + 1) * 512], PS[s4][0:32, :]),
                    waits=[sq_toks[s4]]))
            aL = op("scalar", lambda a: a.activation(
                SMS[:], SMS[:], AF.Ln, scale=1.0 / 128.0, bias=EPSC[0:32, :]),
                waits=ev4)
            aR8 = op("scalar", lambda a: a.activation(
                SMS[:], SMS[:], AF.Exp, scale=-0.5))
            vRB = op("vector", lambda v: v.tensor_copy(RSTB[:], SMS[:]), waits=[aR8])
            bank_tok8 = {i: None for i in range(8)}
            for grp in range(32):
                for s4 in range(4):
                    bank = (grp * 4 + s4) % 8
                    wl = [vRB]
                    if bank_tok8[bank] is not None:
                        wl.append(bank_tok8[bank])
                    tB = op("tensor", lambda t, grp=grp, s4=s4, bank=bank: t.matmul(
                        PS[bank][:, :], EB8[:, grp, :],
                        RSTB[0:32, s4 * 512:(s4 + 1) * 512],
                        start=True, stop=True), waits=wl, inc=True)
                    bank_tok8[bank] = op("vector", lambda v, grp=grp, s4=s4, bank=bank: v.scalar_tensor_tensor(
                        YN[:, grp, s4 * 512:(s4 + 1) * 512],
                        YN[:, grp, s4 * 512:(s4 + 1) * 512],
                        NW[:, grp:grp + 1], PS[bank][:, :],
                        ALU.mult, ALU.mult), waits=[tB])
            barrier()

        if "yn" in dbg:
            t = dbg_tensor("yn", [128, 32 * TM], BF16)
            dma("sync", t.rearrange("p (a b) -> p a b", a=32), YN[:])
            barrier()

        with ExitStack() as es9:
            def sb9(name, shape, dtype):
                return es9.enter_context(nc.sbuf_tensor(name, shape, dtype))

            WOB = [sb9(f"WOB{i}", [128, 32, 128], BF16) for i in range(2)]
            GT9 = [sb9(f"GT9{i}", [128, TM], BF16) for i in range(2)]
            SOT = sb9("SOT", [128, 512], BF16)
            SGT = [sb9(f"SGT{i}", [128, TM], BF16) for i in range(2)]
            bank_tok9 = {i: None for i in range(8)}
            w_free9 = {0: None, 1: None}
            sg_free = {0: None, 1: None}
            for m in range(16):
                p = m % 2
                wl = [w_free9[p]] if w_free9[p] else []
                wW = dma("gpsimd", WOB[p][:], d_wout[m], waits=wl)
                wG = dma("sync", GT9[p][:], g_scr[m * 128:(m + 1) * 128, :],
                         waits=[sg_free[p]] if sg_free[p] else [])
                for s4 in range(4):
                    bank = (m * 4 + s4) % 8

                    def mmo(t, p=p, s4=s4, bank=bank):
                        ins = None
                        for k in range(32):
                            ins = t.matmul(PS[bank][:, :], WOB[p][:, k, :],
                                           YN[:, k, s4 * 512:(s4 + 1) * 512],
                                           start=(k == 0), stop=(k == 31))
                        return ins
                    wl = [wW]
                    if bank_tok9[bank] is not None:
                        wl.append(bank_tok9[bank])
                    tMM = op("tensor", mmo, waits=wl, inc=True)
                    if s4 == 3:
                        w_free9[p] = tMM
                    aS = op("scalar", lambda a, s4=s4, bank=bank: a.activation(
                        SOT[:], PS[bank][:, :], AF.Copy),
                        waits=[tMM, (S["V"], S["V"].n)])
                    bank_tok9[bank] = aS
                    op("vector", lambda v, p=p, s4=s4: v.tensor_tensor(
                        SGT[p][:, s4 * 512:(s4 + 1) * 512], SOT[:],
                        GT9[p][:, s4 * 512:(s4 + 1) * 512], ALU.mult),
                        waits=[aS, wG])
                sg_free[p] = dma("sync", sg_scr[m * 128:(m + 1) * 128, :], SGT[p][:],
                                 waits=[(S["V"], S["V"].n)])
            barrier()
        es_b.close()

        with ExitStack() as es10:
            def sb10(name, shape, dtype):
                return es10.enter_context(nc.sbuf_tensor(name, shape, dtype))

            OUTS = sb10("OUTS", [128, 16, DM], BF16)
            SGB = [sb10(f"SGB{i}", [128, TM], BF16) for i in range(2)]
            UB = [sb10(f"UB{i}", [128, DM], F32) for i in range(2)]
            OUTF = [sb10(f"OUTF{i}", [128, DM], F32) for i in range(2)]
            bank_tokX = {6: None, 7: None}
            sgb_free = {0: None, 1: None}
            for m in range(16):
                p = m % 2
                wl = [sgb_free[p]] if sgb_free[p] else []
                wS9 = dma("sync", SGB[p][:], sg_scr[m * 128:(m + 1) * 128, :], waits=wl)
                for half in range(4):
                    bank = 6 + half % 2
                    wl = [wS9]
                    if bank_tokX[bank] is not None:
                        wl.append(bank_tokX[bank])

                    def tfn(t, p=p, half=half, bank=bank):
                        ins = None
                        for j in range(4):
                            tt = half * 4 + j
                            ins = t.transpose(
                                ptb(bank)[:, j * 128:(j + 1) * 128],
                                SGB[p][:, tt * 128:(tt + 1) * 128], IDENT[:])
                        return ins
                    tT = op("tensor", tfn, waits=wl, inc=True)
                    bank_tokX[bank] = op("vector", lambda v, m=m, half=half, bank=bank: v.tensor_copy(
                        OUTS[:, half * 4:half * 4 + 4, m * 128:(m + 1) * 128],
                        ptb(bank).rearrange("p (j q) -> p j q", j=8)[:, 0:4, :]),
                        waits=[tT])
                sgb_free[p] = (S["V"], S["V"].n)
            vTdone = (S["V"], S["V"].n)
            ub_free = {0: None, 1: None}
            for tt in range(16):
                p = tt % 2
                wl = [ub_free[p]] if ub_free[p] else []
                wU = dma("sync", UB[p][:],
                         d_u[HALO + tt * 128:HALO + (tt + 1) * 128, :], waits=wl)
                vO = op("vector", lambda v, tt=tt, p=p: v.tensor_tensor(
                    OUTF[p][:], UB[p][:], OUTS[:, tt, :], ALU.add),
                    waits=[wU, vTdone])
                ub_free[p] = dma("sync", d_out[tt * 128:(tt + 1) * 128, :],
                                 OUTF[p][:], waits=[vO])
            barrier()

        for (c, t) in B.dma_tokens("sync"):
            B.raw("sync", lambda e, c=c, t=t: e.wait_ge(c.h, t))
        for (c, t) in B.dma_tokens("gpsimd"):
            B.raw("gpsimd", lambda e, c=c, t=t: e.wait_ge(c.h, t))
        B.emit()
    return nc, dbg_outs


def _blk_weightT(wT, mt_w=128, kt=16):
    cin, cout = wT.shape
    assert cin == kt * 128
    blocks = []
    for m0 in range(0, cout, mt_w):
        w = min(mt_w, cout - m0)
        b = wT[:, m0:m0 + w].reshape(kt, 128, w).transpose(1, 0, 2)
        blocks.append(np.ascontiguousarray(b, dtype=np.float32))
    return blocks


def _host_prep(inputs):
    u = np.asarray(inputs["L0_attn_u"])
    WinT = np.ascontiguousarray(np.asarray(inputs["L0_attn_W_in"]).T)
    WupT = np.ascontiguousarray(np.asarray(inputs["L0_ffn_W_up"]).T)
    WgateT = np.ascontiguousarray(np.asarray(inputs["L0_ffn_W_gate"]).T)
    WoutT = np.ascontiguousarray(np.asarray(inputs["L0_attn_W_out"]).T)
    Wdown = np.asarray(inputs["L0_ffn_W_down"])

    win_blk = np.stack(_blk_weightT(WinT[:, :10240], 128, 16))
    win_dt_blk = _blk_weightT(WinT[:, 10240:10304], 64, 16)[0]
    wup_blk = np.stack(_blk_weightT(WupT, 128, 16))
    wgate_blk = np.stack(_blk_weightT(WgateT, 128, 16))
    wout_blk = np.stack(_blk_weightT(WoutT, 128, 32))

    qp = np.arange(128)
    vv = np.arange(64)
    cmap = (qp[None, :] >> 1) * 128 + (qp[None, :] & 1) * 64 + vv[:, None]
    WdT_perm = Wdown.T[cmap.reshape(-1), :].reshape(64, 128, 2048)
    wdown_blk = np.ascontiguousarray(
        WdT_perm.reshape(64, 128, 16, 128).transpose(2, 1, 0, 3), dtype=np.float32)

    def pack_pp(vec, ntile):
        v = np.asarray(vec, np.float32)
        return np.ascontiguousarray(v.reshape(ntile, 128).T)

    convw_pk = np.ascontiguousarray(
        np.asarray(inputs["L0_attn_conv_w"], np.float32).reshape(48, 128, 4).transpose(1, 0, 2))
    convb_pk = pack_pp(inputs["L0_attn_conv_b"], 48)
    wsc = np.asarray(inputs["L0_ffn_Wscale_ug"], np.float32)

    shared = dict(
        win_blk=win_blk, win_dt_blk=win_dt_blk, wup_blk=wup_blk,
        wgate_blk=wgate_blk, wdown_blk=wdown_blk, wout_blk=wout_blk,
        convw_pk=convw_pk, convb_pk=convb_pk,
        dtbias=np.asarray(inputs["L0_attn_dt_bias"], np.float32).reshape(64, 1),
        alog=np.asarray(inputs["L0_attn_A_log"], np.float32).reshape(64, 1),
        Drow=np.asarray(inputs["L0_attn_D"], np.float32).reshape(1, 64),
        nw_pk=pack_pp(inputs["L0_attn_norm_weight"], 32),
        suug_pk=pack_pp(inputs["L0_ffn_SU_ug"], 16),
        sudp_pk=pack_pp(inputs["L0_ffn_SU_dp"], 64),
        wscup_pk=pack_pp(wsc[:INTER], 64),
        wscgate_pk=pack_pp(wsc[INTER:], 64),
        wscdp_pk=pack_pp(inputs["L0_ffn_Wscale_dp"], 16),
    )

    in_maps = []
    for core in range(8):
        b, q = core // 4, core % 4
        start = q * TM
        u_loc = np.zeros((TL, DM), np.float32)
        lo = max(0, start - HALO)
        u_loc[HALO - (start - lo):] = u[b, lo:start + TM]
        u_pfx = np.zeros((PFX, DM), np.float32)
        plo = max(0, start - PFX)
        nreal = start - plo
        if nreal > 0:
            u_pfx[PFX - nreal:] = u[b, plo:start]
        dt_mask = np.zeros((NH, PFX), np.float32)
        if nreal > 0:
            dt_mask[:, PFX - nreal:] = 1.0
        m = dict(shared)
        m.update(u_loc=u_loc, u_pfx=u_pfx, dt_mask=dt_mask)
        in_maps.append(m)
    return in_maps


def kernel(**inputs):
    in_maps = _host_prep(inputs)
    nc, _ = build_program()
    res = run_bass_kernel_spmd(nc, in_maps, core_ids=list(range(8)))
    out = np.zeros((2, 8192, DM), np.float32)
    for core in range(8):
        b, q = core // 4, core % 4
        out[b, q * TM:(q + 1) * TM] = res.results[core]["out_loc"]
    return out
